# version 04
# baseline (speedup 1.0000x reference)
import numpy as np
import ml_dtypes
import math
import contextlib
import concourse.bass as bass
import concourse.mybir as mybir
from concourse.bass_utils import run_bass_kernel_spmd

F32 = mybir.dt.float32
BF16 = mybir.dt.bfloat16
I32 = mybir.dt.int32
U8 = mybir.dt.uint8
AF = mybir.ActivationFunctionType
ALU = mybir.AluOpType
AX = mybir.AxisListType
BF = ml_dtypes.bfloat16

NCORES = 8
D = 4096
NCH = 32
TOK = 1024
TT = 512
EPS = 1e-5


def _core_chunks(c):
    m = c % 4
    return m, 7 - m

def _core_positions(c):
    m, h = _core_chunks(c)
    return np.concatenate([np.arange(512 * m, 512 * m + 512),
                           np.arange(512 * h, 512 * h + 512)])

def _tile_w(Wm):
    din, dout = Wm.shape
    return np.ascontiguousarray(
        Wm.reshape(din // 128, 128, dout // 128, 128).transpose(2, 1, 0, 3)
    ).reshape(dout // 128, 128, (din // 128) * 128)

def _rope_l0(pos):
    inv = 1.0 / (10000.0 ** (np.arange(0, 64, dtype=np.float64) / 64.0))
    ang = inv[:, None] * pos[None, :].astype(np.float64)
    c, s = np.cos(ang), np.sin(ang)
    return (np.vstack([c, c]).astype(np.float32), np.vstack([s, s]).astype(np.float32))

def _rope_l1(pos):
    idx = np.arange(0, 128, 2, dtype=np.float64) / 128.0
    freq = 1.0 / (2.0 * math.pi) * (500000.0 ** (-idx))
    factor, lo, hi, L0 = 32.0, 1.0, 4.0, 8192
    freq_low = lo / L0
    hi_f = hi / L0
    freq_scaled = np.where(freq < freq_low, freq / factor, freq)
    smooth = np.clip((L0 * freq - lo) / (hi - lo), 0.0, 1.0)
    freq_smooth = (1.0 - smooth) * (freq / factor) + smooth * freq
    is_mid = (freq >= freq_low) & (freq <= hi_f)
    f = np.where(is_mid, freq_smooth, freq_scaled)
    phase = 2.0 * math.pi * pos[:, None].astype(np.float64) * f[None, :]
    c, s = np.cos(phase).T, np.sin(phase).T
    return (np.vstack([c, c]).astype(np.float32), np.vstack([s, s]).astype(np.float32))

def _perm_rot():
    P = np.zeros((128, 128), dtype=np.float32)
    for m in range(64):
        P[m + 64, m] = -1.0
        P[m, m + 64] = 1.0
    return P.astype(BF)

def _e16():
    E = np.zeros((8, 128), dtype=np.float32)
    for e in range(8):
        E[e, 16 * e:16 * e + 16] = 2.0
    return E

def _mask_atlas():
    pats = [np.ones((128, 512), np.float32)]
    i = np.arange(128)[:, None]
    j = np.arange(512)[None, :]
    for d in (0, 128, 256, 384):
        pats.append((i <= j - d).astype(np.float32))
    pats.append(np.zeros((128, 512), np.float32))
    return np.concatenate(pats, axis=1).astype(BF)

def _off_feed(c):
    out = []
    for tile, nkt in ((0, 16), (1, 32)):
        p = _core_chunks(c)[tile]
        for kt in range(nkt):
            dp = kt - 4 * p
            out.append(0 if dp < 0 else (1 + dp if dp <= 3 else 5))
    out += [0] * (64 - len(out))
    return np.array(out, dtype=np.int32)[None, :]

def _mt_bf16():
    n = 4096
    t = np.arange(n, dtype=np.float64)
    M = np.empty((n, n), dtype=np.float64)
    M[:, 0] = 1.0 / math.sqrt(n)
    jj = np.arange(1, n // 2, dtype=np.float64)
    ang = 2.0 * math.pi * t[:, None] * jj[None, :] / n
    M[:, 1:n - 1:2] = math.sqrt(2.0 / n) * np.cos(ang)
    M[:, 2:n - 1:2] = math.sqrt(2.0 / n) * np.sin(ang)
    M[:, n - 1] = ((-1.0) ** t) / math.sqrt(n)
    return np.ascontiguousarray(M.T).astype(BF)

_HC = {}

def _cached(key, fn):
    if key not in _HC:
        _HC[key] = fn()
    return _HC[key]


class Sched:
    ENGINES = ("tensor", "vector", "scalar", "gpsimd", "sync")

    def __init__(self, nc, stack):
        self.nc = nc
        self.stack = stack
        self.prog = {e: [] for e in self.ENGINES}
        self.sems = {}
        self.cnt = {}

    def sem(self, name):
        if name not in self.sems:
            self.sems[name] = self.stack.enter_context(self.nc.semaphore(name))
            self.cnt[name] = 0
        return self.sems[name]

    def emit(self, eng, fn, waits=(), inc=None, k=1):
        tok = None
        if inc is not None:
            self.sem(inc)
            self.cnt[inc] += k
            tok = (inc, self.cnt[inc])
        wl = []
        for t in waits:
            if t is None:
                continue
            sname, v = t
            self.sem(sname)
            wl.append((self.sems[sname], v))
        semh = self.sems[inc] if inc is not None else None

        def closure(e, _wl=tuple(wl), _fn=fn, _semh=semh, _k=k):
            for sh, v in _wl:
                e.wait_ge(sh, v)
            inst = _fn(e)
            if _semh is not None:
                inst.then_inc(_semh, _k)
        self.prog[eng].append(closure)
        return tok

    def dma(self, queue, out, in_, waits=()):
        sem = 'dmah' if queue == 'sync' else 'dmas'
        return self.emit(queue, lambda e, o=out, i=in_: e.dma_start(out=o, in_=i),
                         waits=waits, inc=sem, k=16)

    def wait_only(self, eng, waits):
        wl = []
        for t in waits:
            if t is None:
                continue
            sname, v = t
            self.sem(sname)
            wl.append((self.sems[sname], v))

        def closure(e, _wl=tuple(wl)):
            for sh, v in _wl:
                e.wait_ge(sh, v)
        self.prog[eng].append(closure)

    def fence(self):
        toks = [self.last(x) for x in ('pe', 'dve', 'act', 'dmah', 'dmas', 'cc')]
        toks = [t for t in toks if t is not None]
        for e in self.ENGINES:
            self.wait_only(e, toks)

    def last(self, name):
        if name not in self.cnt or self.cnt[name] == 0:
            return None
        return (name, self.cnt[name])

    def finalize(self):
        with self.nc.Block() as block:
            for eng_name in self.ENGINES:
                ops = list(self.prog[eng_name])
                if not ops:
                    continue
                dec = getattr(block, eng_name)

                def body(e, _ops=ops):
                    for f in _ops:
                        f(e)
                dec(body)



def emit_ln(s, b, tile, extra_first_wait=()):
    col = lambda c: c * TOK + tile * TT
    pe_s1 = [None, None]
    pe_s2 = [None, None]
    ac_sq = [None, None]
    t_first = list(extra_first_wait)
    last2 = None
    for c in range(NCH):
        par = c % 2
        t_cast = s.emit('vector', lambda e, c=c, par=par: e.tensor_copy(
            b['sc_bf'][par][:, :], b['x_sb'][:, col(c):col(c) + TT]),
            waits=(t_first if c == 0 else []) + [pe_s1[par], ac_sq[par]], inc='dve')
        t_sq = s.emit('scalar', lambda e, par=par: e.activation(
            out=b['sc_sq'][par][:, :], in_=b['sc_bf'][par][:, :], func=AF.Square),
            waits=[t_cast, pe_s2[par]], inc='act')
        ac_sq[par] = t_sq
        pe_s1[par] = s.emit('tensor', lambda e, c=c, par=par: e.matmul(
            b['pb'][0][0:1, :], b['ones_col'][:, 0:1], b['sc_bf'][par][:, :],
            start=(c == 0), stop=(c == NCH - 1), skip_group_check=True),
            waits=[t_cast] + (t_first if c == 0 else []), inc='pe')
        pe_s2[par] = s.emit('tensor', lambda e, c=c, par=par: e.matmul(
            b['pb'][1][0:1, :], b['ones_col'][:, 0:1], b['sc_sq'][par][:, :],
            start=(c == 0), stop=(c == NCH - 1), skip_group_check=True),
            waits=[t_sq], inc='pe')
        last2 = pe_s2[par]
    t_m = s.emit('vector', lambda e: e.tensor_scalar_mul(
        b['m_row'][:, :], b['pb'][0][0:1, :], 1.0 / D), waits=[pe_s1[1], last2], inc='dve')
    t_msq = s.emit('vector', lambda e: e.tensor_mul(
        b['a_row'][:, :], b['m_row'][:, :], b['m_row'][:, :]), waits=[t_m], inc='dve')
    t_var = s.emit('vector', lambda e: e.scalar_tensor_tensor(
        out=b['a_row'][:, :], in0=b['pb'][1][0:1, :], scalar=1.0 / D,
        in1=b['a_row'][:, :], op0=ALU.mult, op1=ALU.subtract), waits=[t_msq], inc='dve')
    t_ln = s.emit('scalar', lambda e: e.activation(
        out=b['rstd_row'][:, :], in_=b['a_row'][:, :], func=AF.Ln,
        bias=b['eps_row'][:, 0:1]), waits=[t_var], inc='act')
    t_rstd = s.emit('scalar', lambda e: e.activation(
        out=b['a_row'][:, :], in_=b['rstd_row'][:, :], func=AF.Exp, scale=-0.5),
        waits=[t_ln], inc='act')
    t_mrs = s.emit('vector', lambda e: e.tensor_mul(
        b['rstd_row'][:, :], b['m_row'][:, :], b['a_row'][:, :]),
        waits=[t_rstd], inc='dve')
    t_r1 = s.emit('tensor', lambda e: e.matmul(
        b['pb'][2][:, :], b['ones_row_f32'][0:1, :], b['a_row'][:, :],
        start=True, stop=True), waits=[t_rstd, t_mrs], inc='pe')
    t_r2 = s.emit('tensor', lambda e: e.matmul(
        b['pb'][3][:, :], b['ones_row_f32'][0:1, :], b['rstd_row'][:, :],
        start=True, stop=True), waits=[t_mrs], inc='pe')
    t_y = None
    for c in range(NCH):
        s.emit('vector', lambda e, c=c: e.tensor_mul(
            b['sc_f32'][:, :], b['x_sb'][:, col(c):col(c) + TT], b['pb'][2][:, :]),
            waits=[t_r1, t_r2] if c == 0 else [], inc='dve')
        t_y = s.emit('vector', lambda e, c=c: e.tensor_sub(
            b['y_sb'][:, c * TT:(c + 1) * TT], b['sc_f32'][:, :], b['pb'][3][:, :]),
            inc='dve')
    return t_y


def emit_wsweep(s, b, tile, w_dram, n_slab_pre, scale, extra_mm=None,
                first_waits=(), rhs='y'):
    xcol = lambda j: j * TOK + tile * TT
    if rhs == 'y':
        rhs_ap = lambda c: b['y_sb'][:, c * TT:(c + 1) * TT]
    else:
        rhs_ap = lambda c: b['ctx_sb'][:, c * TT:(c + 1) * TT]
    nbank = 4
    evac_tok = [None] * nbank
    slab_tok = [None, None]
    for j in range(2):
        slab_tok[j] = s.dma('gpsimd', b['wslab'][j][:, :], w_dram.ap()[j],
                            waits=[s.last('pe')])
    t_ev = None
    t_mm = None
    for j in range(NCH):
        par = j % 2
        bank = b['pb'][4 + (j % nbank)]
        n_extra = 0 if extra_mm is None else (1 if extra_mm == 'l0inc' else 2)
        for c in range(NCH):
            waits = []
            if c == 0:
                waits = [slab_tok[par], evac_tok[j % nbank]] + (list(first_waits) if j == 0 else [])
            is_last = (c == NCH - 1) and n_extra == 0
            t_mm = s.emit('tensor', lambda e, j=j, c=c, par=par, bank=bank, il=is_last: e.matmul(
                bank[:, :], b['wslab'][par][:, c * 128:(c + 1) * 128], rhs_ap(c),
                start=(c == 0), stop=il, skip_group_check=True),
                waits=waits, inc='pe' if (c == NCH - 1) else None)
        if extra_mm == 'l0inc':
            t_mm = s.emit('tensor', lambda e, j=j, bank=bank: e.matmul(
                bank[:, :], b['ab_sb'][:, j * 128:(j + 1) * 128], b['midw'][:, :],
                start=False, stop=True, skip_group_check=True), inc='pe')
        elif extra_mm == 'l1':
            t_mm = s.emit('tensor', lambda e, j=j, bank=bank: e.matmul(
                bank[:, :], b['g_sb'][0:40, j * 128:(j + 1) * 128], b['e_sb'][0:40, :],
                start=False, stop=True, skip_group_check=True), inc='pe')
        if j + 2 < NCH:
            slab_tok[par] = s.dma('gpsimd', b['wslab'][par][:, :], w_dram.ap()[j + 2],
                                  waits=[t_mm])
        if extra_mm == 'l1':
            s.emit('vector', lambda e, j=j, bank=bank: e.tensor_scalar_add(
                b['sc_f32'][:, :], bank[:, :], b['bb2'][:, j:j + 1]),
                waits=[t_mm], inc='dve')
            t_ev = s.emit('vector', lambda e, j=j, sc=scale: e.scalar_tensor_tensor(
                out=b['x_sb'][:, xcol(j):xcol(j) + TT], in0=b['sc_f32'][:, :], scalar=sc,
                in1=b['x_sb'][:, xcol(j):xcol(j) + TT], op0=ALU.mult, op1=ALU.add),
                inc='dve')
        else:
            t_ev = s.emit('vector', lambda e, j=j, bank=bank, sc=scale: e.scalar_tensor_tensor(
                out=b['x_sb'][:, xcol(j):xcol(j) + TT], in0=bank[:, :], scalar=sc,
                in1=b['x_sb'][:, xcol(j):xcol(j) + TT], op0=ALU.mult, op1=ALU.add),
                waits=[t_mm], inc='dve')
        evac_tok[j % nbank] = t_ev
    return t_ev, t_mm


def emit_moe_head(s, b, tile, t_y, la_tok=None, dbg=None):
    ycol = lambda c: c * TT
    t_log = None
    for sub in range(4):
        for c in range(NCH):
            t_log = s.emit('tensor', lambda e, sub=sub, c=c: e.matmul(
                b['pb'][0][:, sub * 8:sub * 8 + 8],
                b['y_sb'][:, ycol(c) + sub * 128: ycol(c) + sub * 128 + 128],
                b['rt_sb'][:, c * 8:(c + 1) * 8],
                start=(c == 0), stop=(c == NCH - 1), skip_group_check=True),
                waits=[t_y] if (sub == 0 and c == 0) else [],
                inc='pe' if c == NCH - 1 else None)
    g = lambda t: t.rearrange("p (g e) -> p g e", e=8)
    bc = lambda t: t[:, :, None].broadcast_to([128, 4, 8])
    if dbg is not None:
        s.emit('vector', lambda e: e.tensor_copy(b['log_sb'][:, :], b['pb'][0][:, 0:32]),
               waits=[t_log], inc='dve')
    tp = s.emit('vector', lambda e: e.reduce_max(
        b['w4a'][:, :, None], g(b['pb'][0][:, 0:32]), axis=AX.X), waits=[t_log], inc='dve')
    t1 = s.emit('vector', lambda e: e.tensor_tensor(
        out=g(b['w32a']), in0=g(b['pb'][0][:, 0:32]), in1=bc(b['w4a']),
        op=ALU.subtract), waits=[tp], inc='dve')
    t2 = s.emit('scalar', lambda e: e.activation(
        out=b['w32p'][:, :], in_=b['w32a'][:, :], func=AF.Exp), waits=[t1], inc='act')
    tp = s.emit('vector', lambda e: e.reduce_max(
        b['w4a'][:, :, None], g(b['w32p']), axis=AX.X), waits=[t2], inc='dve')
    tp = s.emit('vector', lambda e: e.tensor_tensor(
        out=g(b['w32a']), in0=g(b['w32p']), in1=bc(b['w4a']), op=ALU.is_equal),
        waits=[tp], inc='dve')
    tp = s.emit('vector', lambda e: e.scalar_tensor_tensor(
        out=g(b['w32b']), in0=g(b['w32a']), scalar=-2.0, in1=g(b['w32p']),
        op0=ALU.mult, op1=ALU.add), waits=[tp], inc='dve')
    t6 = s.emit('vector', lambda e: e.reduce_max(
        b['w4b'][:, :, None], g(b['w32b']), axis=AX.X), waits=[tp], inc='dve')
    t8 = s.emit('vector', lambda e: e.tensor_add(
        b['w4a'][:, :], b['w4a'][:, :], b['w4b'][:, :]), waits=[t6], inc='dve')
    tp = s.emit('vector', lambda e: e.tensor_tensor(
        out=g(b['w32b']), in0=g(b['w32p']), in1=bc(b['w4b']), op=ALU.is_ge),
        waits=[t8], inc='dve')
    t9 = s.emit('vector', lambda e: e.reciprocal(
        b['w4b'][:, :], b['w4a'][:, :]), waits=[tp], inc='dve')
    tp = s.emit('vector', lambda e: e.tensor_tensor(
        out=g(b['w32a']), in0=g(b['w32p']), in1=g(b['w32b']), op=ALU.mult),
        waits=[t9], inc='dve')
    t11 = s.emit('vector', lambda e: e.tensor_tensor(
        out=g(b['w32a']), in0=g(b['w32a']), in1=bc(b['w4b']), op=ALU.mult),
        waits=[tp], inc='dve')
    tT = None
    for sub in range(4):
        tT = s.emit('tensor', lambda e, sub=sub: e.transpose(
            b['pb'][1][0:8, sub * 128:(sub + 1) * 128],
            b['w32a'][:, sub * 8:(sub + 1) * 8], b['ident'][:, :]),
            waits=[t11] if sub == 0 else [], inc='pe' if sub == 3 else None)
    tE = s.emit('vector', lambda e: e.tensor_copy(
        b['wfm'][:, :], b['pb'][1][0:8, 0:512]), waits=[tT], inc='dve')
    tR = s.emit('tensor', lambda e: e.matmul(
        b['pb'][2][:, :], b['e16'][:, :], b['wfm'][:, :], start=True, stop=True),
        waits=[tE], inc='pe')
    tRe = s.emit('vector', lambda e: e.tensor_copy(
        b['wrep'][:, :], b['pb'][2][:, :]), waits=[tR], inc='dve')
    tM = None
    for c in range(NCH):
        tM = s.emit('tensor', lambda e, c=c: e.matmul(
            b['pb'][3][:, :], b['ab_sb'][:, c * 128:(c + 1) * 128],
            b['y_sb'][:, c * TT:(c + 1) * TT],
            start=(c == 0), stop=(c == NCH - 1), skip_group_check=True),
            waits=[la_tok] if c == 0 else [], inc='pe' if c == NCH - 1 else None)
    tMw = s.emit('vector', lambda e: e.tensor_tensor(
        out=b['midw'][:, :], in0=b['pb'][3][:, :], in1=b['wrep'][:, :], op=ALU.mult),
        waits=[tM, tRe], inc='dve')
    return tMw


def emit_l1_head(s, b, t_y):
    tE = None
    for c in range(NCH):
        tE = s.emit('tensor', lambda e, c=c: e.matmul(
            b['pb'][0][0:40, :], b['enc_sb'][:, c * 40:(c + 1) * 40],
            b['y_sb'][:, c * TT:(c + 1) * TT],
            start=(c == 0), stop=(c == NCH - 1), skip_group_check=True),
            waits=[t_y] if c == 0 else [], inc='pe' if c == NCH - 1 else None)
    tEv = s.emit('vector', lambda e: e.tensor_copy(
        b['e_sb'][:, :], b['pb'][0][0:40, :]), waits=[tE], inc='dve')
    return tEv

def emit_gprep(s, b, dr):
    ws0 = b['wslab'][0]
    cf = lambda n: ws0[:, 0:NCH * n]
    mk = lambda n: ws0[:, 1024:1024 + NCH * n]
    wbuf = {'w0s': ws0[:, 2048:2048 + 1024], 'w1s': ws0[:, 3072:3072 + 256]}
    t_acc = None
    for which, n_r, w_buf in (('c0', 32, 'w0s'), ('c1', 8, 'w1s')):
        for e_i in range(3):
            cf_ap = cf(n_r).rearrange("p (c r) -> p c r", r=n_r)
            mk_ap = mk(n_r).rearrange("p (c r) -> p c r", r=n_r)
            t_cf = s.dma('gpsimd', cf_ap,
                         dr[which].ap()[e_i].rearrange("c p r -> p c r"),
                         waits=[t_acc])
            t_mk = s.dma('gpsimd', mk_ap,
                         dr['m' + which].ap()[e_i].rearrange("c p r -> p c r"))
            s.emit('vector', lambda e, n_r=n_r: e.tensor_tensor(
                out=cf(n_r), in0=cf(n_r), in1=mk(n_r), op=ALU.mult),
                waits=[t_cf, t_mk], inc='dve')
            if e_i == 0:
                t_acc = s.emit('vector', lambda e, n_r=n_r, w_buf=w_buf: e.tensor_copy(
                    wbuf[w_buf], cf(n_r)), inc='dve')
            else:
                t_acc = s.emit('vector', lambda e, n_r=n_r, w_buf=w_buf: e.tensor_tensor(
                    out=wbuf[w_buf], in0=wbuf[w_buf], in1=cf(n_r), op=ALU.add), inc='dve')
    r3 = lambda t, r, w: t.rearrange("p (c r) -> p c r", r=w)
    s.emit('vector', lambda e: e.tensor_scalar_mul(
        b['wcomb'].rearrange("p (c r) -> p c r", r=40)[:, :, 0:32],
        wbuf['w0s'].rearrange("p (c r) -> p c r", r=32), 0.5), waits=[t_acc], inc='dve')
    t_wc = s.emit('vector', lambda e: e.tensor_scalar_mul(
        b['wcomb'].rearrange("p (c r) -> p c r", r=40)[:, :, 32:40],
        wbuf['w1s'].rearrange("p (c r) -> p c r", r=8), 0.5), inc='dve')
    slab_tok = [None, None]
    for c in range(2):
        slab_tok[c] = s.dma('sync', b['wslab'][c][:, :], dr['mt'].ap()[c],
                            waits=[t_wc] if c == 0 else [s.last('pe')])
    t_mm = None
    for c in range(NCH):
        par = c % 2
        for tt_i in range(8):
            t_mm = s.emit('tensor', lambda e, c=c, par=par, tt_i=tt_i: e.matmul(
                b['pb'][tt_i][0:40, :], b['wcomb'][:, c * 40:(c + 1) * 40],
                b['wslab'][par][:, tt_i * 512:(tt_i + 1) * 512],
                start=(c == 0), stop=(c == NCH - 1), skip_group_check=True),
                waits=([slab_tok[par], t_wc] if tt_i == 0 else []),
                inc='pe' if tt_i == 7 else None)
        if c + 2 < NCH:
            slab_tok[par] = s.dma('sync', b['wslab'][par][:, :], dr['mt'].ap()[c + 2],
                                  waits=[t_mm])
    t_g = None
    for tt_i in range(8):
        t_g = s.emit('vector', lambda e, tt_i=tt_i: e.tensor_copy(
            b['g_sb'][:, tt_i * 512:(tt_i + 1) * 512], b['pb'][tt_i][0:40, :]),
            waits=[t_mm], inc='dve')
    return t_g, t_mm


def _alloc_common(nc, st, b, pb_n=8, ln=True, ctx=True):
    al = (lambda n, sh, dt: st.enter_context(nc.sbuf_tensor(n, sh, dt))) if ctx \
        else (lambda n, sh, dt: nc.alloc_sbuf_tensor(n, sh, dt))
    b['x_sb'] = al("x_sb", [128, NCH * TOK], F32)
    b['ones_col'] = al("ones_col", [128, 1], BF16)
    b['pb'] = [st.enter_context(nc.psum_tensor(f"pb{i}", [128, TT], F32)) for i in range(pb_n)]
    if ln:
        b['sc_bf'] = [al(f"scbf{i}", [128, TT], BF16) for i in range(2)]
        b['sc_sq'] = [al(f"scsq{i}", [128, TT], BF16) for i in range(2)]
        b['sc_f32'] = al("scf32", [128, TT], F32)
        for r in ('m_row', 'a_row', 'rstd_row'):
            b[r] = al(r, [1, TT], F32)
        b['eps_row'] = al("eps_row", [1, 1], F32)
        b['ones_row_f32'] = al("ones_row_f32", [1, 128], F32)

def _emit_const_init(s, b):
    s.emit('vector', lambda e: e.memset(b['ones_col'][:, :], 1.0), inc='dve')
    if 'ones_row_f32' in b:
        s.emit('vector', lambda e: e.memset(b['ones_row_f32'][:, :], 1.0), inc='dve')
    if 'eps_row' in b:
        s.emit('vector', lambda e: e.memset(b['eps_row'][:, :], EPS), inc='dve')

def _emit_x_load(s, b, x_dram):
    toks = []
    for q in range(4):
        toks.append(s.dma('sync',
                          b['x_sb'].ap().rearrange("p (c t) -> p c t", t=TOK)[:, q * 8:(q + 1) * 8],
                          x_dram.ap().rearrange("(c p) t -> p c t", p=128)[:, q * 8:(q + 1) * 8]))
    return toks

def _emit_x_store(s, b, xo_dram, waits):
    toks = []
    for q in range(4):
        toks.append(s.dma('sync',
                          xo_dram.ap().rearrange("(c p) t -> p c t", p=128)[:, q * 8:(q + 1) * 8],
                          b['x_sb'].ap().rearrange("p (c t) -> p c t", t=TOK)[:, q * 8:(q + 1) * 8],
                          waits=waits))
    return toks


def build_ffn_stage(layer):
    stack = contextlib.ExitStack()
    with stack:
        nc = bass.Bass("TRN2", target_bir_lowering=False, debug=False)
        b = {}
        x_in = nc.dram_tensor("x_in", [D, TOK], F32, kind="ExternalInput")
        x_out = nc.dram_tensor("x_out", [D, TOK], F32, kind="ExternalOutput")
        b['bw_dram'] = nc.dram_tensor("bw", [NCH, 128, NCH * 128], F32, kind="ExternalInput")
        if layer == 0:
            rt = nc.dram_tensor("rt", [128, NCH * 8], F32, kind="ExternalInput")
            la = nc.dram_tensor("la", [128, NCH * 128], F32, kind="ExternalInput")
            lb = nc.dram_tensor("lb", [128, NCH * 128], F32, kind="ExternalInput")
            e16 = nc.dram_tensor("e16", [8, 128], F32, kind="ExternalInput")
            ident = nc.dram_tensor("ident", [128, 128], F32, kind="ExternalInput")
        else:
            enc = nc.dram_tensor("enc", [128, NCH * 40], F32, kind="ExternalInput")
            bbias = nc.dram_tensor("bbias", [128, NCH], F32, kind="ExternalInput")
            b['c0_dram'] = nc.dram_tensor("c0", [3, NCH, 128, 32], F32, kind="ExternalInput")
            b['c1_dram'] = nc.dram_tensor("c1", [3, NCH, 128, 8], F32, kind="ExternalInput")
            b['mc0_dram'] = nc.dram_tensor("mc0", [3, NCH, 128, 32], U8, kind="ExternalInput")
            b['mc1_dram'] = nc.dram_tensor("mc1", [3, NCH, 128, 8], U8, kind="ExternalInput")
            b['mt_dram'] = nc.dram_tensor("mt", [NCH, 128, D], BF16, kind="ExternalInput")

        _alloc_common(nc, stack, b)
        b['y_sb'] = stack.enter_context(nc.sbuf_tensor("y_sb", [128, NCH * TT], BF16))
        b['wslab'] = [stack.enter_context(nc.sbuf_tensor(f"wslab{i}", [128, NCH * 128], BF16))
                      for i in range(2)]
        if layer == 0:
            b['rt_sb'] = stack.enter_context(nc.sbuf_tensor("rt_sb", [128, NCH * 8], BF16))
            b['ab_sb'] = stack.enter_context(nc.sbuf_tensor("ab_sb", [128, NCH * 128], BF16))
            b['e16'] = stack.enter_context(nc.sbuf_tensor("e16_sb", [8, 128], F32))
            b['ident'] = stack.enter_context(nc.sbuf_tensor("ident_sb", [128, 128], F32))
            for n, sh, dt in (('w32a', [128, 32], F32), ('w32b', [128, 32], F32),
                              ('w32p', [128, 32], F32), ('w4a', [128, 4], F32),
                              ('w4b', [128, 4], F32), ('wfm', [8, TT], F32),
                              ('wrep', [128, TT], F32), ('midw', [128, TT], BF16)):
                b[n] = stack.enter_context(nc.sbuf_tensor(n, sh, dt))
        else:
            b['enc_sb'] = stack.enter_context(nc.sbuf_tensor("enc_sb", [128, NCH * 40], BF16))
            b['bb2'] = stack.enter_context(nc.sbuf_tensor("bb2", [128, NCH], F32))
            b['e_sb'] = stack.enter_context(nc.sbuf_tensor("e_sb", [40, TT], BF16))
            b['g_sb'] = stack.enter_context(nc.sbuf_tensor("g_sb", [40, D], BF16))
            b['wcomb'] = stack.enter_context(nc.sbuf_tensor("wcomb", [128, NCH * 40], BF16))

        s = Sched(nc, stack)
        _emit_const_init(s, b)
        x_toks = _emit_x_load(s, b, x_in)
        dr = dict(bw=b['bw_dram'])
        if layer == 0:
            dr.update(rt=rt, la=la, lb=lb, e16=e16, ident=ident)
        else:
            dr.update(enc=enc, bbias=bbias, c0=b['c0_dram'], c1=b['c1_dram'],
                      mc0=b['mc0_dram'], mc1=b['mc1_dram'], mt=b['mt_dram'])
        last_ev = emit_ffn_body(s, b, dr, layer, x_toks)
        st_toks = _emit_x_store(s, b, x_out, waits=[last_ev])
        s.wait_only('sync', st_toks)
        s.finalize()
    return nc


def emit_ffn_body(s, b, dr, layer, first_toks):
    if layer == 0:
        small = [s.dma('gpsimd', b['rt_sb'][:, :], dr['rt'].ap(), waits=[s.last('pe')]),
                 s.dma('sync', b['e16'][:, :], dr['e16'].ap()),
                 s.dma('sync', b['ident'][:, :], dr['ident'].ap())]
        t_g = None
    else:
        small = [s.dma('gpsimd', b['enc_sb'][:, 0:NCH * 40], dr['enc'].ap(),
                       waits=[s.last('pe')]),
                 s.dma('sync', b['bb2'][:, :], dr['bbias'].ap())]
        t_g, _ = emit_gprep(s, b, dr)

    last_ev = None
    for tile in range(2):
        first = list(first_toks) + small + ([t_g] if layer == 1 else []) + \
            ([last_ev] if last_ev else [])
        t_y = emit_ln(s, b, tile, extra_first_wait=first)
        if layer == 0:
            la_tok = s.dma('gpsimd', b['ab_sb'][:, :], dr['la'].ap(),
                           waits=[s.last('pe')])
            tMw = emit_moe_head(s, b, tile, t_y, la_tok)
            lb_tok = s.dma('gpsimd', b['ab_sb'][:, :], dr['lb'].ap(),
                           waits=[s.last('pe')])
            last_ev, _ = emit_wsweep(s, b, tile, dr['bw'], 2, 0.5,
                                     extra_mm='l0inc', first_waits=[tMw, lb_tok])
        else:
            tEv = emit_l1_head(s, b, t_y)
            last_ev, _ = emit_wsweep(s, b, tile, dr['bw'], 2, 0.5,
                                     extra_mm='l1', first_waits=[tEv])
    return last_ev


def emit_rope_head(s, b, tile, t_proj, bankA, bankB, ct, st_, out_sb, war_dve=None):
    t_cast = s.emit('vector', lambda e: e.tensor_copy(b['qb'][:, :], bankA[:, :]),
                    waits=[t_proj] + ([war_dve] if war_dve else []), inc='dve')
    t_rot = s.emit('tensor', lambda e: e.matmul(
        bankB[:, :], b['perm_sb'][:, :], b['qb'][:, :], start=True, stop=True),
        waits=[t_cast], inc='pe')
    s.emit('vector', lambda e: e.tensor_tensor(
        out=b['rtmp1'][:, :], in0=b['qb'][:, :],
        in1=ct[:, tile * TT:tile * TT + TT], op=ALU.mult), inc='dve')
    s.emit('vector', lambda e: e.tensor_tensor(
        out=out_sb[:, :], in0=bankB[:, :],
        in1=st_[:, tile * TT:tile * TT + TT], op=ALU.mult),
        waits=[t_rot], inc='dve')
    t_out = s.emit('vector', lambda e: e.tensor_add(
        out_sb[:, :], b['rtmp1'][:, :], out_sb[:, :]), inc='dve')
    return t_out


def build_qkv_stage(layer):
    stack = contextlib.ExitStack()
    with stack:
        nc = bass.Bass("TRN2", target_bir_lowering=False, debug=False)
        b = {}
        x_in = nc.dram_tensor("x_in", [D, TOK], F32, kind="ExternalInput")
        wq = nc.dram_tensor("wq", [NCH, 128, NCH * 128], F32, kind="ExternalInput")
        wk = nc.dram_tensor("wk", [8, 128, NCH * 128], F32, kind="ExternalInput")
        wv = nc.dram_tensor("wv", [NCH, 128, 1024], F32, kind="ExternalInput")
        cq_d = nc.dram_tensor("cq", [128, TOK], F32, kind="ExternalInput")
        sq_d = nc.dram_tensor("sq", [128, TOK], F32, kind="ExternalInput")
        ck_d = nc.dram_tensor("ck", [128, TOK], F32, kind="ExternalInput")
        sk_d = nc.dram_tensor("sk", [128, TOK], F32, kind="ExternalInput")
        perm_d = nc.dram_tensor("perm", [128, 128], BF16, kind="ExternalInput")
        q_out = nc.dram_tensor("q_out", [D, TOK], BF16, kind="ExternalOutput")
        k_out = nc.dram_tensor("k_out", [1024, TOK], BF16, kind="ExternalOutput")
        v_out = nc.dram_tensor("v_out", [TOK, 1024], BF16, kind="ExternalOutput")
        kn_out = nc.dram_tensor("kn_out", [8, TOK], F32, kind="ExternalOutput")
        qm_out = nc.dram_tensor("qm_out", [1, 64], F32, kind="ExternalOutput")

        _alloc_common(nc, stack, b)
        b['y_sb'] = stack.enter_context(nc.sbuf_tensor("y_sb", [128, NCH * TT], BF16))
        b['wslab'] = [stack.enter_context(nc.sbuf_tensor(f"wslab{i}", [128, NCH * 128], BF16))
                      for i in range(2)]
        b['wv_sb'] = [stack.enter_context(nc.sbuf_tensor(f"wvsb{i}", [128, 1024], BF16))
                      for i in range(2)]
        b['ct_sb'] = stack.enter_context(nc.sbuf_tensor("ct_sb", [128, TOK], BF16))
        b['st_sb'] = stack.enter_context(nc.sbuf_tensor("st_sb", [128, TOK], BF16))
        b['perm_sb'] = stack.enter_context(nc.sbuf_tensor("perm_sb", [128, 128], BF16))
        b['qb'] = stack.enter_context(nc.sbuf_tensor("qb", [128, TT], BF16))
        b['rtmp1'] = stack.enter_context(nc.sbuf_tensor("rtmp1", [128, TT], F32))
        b['qrope'] = stack.enter_context(nc.sbuf_tensor("qrope", [128, TT], BF16))
        b['qsq'] = stack.enter_context(nc.sbuf_tensor("qsq", [128, TT], BF16))
        b['kn_row'] = stack.enter_context(nc.sbuf_tensor("kn_row", [1, TT], F32))
        b['qm_sb'] = stack.enter_context(nc.sbuf_tensor("qm_sb", [1, 64], F32))
        b['vtmp'] = [b['qb'], b['qrope']]

        s = Sched(nc, stack)
        _emit_const_init(s, b)
        x_toks = _emit_x_load(s, b, x_in)
        tbl = [s.dma('sync', b['perm_sb'][:, :], perm_d.ap())]
        t_qm_last = None
        last_y = None
        for tile in range(2):
            t_y = emit_ln(s, b, tile, extra_first_wait=x_toks + tbl +
                          ([last_y] if last_y else []))
            last_y = t_y
            for phase, nh, wdram, ctd, std in (
                    ('q', NCH, wq, cq_d, sq_d),
                    ('k', 8, wk, ck_d, sk_d)):
                ct, st_ = b['ct_sb'], b['st_sb']
                t_tb = [s.dma('gpsimd', b['ct_sb'][:, :], ctd.ap(),
                              waits=[s.last('dve')]),
                        s.dma('gpsimd', b['st_sb'][:, :], std.ap())]
                slab_tok = [None, None]
                for h in range(2):
                    slab_tok[h] = s.dma('gpsimd',
                                        b['wslab'][h].ap().rearrange("p (c f) -> p c f", f=128),
                                        wdram.ap()[h].rearrange("c p f -> p c f"),
                                        waits=[s.last('pe')])
                prev_tn = None
                for h in range(nh):
                    par = h % 2
                    bankA = b['pb'][par]
                    bankB = b['pb'][2 + par]
                    t_proj = None
                    for c in range(NCH):
                        waits = []
                        if c == 0:
                            waits = [slab_tok[par], t_y, s.last('dve')] + \
                                (t_tb if h < 2 else [])
                        t_proj = s.emit('tensor', lambda e, c=c, par=par, bankA=bankA: e.matmul(
                            bankA[:, :], b['wslab'][par][:, c * 128:(c + 1) * 128],
                            b['y_sb'][:, c * TT:(c + 1) * TT],
                            start=(c == 0), stop=(c == NCH - 1), skip_group_check=True),
                            waits=waits, inc='pe' if c == NCH - 1 else None)
                    if h + 2 < nh:
                        slab_tok[par] = s.dma('gpsimd',
                                              b['wslab'][par].ap().rearrange("p (c f) -> p c f", f=128),
                                              wdram.ap()[h + 2].rearrange("c p f -> p c f"),
                                              waits=[t_proj])
                    t_rope = emit_rope_head(s, b, tile, t_proj, bankA, bankB, ct, st_,
                                            b['qrope'], war_dve=s.last('dmah'))
                    if phase == 'q':
                        dst = q_out.ap()[h * 128:(h + 1) * 128, tile * TT:tile * TT + TT]
                    else:
                        dst = k_out.ap()[h * 128:(h + 1) * 128, tile * TT:tile * TT + TT]
                    s.dma('sync', dst, b['qrope'][:, :], waits=[t_rope])
                    t_sq = s.emit('scalar', lambda e: e.activation(
                        out=b['qsq'][:, :], in_=b['qrope'][:, :], func=AF.Square),
                        waits=[t_rope, prev_tn], inc='act')
                    t_n = s.emit('tensor', lambda e: e.matmul(
                        b['pb'][4][0:1, :], b['ones_col'][:, 0:1], b['qsq'][:, :],
                        start=True, stop=True, skip_group_check=True),
                        waits=[t_sq, s.last('dve')], inc='pe')
                    prev_tn = t_n
                    if phase == 'q':
                        slot = tile * 32 + h
                        t_qm_last = s.emit('vector', lambda e, slot=slot: e.reduce_max(
                            b['qm_sb'][:, slot:slot + 1], b['pb'][4][0:1, :], axis=AX.X),
                            waits=[t_n], inc='dve')
                    else:
                        t_kn = s.emit('vector', lambda e: e.tensor_copy(
                            b['kn_row'][:, :], b['pb'][4][0:1, :]),
                            waits=[t_n, s.last('dmah')], inc='dve')
                        s.dma('sync', kn_out.ap()[h:h + 1, tile * TT:tile * TT + TT],
                              b['kn_row'][:, :], waits=[t_kn])
            wv_tok = [None, None]
            for c in range(2):
                wv_tok[c] = s.dma('gpsimd', b['wv_sb'][c][:, :], wv.ap()[c],
                                  waits=[s.last('pe')])
            t_vmm = None
            for c in range(NCH):
                par = c % 2
                for sub in range(4):
                    for dv in range(2):
                        bank = b['pb'][sub * 2 + dv]
                        waits = []
                        if sub == 0 and dv == 0:
                            waits = [wv_tok[par]]
                            if c == 0:
                                waits.append(s.last('dve'))
                        t_vmm = s.emit('tensor', lambda e, c=c, par=par, sub=sub, dv=dv, bank=bank: e.matmul(
                            bank[:, :], b['y_sb'][:, c * TT + sub * 128:c * TT + sub * 128 + 128],
                            b['wv_sb'][par][:, dv * TT:(dv + 1) * TT],
                            start=(c == 0), stop=(c == NCH - 1), skip_group_check=True),
                            waits=waits,
                            inc='pe' if (sub == 3 and dv == 1) else None)
                if c + 2 < NCH:
                    wv_tok[par] = s.dma('gpsimd', b['wv_sb'][par][:, :], wv.ap()[c + 2],
                                        waits=[t_vmm])
            for sub in range(4):
                for dv in range(2):
                    bank = b['pb'][sub * 2 + dv]
                    vp = (sub * 2 + dv) % 2
                    t_ev = s.emit('vector', lambda e, bank=bank, vp=vp: e.tensor_copy(
                        b['vtmp'][vp][:, :], bank[:, :]),
                        waits=[t_vmm, s.last('dmah')], inc='dve')
                    s.dma('sync',
                          v_out.ap()[tile * TT + sub * 128: tile * TT + sub * 128 + 128,
                                     dv * TT:(dv + 1) * TT],
                          b['vtmp'][vp][:, :], waits=[t_ev])
        s.dma('sync', qm_out.ap()[:, :], b['qm_sb'][:, :], waits=[t_qm_last])
        s.wait_only('sync', [s.last('dmah'), s.last('dmas')])
        s.finalize()
    return nc


def build_att_stage(layer):
    stack = contextlib.ExitStack()
    with stack:
        nc = bass.Bass("TRN2", target_bir_lowering=False, debug=False)
        b = {}
        x_in = nc.dram_tensor("x_in", [D, TOK], F32, kind="ExternalInput")
        q_in = nc.dram_tensor("q_in", [D, TOK], BF16, kind="ExternalInput")
        kg = nc.dram_tensor("kg", [8, 128, 4096], BF16, kind="ExternalInput")
        vg = nc.dram_tensor("vg", [8, 128, 4096], BF16, kind="ExternalInput")
        kng = nc.dram_tensor("kng", [8, 4096], F32, kind="ExternalInput")
        qm_d = nc.dram_tensor("qm", [1, 64], F32, kind="ExternalInput")
        wo = nc.dram_tensor("wo", [NCH, 128, NCH * 128], F32, kind="ExternalInput")
        qpos_d = nc.dram_tensor("qpos", [1, TOK], F32, kind="ExternalInput")
        kpos_d = nc.dram_tensor("kpos", [128, 32], F32, kind="ExternalInput")
        ident_d = nc.dram_tensor("ident", [128, 128], F32, kind="ExternalInput")
        negrow_d = nc.dram_tensor("negrow", [1, 128], BF16, kind="ExternalInput")
        posrow_d = nc.dram_tensor("posrow", [1, 128], BF16, kind="ExternalInput")
        x_out = nc.dram_tensor("x_out", [D, TOK], F32, kind="ExternalOutput")

        _alloc_common(nc, stack, b, ln=False)
        b['y_sb'] = stack.enter_context(nc.sbuf_tensor("ctx_y", [128, NCH * TT], BF16))
        b['ctx_sb'] = b['y_sb']
        b['wslab'] = [stack.enter_context(nc.sbuf_tensor(f"wslab{i}", [128, NCH * 128], BF16))
                      for i in range(2)]
        b['ksl'] = b['wslab'][0]
        b['vsl'] = b['wslab'][1]
        b['q_sb'] = stack.enter_context(nc.sbuf_tensor("q_sb", [128, TT], BF16))
        b['exp_sb'] = [stack.enter_context(nc.sbuf_tensor(f"exp{i}", [128, TT], BF16))
                       for i in range(2)]
        b['qpos'] = stack.enter_context(nc.sbuf_tensor("qpos_sb", [1, TOK], F32))
        b['kpos'] = stack.enter_context(nc.sbuf_tensor("kpos_sb", [128, 32], F32))
        b['qpos_rep'] = stack.enter_context(nc.sbuf_tensor("qpos_rep", [128, TT], F32))
        b['ones_row_f32'] = stack.enter_context(nc.sbuf_tensor("ones_row_f32", [1, 128], F32))
        b['kn_sb'] = stack.enter_context(nc.sbuf_tensor("kn_sb", [8, TT], F32))
        b['mkacc'] = stack.enter_context(nc.sbuf_tensor("mkacc", [8, 8], F32))
        b['mk8'] = stack.enter_context(nc.sbuf_tensor("mk8", [8, 1], F32))
        b['mk2'] = stack.enter_context(nc.sbuf_tensor("mk2", [1, 8], F32))
        b['qm_sb'] = stack.enter_context(nc.sbuf_tensor("qm_sb", [1, 64], F32))
        b['negB'] = stack.enter_context(nc.sbuf_tensor("negB", [128, 1], F32))
        b['b2row'] = stack.enter_context(nc.sbuf_tensor("b2row", [1, 1], F32))
        b['b1row'] = stack.enter_context(nc.sbuf_tensor("b1row", [1, 1], F32))
        b['b1bf'] = stack.enter_context(nc.sbuf_tensor("b1bf", [1, 1], BF16))
        b['rcp'] = stack.enter_context(nc.sbuf_tensor("rcp", [1, TT], F32))
        b['rcpb'] = stack.enter_context(nc.sbuf_tensor("rcpb", [1, TT], BF16))
        b['rrep'] = stack.enter_context(nc.sbuf_tensor("rrep", [128, TT], F32))
        b['ident'] = stack.enter_context(nc.sbuf_tensor("ident_sb", [128, 128], F32))
        b['negrow'] = stack.enter_context(nc.sbuf_tensor("negrow_sb", [1, 128], BF16))
        b['posrow'] = stack.enter_context(nc.sbuf_tensor("posrow_sb", [1, 128], BF16))

        s = Sched(nc, stack)
        _emit_const_init(s, b)
        x_toks = _emit_x_load(s, b, x_in)
        pre = [s.dma('sync', b['qpos'][:, :], qpos_d.ap()),
               s.dma('sync', b['kpos'][:, :], kpos_d.ap()),
               s.dma('sync', b['qm_sb'][:, :], qm_d.ap()),
               s.dma('sync', b['ident'][:, :], ident_d.ap()),
               s.dma('sync', b['negrow'][:, :], negrow_d.ap()),
               s.dma('sync', b['posrow'][:, :], posrow_d.ap())]
        t_mk = None
        for i in range(8):
            t_l = s.dma('sync', b['kn_sb'][:, :],
                        kng.ap()[:, i * 512:(i + 1) * 512], waits=[t_mk])
            t_mk = s.emit('vector', lambda e, i=i: e.reduce_max(
                b['mkacc'][:, i:i + 1], b['kn_sb'][:, :], axis=AX.X),
                waits=[t_l], inc='dve')
        t_mk = s.emit('vector', lambda e: e.reduce_max(
            b['mk8'][:, :], b['mkacc'][:, :], axis=AX.X), waits=[t_mk], inc='dve')
        t_tr = s.emit('tensor', lambda e: e.transpose(
            b['pb'][6][0:1, 0:8], b['mk8'][:, 0:1], b['ident'][0:8, 0:8]),
            waits=[t_mk, pre[3]], inc='pe')
        t_mk2 = s.emit('vector', lambda e: e.tensor_copy(
            b['mk2'][:, :], b['pb'][6][0:1, 0:8]), waits=[t_tr], inc='dve')

        last_store_gate = None
        for tile in range(2):
            nkt = 16 if tile == 0 else 32
            t_qr = s.emit('tensor', lambda e, tile=tile: e.matmul(
                b['pb'][6][:, :], b['ones_row_f32'][0:1, :],
                b['qpos'][0:1, tile * TT:tile * TT + TT],
                start=True, stop=True, skip_group_check=True),
                waits=[pre[0], s.last('dve')], inc='pe')
            t_qrep = s.emit('vector', lambda e: e.tensor_copy(
                b['qpos_rep'][:, :], b['pb'][6][:, :]), waits=[t_qr], inc='dve')
            t_ctx_all = None
            t_kv_prev = None
            for kvh in range(8):
                t_k = s.dma('sync', b['ksl'][:, 0:nkt * 128],
                            kg.ap()[kvh][:, 0:nkt * 128], waits=[s.last('pe')])
                t_v = s.dma('sync', b['vsl'][:, 0:nkt * 128],
                            vg.ap()[kvh][:, 0:nkt * 128])
                for qh in range(4):
                    h = kvh * 4 + qh
                    t_q = s.dma('sync', b['q_sb'][:, :],
                                q_in.ap()[h * 128:(h + 1) * 128, tile * TT:tile * TT + TT],
                                waits=[s.last('pe')])
                    t_b2 = s.emit('vector', lambda e, tile=tile, h=h, kvh=kvh: e.tensor_tensor(
                        out=b['b2row'][:, :], in0=b['qm_sb'][:, tile * 32 + h:tile * 32 + h + 1],
                        in1=b['mk2'][:, kvh:kvh + 1], op=ALU.mult),
                        waits=[t_mk2, pre[2], s.last('act')], inc='dve')
                    t_b1 = s.emit('scalar', lambda e: e.activation(
                        out=b['b1row'][:, :], in_=b['b2row'][:, :], func=AF.Sqrt),
                        waits=[t_b2], inc='act')
                    t_bb = s.emit('vector', lambda e: e.tensor_copy(
                        b['b1bf'][:, :], b['b1row'][:, :]),
                        waits=[t_b1, s.last('pe')], inc='dve')
                    t_bn = s.emit('tensor', lambda e: e.matmul(
                        b['pb'][4][:, 0:1], b['negrow'][0:1, :], b['b1bf'][:, :],
                        start=True, stop=True, skip_group_check=True),
                        waits=[t_bb, pre[4]], inc='pe')
                    t_nb = s.emit('vector', lambda e: e.tensor_copy(
                        b['negB'][:, :], b['pb'][4][:, 0:1]),
                        waits=[t_bn, s.last('act')], inc='dve')
                    t_exp = [None, None]
                    t_msk = [None, None]
                    t_den = [None, None]
                    for kt in range(nkt):
                        par = kt % 2
                        t_sT = s.emit('tensor', lambda e, kt=kt, par=par: e.matmul(
                            b['pb'][par][:, :], b['ksl'][:, kt * 128:(kt + 1) * 128],
                            b['q_sb'][:, :], start=True, stop=True, skip_group_check=True),
                            waits=[t_exp[par], t_q] + ([t_k, t_v, s.last('dve')] if kt == 0 else []),
                            inc='pe')
                        t_exp[par] = s.emit('scalar', lambda e, par=par: e.activation(
                            out=b['exp_sb'][par][:, :], in_=b['pb'][par][:, :], func=AF.Exp,
                            bias=b['negB'][:, 0:1]),
                            waits=[t_sT, t_nb, t_den[par], t_msk[par]], inc='act')
                        t_msk[par] = s.emit('vector', lambda e, par=par, kt=kt: e.scalar_tensor_tensor(
                            out=b['exp_sb'][par][:, :], in0=b['qpos_rep'][:, :],
                            scalar=b['kpos'][:, kt:kt + 1], in1=b['exp_sb'][par][:, :],
                            op0=ALU.is_ge, op1=ALU.mult),
                            waits=[t_exp[par], t_qrep, pre[1]], inc='dve')
                        s.emit('tensor', lambda e, kt=kt, par=par, nkt=nkt: e.matmul(
                            b['pb'][2][:, :], b['vsl'][:, kt * 128:(kt + 1) * 128],
                            b['exp_sb'][par][:, :], start=(kt == 0), stop=(kt == nkt - 1),
                            skip_group_check=True),
                            waits=[t_msk[par]], inc=None)
                        t_den[par] = s.emit('tensor', lambda e, kt=kt, par=par, nkt=nkt: e.matmul(
                            b['pb'][3][0:1, :], b['ones_col'][:, 0:1],
                            b['exp_sb'][par][:, :], start=(kt == 0), stop=(kt == nkt - 1),
                            skip_group_check=True), inc='pe')
                    t_den_last = t_den[(nkt - 1) % 2]
                    t_rc = s.emit('vector', lambda e: e.reciprocal(
                        b['rcp'][:, :], b['pb'][3][0:1, :]), waits=[t_den_last], inc='dve')
                    t_rcb = s.emit('vector', lambda e: e.tensor_copy(
                        b['rcpb'][:, :], b['rcp'][:, :]), waits=[t_rc], inc='dve')
                    t_rr = s.emit('tensor', lambda e: e.matmul(
                        b['pb'][5][:, :], b['posrow'][0:1, :], b['rcpb'][:, :],
                        start=True, stop=True, skip_group_check=True),
                        waits=[t_rcb, pre[5]], inc='pe')
                    t_re = s.emit('vector', lambda e: e.tensor_copy(
                        b['rrep'][:, :], b['pb'][5][:, :]), waits=[t_rr], inc='dve')
                    t_ctx_all = s.emit('vector', lambda e, h=h: e.tensor_tensor(
                        out=b['ctx_sb'][:, h * TT:(h + 1) * TT], in0=b['pb'][2][:, :],
                        in1=b['rrep'][:, :], op=ALU.mult),
                        waits=[t_re], inc='dve')
            last_ev, _ = emit_wsweep(s, b, tile, wo, 2, 1.0, extra_mm=None,
                                     first_waits=[t_ctx_all], rhs='y')
            last_store_gate = last_ev
        st_toks = _emit_x_store(s, b, x_out, waits=[last_store_gate])
        s.wait_only('sync', st_toks)
        s.finalize()
    return nc


def _prep_shared(inp):
    p = {}
    p['bw0'] = _tile_w(np.ascontiguousarray(inp['L0_ffn_base_W'].T))
    p['rt0'] = np.ascontiguousarray(np.asarray(inp['L0_ffn_router_W']).T.reshape(NCH, 128, 8).transpose(1, 0, 2)).reshape(128, NCH * 8)
    p['la0'] = np.ascontiguousarray(
        np.asarray(inp['L0_ffn_lora_A']).transpose(1, 0, 2).reshape(D, 128)
        .reshape(NCH, 128, 128).transpose(1, 0, 2)).reshape(128, NCH * 128)
    p['lb0'] = np.ascontiguousarray(np.asarray(inp['L0_ffn_lora_B']).reshape(128, D))
    p['wq0'] = _tile_w(inp['L0_attn_Wq'])
    p['wk0'] = _tile_w(inp['L0_attn_Wk'])
    p['wv0'] = np.ascontiguousarray(inp['L0_attn_Wv']).reshape(NCH, 128, 1024)
    p['wo0'] = _tile_w(inp['L0_attn_Wo'])
    p['bw1'] = _tile_w(np.ascontiguousarray(inp['L1_ffn_W_base'].T))
    p['enc1'] = np.ascontiguousarray(
        np.concatenate([np.asarray(inp['L1_ffn_enc0']).T,
                        np.asarray(inp['L1_ffn_enc1']).T], axis=1)
        .reshape(NCH, 128, 40).transpose(1, 0, 2)).reshape(128, NCH * 40)
    p['bb1'] = np.ascontiguousarray(np.asarray(inp['L1_ffn_b_base']).reshape(NCH, 128).T)
    p['c0'] = np.ascontiguousarray(inp['L1_ffn_coefs0']).reshape(3, NCH, 128, 32)
    p['c1'] = np.ascontiguousarray(inp['L1_ffn_coefs1']).reshape(3, NCH, 128, 8)
    p['mc0'] = np.ascontiguousarray(inp['L1_ffn_mask0']).view(np.uint8).reshape(3, NCH, 128, 32)
    p['mc1'] = np.ascontiguousarray(inp['L1_ffn_mask1']).view(np.uint8).reshape(3, NCH, 128, 8)
    p['wq1'] = _tile_w(inp['L1_attn_Wq'])
    p['wk1'] = _tile_w(inp['L1_attn_Wk'])
    p['wv1'] = np.ascontiguousarray(inp['L1_attn_Wv']).reshape(NCH, 128, 1024)
    p['wo1'] = _tile_w(inp['L1_attn_Wo'])
    p['mt'] = _cached('mt', _mt_bf16).reshape(NCH, 128, D)
    p['e16'] = _cached('e16', _e16)
    p['ident'] = _cached('ident', lambda: np.eye(128, dtype=np.float32))
    p['perm'] = _cached('perm', _perm_rot)
    p['kpos'] = _cached('kpos', lambda: (
        np.arange(32, dtype=np.float32)[None, :] * 128 +
        np.arange(128, dtype=np.float32)[:, None]).astype(np.float32))
    p['negrow'] = _cached('negrow', lambda: np.full((1, 128), -1.02, dtype=np.float32).astype(BF))
    p['posrow'] = _cached('posrow', lambda: np.ones((1, 128), dtype=np.float32).astype(BF))
    return p


def _prep_cores():
    out = []
    scale = 1.0 / math.sqrt(128.0)
    for c in range(NCORES):
        pos = _core_positions(c)
        c0q, s0q = _cached(('r0', c), lambda pos=pos: _rope_l0(pos))
        c1q, s1q = _cached(('r1', c), lambda pos=pos: _rope_l1(pos))
        out.append(dict(
            cq0=np.ascontiguousarray(c0q * scale), sq0=np.ascontiguousarray(s0q * scale),
            ck0=c0q, sk0=s0q,
            cq1=np.ascontiguousarray(c1q * scale), sq1=np.ascontiguousarray(s1q * scale),
            ck1=c1q, sk1=s1q,
            qpos=np.ascontiguousarray(pos.astype(np.float32)[None, :])))
    return out


def _gather_kv(res, cores=range(NCORES)):
    Kg = [np.zeros((1024, 4096), dtype=BF) for _ in range(2)]
    Vg = [np.zeros((4096, 1024), dtype=BF) for _ in range(2)]
    KNg = [np.zeros((8, 4096), dtype=np.float32) for _ in range(2)]
    for c in cores:
        bt = c // 4
        m, hch = _core_chunks(c)
        k = res[c]['k_out']; v = res[c]['v_out']; kn = res[c]['kn_out']
        Kg[bt][:, 512 * m:512 * m + 512] = k[:, 0:512]
        Kg[bt][:, 512 * hch:512 * hch + 512] = k[:, 512:1024]
        Vg[bt][512 * m:512 * m + 512, :] = v[0:512, :]
        Vg[bt][512 * hch:512 * hch + 512, :] = v[512:1024, :]
        KNg[bt][:, 512 * m:512 * m + 512] = kn[:, 0:512]
        KNg[bt][:, 512 * hch:512 * hch + 512] = kn[:, 512:1024]
    kg_f = [np.ascontiguousarray(Kg[bt].reshape(8, 128, 4096)) for bt in range(2)]
    vg_f = [np.ascontiguousarray(
        Vg[bt].reshape(32, 128, 8, 128).transpose(2, 1, 0, 3).reshape(8, 128, 4096))
        for bt in range(2)]
    return kg_f, vg_f, KNg


_NC_CACHE = {}

def _get_nc(kind, layer):
    key = (kind, layer)
    if key not in _NC_CACHE:
        if kind == 'ffn':
            _NC_CACHE[key] = build_ffn_stage(layer)
        elif kind == 'qkv':
            _NC_CACHE[key] = build_qkv_stage(layer)
        else:
            _NC_CACHE[key] = build_att_stage(layer)
    return _NC_CACHE[key]


def _run(nc, in_maps):
    res = run_bass_kernel_spmd(nc, in_maps, core_ids=list(range(NCORES)))
    return res.results


def kernel_unfused(**inp):
    p = _prep_shared(inp)
    pc = _prep_cores()
    X = np.asarray(inp['L0_attn_X'])
    xs = [np.ascontiguousarray(X[c // 4, _core_positions(c), :].T.astype(np.float32))
          for c in range(NCORES)]

    def ffn_maps(layer, xs_cur):
        maps = []
        for c in range(NCORES):
            m = {'x_in': xs_cur[c], 'bw': p['bw0'] if layer == 0 else p['bw1']}
            if layer == 0:
                m.update(rt=p['rt0'], la=p['la0'], lb=p['lb0'], e16=p['e16'],
                         ident=p['ident'])
            else:
                m.update(enc=p['enc1'], bbias=p['bb1'], c0=p['c0'], c1=p['c1'],
                         mc0=p['mc0'], mc1=p['mc1'], mt=p['mt'])
            maps.append(m)
        return maps

    def qkv_maps(layer, xs_cur):
        maps = []
        for c in range(NCORES):
            t = pc[c]
            if layer == 0:
                m = dict(x_in=xs_cur[c], wq=p['wq0'], wk=p['wk0'], wv=p['wv0'],
                         cq=t['cq0'], sq=t['sq0'], ck=t['ck0'], sk=t['sk0'],
                         perm=p['perm'])
            else:
                m = dict(x_in=xs_cur[c], wq=p['wq1'], wk=p['wk1'], wv=p['wv1'],
                         cq=t['cq1'], sq=t['sq1'], ck=t['ck1'], sk=t['sk1'],
                         perm=p['perm'])
            maps.append(m)
        return maps

    def att_maps(layer, xs_cur, qk):
        kg_f, vg_f, KNg = qk['gather']
        maps = []
        for c in range(NCORES):
            bt = c // 4
            maps.append(dict(
                x_in=xs_cur[c], q_in=qk['res'][c]['q_out'], kg=kg_f[bt], vg=vg_f[bt],
                kng=KNg[bt], qm=qk['res'][c]['qm_out'],
                wo=p['wo0'] if layer == 0 else p['wo1'],
                qpos=pc[c]['qpos'], kpos=p['kpos'], ident=p['ident'],
                negrow=p['negrow'], posrow=p['posrow']))
        return maps

    cur = xs
    for layer in (0, 1):
        res = _run(_get_nc('ffn', layer), ffn_maps(layer, cur))
        cur = [res[c]['x_out'] for c in range(NCORES)]
        qres = _run(_get_nc('qkv', layer), qkv_maps(layer, cur))
        qk = {'res': qres, 'gather': _gather_kv(qres)}
        res = _run(_get_nc('att', layer), att_maps(layer, cur, qk))
        cur = [res[c]['x_out'] for c in range(NCORES)]
        res = _run(_get_nc('ffn', layer), ffn_maps(layer, cur))
        cur = [res[c]['x_out'] for c in range(NCORES)]

    out = np.empty((2, 4096, D), dtype=np.float32)
    for c in range(NCORES):
        out[c // 4, _core_positions(c), :] = cur[c].T
    return out


def _carve(nc, base, cursor, name, shape, dtype):
    nbytes = int(np.prod(shape[1:])) * (2 if dtype in (BF16,) else (1 if dtype == U8 else 4))
    off = (cursor + 31) & ~31
    h = nc.alloc_sbuf_tensor_at(name, shape, dtype, offset=base + off)
    return h, off + nbytes


def build_fused():
    stack = contextlib.ExitStack()
    with stack:
        nc = bass.Bass("TRN2", target_bir_lowering=False, debug=False)
        b = {}
        x_in = nc.dram_tensor("x_in", [D, TOK], F32, kind="ExternalInput")
        x_out = nc.dram_tensor("x_out", [D, TOK], F32, kind="ExternalOutput")
        dri = {}
        for l in (0, 1):
            dri[('bw', l)] = nc.dram_tensor(f"bw{l}", [NCH, 128, NCH * 128], F32, kind="ExternalInput")
            dri[('wq', l)] = nc.dram_tensor(f"wq{l}", [NCH, 128, NCH * 128], F32, kind="ExternalInput")
            dri[('wk', l)] = nc.dram_tensor(f"wk{l}", [8, 128, NCH * 128], F32, kind="ExternalInput")
            dri[('wv', l)] = nc.dram_tensor(f"wv{l}", [NCH, 128, 1024], F32, kind="ExternalInput")
            dri[('wo', l)] = nc.dram_tensor(f"wo{l}", [NCH, 128, NCH * 128], F32, kind="ExternalInput")
            for t in ('cq', 'sq', 'ck', 'sk'):
                dri[(t, l)] = nc.dram_tensor(f"{t}{l}", [128, TOK], F32, kind="ExternalInput")
        rt_d = nc.dram_tensor("rt", [128, NCH * 8], F32, kind="ExternalInput")
        la_d = nc.dram_tensor("la", [128, NCH * 128], F32, kind="ExternalInput")
        lb_d = nc.dram_tensor("lb", [128, NCH * 128], F32, kind="ExternalInput")
        enc_d = nc.dram_tensor("enc", [128, NCH * 40], F32, kind="ExternalInput")
        bbias_d = nc.dram_tensor("bbias", [128, NCH], F32, kind="ExternalInput")
        c0_d = nc.dram_tensor("c0", [3, NCH, 128, 32], F32, kind="ExternalInput")
        c1_d = nc.dram_tensor("c1", [3, NCH, 128, 8], F32, kind="ExternalInput")
        mc0_d = nc.dram_tensor("mc0", [3, NCH, 128, 32], U8, kind="ExternalInput")
        mc1_d = nc.dram_tensor("mc1", [3, NCH, 128, 8], U8, kind="ExternalInput")
        mt_d = nc.dram_tensor("mt", [NCH, 128, D], BF16, kind="ExternalInput")
        e16_d = nc.dram_tensor("e16", [8, 128], F32, kind="ExternalInput")
        ident_d = nc.dram_tensor("ident", [128, 128], F32, kind="ExternalInput")
        perm_d = nc.dram_tensor("perm", [128, 128], BF16, kind="ExternalInput")
        kpos_d = nc.dram_tensor("kpos", [128, 32], F32, kind="ExternalInput")
        qpos_d = nc.dram_tensor("qpos", [1, TOK], F32, kind="ExternalInput")
        negrow_d = nc.dram_tensor("negrow", [1, 128], BF16, kind="ExternalInput")
        posrow_d = nc.dram_tensor("posrow", [1, 128], BF16, kind="ExternalInput")
        q_scr = nc.dram_tensor("q_scr", [D, TOK], BF16, kind="Internal")
        kv_stg = nc.dram_tensor("kv_stg", [2056, TOK], BF16, kind="Internal")
        kv_gth = nc.dram_tensor("kv_gth", [4 * 2056, TOK], BF16, kind="Internal")

        _alloc_common(nc, stack, b, ctx=False)
        b['y_sb'] = nc.alloc_sbuf_tensor("y_sb", [128, NCH * TT], BF16)
        b['ctx_sb'] = b['y_sb']
        b['wslab'] = [nc.alloc_sbuf_tensor(f"wslab{i}", [128, NCH * 128], BF16)
                      for i in range(2)]
        b['ksl'] = b['wslab'][0]
        b['vsl'] = b['wslab'][1]
        b['rt_sb'] = nc.alloc_sbuf_tensor("rt_sb", [128, NCH * 8], BF16)
        b['e16'] = nc.alloc_sbuf_tensor("e16_sb", [8, 128], F32)
        b['ident'] = nc.alloc_sbuf_tensor("ident_sb", [128, 128], F32)
        b['perm_sb'] = nc.alloc_sbuf_tensor("perm_sb", [128, 128], BF16)
        b['bb2'] = nc.alloc_sbuf_tensor("bb2", [128, NCH], F32)
        b['kpos'] = nc.alloc_sbuf_tensor("kpos_sb", [128, 32], F32)
        b['qm_sb'] = nc.alloc_sbuf_tensor("qm_sb", [1, 64], F32)
        b['negrow'] = nc.alloc_sbuf_tensor("negrow_sb", [1, 128], BF16)
        b['posrow'] = nc.alloc_sbuf_tensor("posrow_sb", [1, 128], BF16)

        szR = 15744
        baseR = nc.bump_sbuf(szR)[0]
        cur = 0
        b['e_sb'], cur = _carve(nc, baseR, cur, "e_sb", [40, TT], BF16)
        b['wcomb'], cur = _carve(nc, baseR, cur, "wcomb", [128, NCH * 40], BF16)
        b['g_sb'], cur = _carve(nc, baseR, cur, "g_sb", [40, D], BF16)
        b['enc_sb'], cur = _carve(nc, baseR, cur, "enc_sb", [128, NCH * 40], BF16)
        assert cur <= szR, cur
        cur = 0
        b['w32a'], cur = _carve(nc, baseR, cur, "w32a", [128, 32], F32)
        b['w32b'], cur = _carve(nc, baseR, cur, "w32b", [128, 32], F32)
        b['w32p'], cur = _carve(nc, baseR, cur, "w32p", [128, 32], F32)
        b['w4a'], cur = _carve(nc, baseR, cur, "w4a", [128, 4], F32)
        b['w4b'], cur = _carve(nc, baseR, cur, "w4b", [128, 4], F32)
        b['wfm'], cur = _carve(nc, baseR, cur, "wfm", [8, TT], F32)
        b['wrep'], cur = _carve(nc, baseR, cur, "wrep", [128, TT], F32)
        b['midw'], cur = _carve(nc, baseR, cur, "midw", [128, TT], BF16)
        b['ab_sb'], cur = _carve(nc, baseR, cur, "ab_sb", [128, NCH * 128], BF16)
        assert cur <= szR, cur
        cur = 0
        b['ct_sb'], cur = _carve(nc, baseR, cur, "ct_sb", [128, TOK], BF16)
        b['st_sb'], cur = _carve(nc, baseR, cur, "st_sb", [128, TOK], BF16)
        b['qb'], cur = _carve(nc, baseR, cur, "qb", [128, TT], BF16)
        b['rtmp1'], cur = _carve(nc, baseR, cur, "rtmp1", [128, TT], F32)
        b['qrope'], cur = _carve(nc, baseR, cur, "qrope", [128, TT], BF16)
        b['qsq'], cur = _carve(nc, baseR, cur, "qsq", [128, TT], BF16)
        b['kn_row'], cur = _carve(nc, baseR, cur, "kn_row", [1, TT], F32)
        assert cur <= szR, cur
        b['wv_sb'] = [b['ct_sb'], b['st_sb']]
        b['vtmp'] = [b['qb'], b['qrope']]
        cur = 0
        b['q_sb2'] = [None, None]
        b['q_sb2'][0], cur = _carve(nc, baseR, cur, "q_sb0", [128, TT], BF16)
        b['q_sb2'][1], cur = _carve(nc, baseR, cur, "q_sb1", [128, TT], BF16)
        b['exp_sb'] = [None] * 4
        b['exp_sb'][0], cur = _carve(nc, baseR, cur, "exp0", [128, TT], BF16)
        b['exp_sb'][1], cur = _carve(nc, baseR, cur, "exp1", [128, TT], BF16)
        b['qpos_rep'], cur = _carve(nc, baseR, cur, "qpos_rep", [128, TT], F32)
        b['qpos'], cur = _carve(nc, baseR, cur, "qpos_sb", [1, TT], F32)
        knb_off = cur
        b['exp_sb'][2], cur = _carve(nc, baseR, cur, "exp2", [128, TT], BF16)
        b['exp_sb'][3], cur = _carve(nc, baseR, cur, "exp3", [128, TT], BF16)
        b['knb'] = nc.alloc_sbuf_tensor_at("knb", [8, TOK], BF16, offset=baseR + knb_off)
        b['mkacc'], cur = _carve(nc, baseR, cur, "mkacc", [8, 8], F32)
        b['mk8'], cur = _carve(nc, baseR, cur, "mk8", [8, 1], F32)
        b['mk2'], cur = _carve(nc, baseR, cur, "mk2", [1, 8], F32)
        b['negB2'] = [None, None]
        b['negB2'][0], cur = _carve(nc, baseR, cur, "negB0", [128, 1], F32)
        b['negB2'][1], cur = _carve(nc, baseR, cur, "negB1", [128, 1], F32)
        b['b2row2'] = [None, None]
        b['b2row2'][0], cur = _carve(nc, baseR, cur, "b2row0", [1, 1], F32)
        b['b2row2'][1], cur = _carve(nc, baseR, cur, "b2row1", [1, 1], F32)
        b['b1row2'] = [None, None]
        b['b1row2'][0], cur = _carve(nc, baseR, cur, "b1row0", [1, 1], F32)
        b['b1row2'][1], cur = _carve(nc, baseR, cur, "b1row1", [1, 1], F32)
        b['b1bf2'] = [None, None]
        b['b1bf2'][0], cur = _carve(nc, baseR, cur, "b1bf0", [1, 1], BF16)
        b['b1bf2'][1], cur = _carve(nc, baseR, cur, "b1bf1", [1, 1], BF16)
        b['rcp'], cur = _carve(nc, baseR, cur, "rcp", [1, TT], F32)
        b['rcpb'], cur = _carve(nc, baseR, cur, "rcpb", [1, TT], BF16)
        b['rrep'], cur = _carve(nc, baseR, cur, "rrep", [128, TT], F32)
        assert cur <= szR, cur

        s = Sched(nc, stack)
        _emit_const_init(s, b)
        x_toks = _emit_x_load(s, b, x_in)
        consts = [s.dma('sync', b['perm_sb'][:, :], perm_d.ap()),
                  s.dma('sync', b['kpos'][:, :], kpos_d.ap()),
                  s.dma('sync', b['negrow'][:, :], negrow_d.ap()),
                  s.dma('sync', b['posrow'][:, :], posrow_d.ap())]

        for layer in (0, 1):
            dr = dict(bw=dri[('bw', layer)])
            if layer == 0:
                dr.update(rt=rt_d, la=la_d, lb=lb_d, e16=e16_d, ident=ident_d)
            else:
                dr.update(enc=enc_d, bbias=bbias_d, c0=c0_d, c1=c1_d,
                          mc0=mc0_d, mc1=mc1_d, mt=mt_d)
            emit_ffn_body(s, b, dr, layer, x_toks if layer == 0 else [])
            s.fence()
            t_qm = emit_qkv_fused(s, b, dri, layer, q_scr, kv_stg, consts)
            s.wait_only('gpsimd', [s.last('dmah'), s.last('dmas')])
            t_cc = s.emit('gpsimd', lambda e: e.collective_compute(
                "AllGather", ALU.bypass,
                replica_groups=[[0, 1, 2, 3], [4, 5, 6, 7]],
                ins=[kv_stg.ap().opt()], outs=[kv_gth.ap().opt()]), inc='cc', k=1)
            s.fence()
            emit_att_fused(s, b, dri, layer, q_scr, kv_gth, qpos_d, t_cc, t_qm, consts)
            s.fence()
            emit_ffn_body(s, b, dr, layer, [])
            s.fence()
        st_toks = _emit_x_store(s, b, x_out, waits=[s.last('dve')])
        s.wait_only('sync', st_toks)
        s.finalize()
    return nc


def emit_qkv_fused(s, b, dri, layer, q_scr, kv_stg, consts):
    wq, wk, wv = dri[('wq', layer)], dri[('wk', layer)], dri[('wv', layer)]
    t_qm_last = None
    last_y = None
    for tile in range(2):
        t_y = emit_ln(s, b, tile, extra_first_wait=consts +
                      ([last_y] if last_y else []))
        last_y = t_y
        for phase, nh, wdram, ctd, std in (
                ('q', NCH, wq, dri[('cq', layer)], dri[('sq', layer)]),
                ('k', 8, wk, dri[('ck', layer)], dri[('sk', layer)])):
            ct, st_ = b['ct_sb'], b['st_sb']
            t_tb = [s.dma('gpsimd', b['ct_sb'][:, :], ctd.ap(), waits=[s.last('dve')]),
                    s.dma('gpsimd', b['st_sb'][:, :], std.ap())]
            slab_tok = [None, None]
            for h in range(2):
                slab_tok[h] = s.dma('gpsimd', b['wslab'][h][:, :], wdram.ap()[h],
                                    waits=[s.last('pe')])
            prev_tn = None
            for h in range(nh):
                par = h % 2
                bankA = b['pb'][par]
                bankB = b['pb'][2 + par]
                t_proj = None
                for c in range(NCH):
                    waits = []
                    if c == 0:
                        waits = [slab_tok[par], t_y, s.last('dve')] + \
                            (t_tb if h < 2 else [])
                    t_proj = s.emit('tensor', lambda e, c=c, par=par, bankA=bankA: e.matmul(
                        bankA[:, :], b['wslab'][par][:, c * 128:(c + 1) * 128],
                        b['y_sb'][:, c * TT:(c + 1) * TT],
                        start=(c == 0), stop=(c == NCH - 1), skip_group_check=True),
                        waits=waits, inc='pe' if c == NCH - 1 else None)
                if h + 2 < nh:
                    slab_tok[par] = s.dma('gpsimd', b['wslab'][par][:, :],
                                          wdram.ap()[h + 2], waits=[t_proj])
                t_rope = emit_rope_head(s, b, tile, t_proj, bankA, bankB, ct, st_,
                                        b['qrope'], war_dve=s.last('dmah'))
                if phase == 'q':
                    dst = q_scr.ap()[h * 128:(h + 1) * 128, tile * TT:tile * TT + TT]
                else:
                    dst = kv_stg.ap()[h * 128:(h + 1) * 128, tile * TT:tile * TT + TT]
                s.dma('sync', dst, b['qrope'][:, :], waits=[t_rope])
                t_sq = s.emit('scalar', lambda e: e.activation(
                    out=b['qsq'][:, :], in_=b['qrope'][:, :], func=AF.Square),
                    waits=[t_rope, prev_tn], inc='act')
                t_n = s.emit('tensor', lambda e: e.matmul(
                    b['pb'][4][0:1, :], b['ones_col'][:, 0:1], b['qsq'][:, :],
                    start=True, stop=True, skip_group_check=True),
                    waits=[t_sq, s.last('dve')], inc='pe')
                prev_tn = t_n
                if phase == 'q':
                    slot = tile * 32 + h
                    t_qm_last = s.emit('vector', lambda e, slot=slot: e.reduce_max(
                        b['qm_sb'][:, slot:slot + 1], b['pb'][4][0:1, :], axis=AX.X),
                        waits=[t_n], inc='dve')
                else:
                    t_kn = s.emit('vector', lambda e: e.tensor_copy(
                        b['kn_row'][:, :], b['pb'][4][0:1, :]),
                        waits=[t_n, s.last('dmas')], inc='dve')
                    s.dma('gpsimd', kv_stg.ap()[2048 + h:2049 + h, tile * TT:tile * TT + TT],
                          b['kn_row'][:, :], waits=[t_kn])
        wv_tok = [None, None]
        for c in range(2):
            wv_tok[c] = s.dma('gpsimd', b['wv_sb'][c][:, :], wv.ap()[c],
                              waits=[s.last('pe'), s.last('dve')])
        t_vmm = None
        for c in range(NCH):
            par = c % 2
            for sub in range(4):
                for dv in range(2):
                    bank = b['pb'][sub * 2 + dv]
                    waits = []
                    if sub == 0 and dv == 0:
                        waits = [wv_tok[par]]
                        if c == 0:
                            waits.append(s.last('dve'))
                    t_vmm = s.emit('tensor', lambda e, c=c, par=par, sub=sub, dv=dv, bank=bank: e.matmul(
                        bank[:, :], b['y_sb'][:, c * TT + sub * 128:c * TT + sub * 128 + 128],
                        b['wv_sb'][par][:, dv * TT:(dv + 1) * TT],
                        start=(c == 0), stop=(c == NCH - 1), skip_group_check=True),
                        waits=waits, inc='pe' if (sub == 3 and dv == 1) else None)
            if c + 2 < NCH:
                wv_tok[par] = s.dma('gpsimd', b['wv_sb'][par][:, :], wv.ap()[c + 2],
                                    waits=[t_vmm])
        for sub in range(4):
            for dv in range(2):
                bank = b['pb'][sub * 2 + dv]
                vp = (sub * 2 + dv) % 2
                t_ev = s.emit('vector', lambda e, bank=bank, vp=vp: e.tensor_copy(
                    b['vtmp'][vp][:, :], bank[:, :]),
                    waits=[t_vmm, s.last('dmah')], inc='dve')
                s.dma('sync',
                      kv_stg.ap()[1024 + tile * TT + sub * 128: 1024 + tile * TT + sub * 128 + 128,
                                  dv * TT:(dv + 1) * TT],
                      b['vtmp'][vp][:, :], waits=[t_ev])
    return t_qm_last


def emit_att_fused(s, b, dri, layer, q_scr, kv_gth, qpos_d, t_cc, t_qm, consts):
    wo = dri[('wo', layer)]
    t_mk = None
    for r4 in range(4):
        t_l = s.dma('sync', b['knb'][:, :],
                    kv_gth.ap()[r4 * 2056 + 2048: r4 * 2056 + 2056, :],
                    waits=[t_cc] + ([t_mk] if t_mk else []))
        t_mk = s.emit('vector', lambda e, r4=r4: e.reduce_max(
            b['mkacc'][:, r4:r4 + 1], b['knb'][:, :], axis=AX.X),
            waits=[t_l], inc='dve')
    t_mk = s.emit('vector', lambda e: e.reduce_max(
        b['mk8'][:, :], b['mkacc'][:, 0:4], axis=AX.X), waits=[t_mk], inc='dve')
    t_tr = s.emit('tensor', lambda e: e.transpose(
        b['pb'][6][0:1, 0:8], b['mk8'][:, 0:1], b['ident'][0:8, 0:8]),
        waits=[t_mk, s.last('dve')], inc='pe')
    t_mk2 = s.emit('vector', lambda e: e.tensor_copy(
        b['mk2'][:, :], b['pb'][6][0:1, 0:8]), waits=[t_tr], inc='dve')

    for tile in range(2):
        nkt = 16 if tile == 0 else 32
        t_qp = s.dma('sync', b['qpos'][:, :],
                     qpos_d.ap()[:, tile * TT:tile * TT + TT], waits=[s.last('dve')])
        t_qr = s.emit('tensor', lambda e, tile=tile: e.matmul(
            b['pb'][6][:, :], b['ones_row_f32'][0:1, :], b['qpos'][0:1, :],
            start=True, stop=True, skip_group_check=True),
            waits=[t_qp, s.last('dve')], inc='pe')
        t_qrep = s.emit('vector', lambda e: e.tensor_copy(
            b['qpos_rep'][:, :], b['pb'][6][:, :]), waits=[t_qr], inc='dve')
        t_ctx_all = None
        t_q = [None, None]
        t_q_lastuse = [None, None]
        t_exp_head = [None, None]
        pbS = [b['pb'][0], b['pb'][1], b['pb'][6], b['pb'][7]]
        for kvh in range(8):
            kt_toks = []
            for r4 in range(4):
                halves = (0,) if tile == 0 else (0, 1)
                for half in halves:
                    chunk = r4 if half == 0 else 7 - r4
                    kt_toks.append(s.dma(
                        'sync', b['ksl'][:, 512 * chunk:512 * chunk + 512],
                        kv_gth.ap()[r4 * 2056 + kvh * 128: r4 * 2056 + kvh * 128 + 128,
                                    half * 512:half * 512 + 512],
                        waits=[s.last('pe')] if (r4 == 0 and half == 0) else []))
                    vin = kv_gth.ap().rearrange("(r t) d -> r t d", t=2056)[r4]
                    kt_toks.append(s.dma(
                        'gpsimd',
                        b['vsl'][:, 512 * chunk: 512 * chunk + 512]
                        .rearrange("p (q d) -> p q d", d=128),
                        vin[1024 + half * 512:1024 + half * 512 + 512]
                        .rearrange("(q p) d -> p q d", p=128)[:, :, kvh * 128:kvh * 128 + 128],
                        waits=[s.last('pe')] if (r4 == 0 and half == 0) else []))
            for qh in range(4):
                h = kvh * 4 + qh
                hp = h % 2
                if qh == 0 and kvh == 0:
                    t_q[hp] = s.dma('sync', b['q_sb2'][hp][:, :],
                                    q_scr.ap()[h * 128:(h + 1) * 128,
                                               tile * TT:tile * TT + TT],
                                    waits=[s.last('pe')])
                t_b2 = s.emit('vector', lambda e, tile=tile, h=h, kvh=kvh, hp=hp: e.tensor_tensor(
                    out=b['b2row2'][hp][:, :],
                    in0=b['qm_sb'][:, tile * 32 + h:tile * 32 + h + 1],
                    in1=b['mk2'][:, kvh:kvh + 1], op=ALU.mult),
                    waits=[t_mk2, t_qm, t_exp_head[hp]], inc='dve')
                t_b1 = s.emit('scalar', lambda e, hp=hp: e.activation(
                    out=b['b1row2'][hp][:, :], in_=b['b2row2'][hp][:, :], func=AF.Sqrt),
                    waits=[t_b2], inc='act')
                t_bb = s.emit('vector', lambda e, hp=hp: e.tensor_copy(
                    b['b1bf2'][hp][:, :], b['b1row2'][hp][:, :]),
                    waits=[t_b1], inc='dve')
                t_bn = s.emit('tensor', lambda e, hp=hp: e.matmul(
                    b['pb'][4][:, hp:hp + 1], b['negrow'][0:1, :], b['b1bf2'][hp][:, :],
                    start=True, stop=True, skip_group_check=True),
                    waits=[t_bb], inc='pe')
                t_nb = s.emit('vector', lambda e, hp=hp: e.tensor_copy(
                    b['negB2'][hp][:, :], b['pb'][4][:, hp:hp + 1]),
                    waits=[t_bn, t_exp_head[hp]], inc='dve')
                if h + 1 < 32:
                    np_ = (h + 1) % 2
                    t_q[np_] = s.dma('sync', b['q_sb2'][np_][:, :],
                                     q_scr.ap()[(h + 1) * 128:(h + 2) * 128,
                                                tile * TT:tile * TT + TT],
                                     waits=[t_q_lastuse[np_]])
                t_exp = [None] * 4
                t_msk = [None] * 4
                t_den = [None] * 4
                for kt in range(nkt):
                    par = kt % 4
                    t_sT = s.emit('tensor', lambda e, kt=kt, par=par, hp=hp: e.matmul(
                        pbS[par][:, :], b['ksl'][:, kt * 128:(kt + 1) * 128],
                        b['q_sb2'][hp][:, :], start=True, stop=True, skip_group_check=True),
                        waits=[t_exp[par], t_q[hp]] +
                        (kt_toks + [s.last('dve')] if kt == 0 else []),
                        inc='pe')
                    if kt == nkt - 1:
                        t_q_lastuse[hp] = t_sT
                    t_exp[par] = s.emit('scalar', lambda e, par=par, hp=hp: e.activation(
                        out=b['exp_sb'][par][:, :], in_=pbS[par][:, :], func=AF.Exp,
                        bias=b['negB2'][hp][:, 0:1]),
                        waits=[t_sT, t_nb, t_den[par], t_msk[par]], inc='act')
                    t_msk[par] = s.emit('vector', lambda e, par=par, kt=kt: e.scalar_tensor_tensor(
                        out=b['exp_sb'][par][:, :], in0=b['qpos_rep'][:, :],
                        scalar=b['kpos'][:, kt:kt + 1], in1=b['exp_sb'][par][:, :],
                        op0=ALU.is_ge, op1=ALU.mult),
                        waits=[t_exp[par], t_qrep], inc='dve')
                    s.emit('tensor', lambda e, kt=kt, par=par, nkt=nkt: e.matmul(
                        b['pb'][2][:, :], b['vsl'][:, kt * 128:(kt + 1) * 128],
                        b['exp_sb'][par][:, :], start=(kt == 0), stop=(kt == nkt - 1),
                        skip_group_check=True),
                        waits=[t_msk[par]], inc=None)
                    t_den[par] = s.emit('tensor', lambda e, kt=kt, par=par, nkt=nkt: e.matmul(
                        b['pb'][3][0:1, :], b['ones_col'][:, 0:1],
                        b['exp_sb'][par][:, :], start=(kt == 0), stop=(kt == nkt - 1),
                        skip_group_check=True), inc='pe')
                t_exp_head[hp] = t_exp[(nkt - 1) % 4]
                t_den_last = t_den[(nkt - 1) % 4]
                t_rc = s.emit('vector', lambda e: e.reciprocal(
                    b['rcp'][:, :], b['pb'][3][0:1, :]), waits=[t_den_last], inc='dve')
                t_rcb = s.emit('vector', lambda e: e.tensor_copy(
                    b['rcpb'][:, :], b['rcp'][:, :]), waits=[t_rc], inc='dve')
                t_rr = s.emit('tensor', lambda e: e.matmul(
                    b['pb'][5][:, :], b['posrow'][0:1, :], b['rcpb'][:, :],
                    start=True, stop=True, skip_group_check=True),
                    waits=[t_rcb], inc='pe')
                t_re = s.emit('vector', lambda e: e.tensor_copy(
                    b['rrep'][:, :], b['pb'][5][:, :]), waits=[t_rr], inc='dve')
                t_ctx_all = s.emit('vector', lambda e, h=h: e.tensor_tensor(
                    out=b['ctx_sb'][:, h * TT:(h + 1) * TT], in0=b['pb'][2][:, :],
                    in1=b['rrep'][:, :], op=ALU.mult),
                    waits=[t_re], inc='dve')
        last_ev, _ = emit_wsweep(s, b, tile, wo, 2, 1.0, extra_mm=None,
                                 first_waits=[t_ctx_all], rhs='y')
    return last_ev


_FUSED_NC = [None]

def kernel(**inp):
    p = _prep_shared(inp)
    pc = _prep_cores()
    X = np.asarray(inp['L0_attn_X'])
    xs = [np.ascontiguousarray(X[c // 4, _core_positions(c), :].T.astype(np.float32))
          for c in range(NCORES)]
    if _FUSED_NC[0] is None:
        _FUSED_NC[0] = build_fused()
    nc = _FUSED_NC[0]
    maps = []
    for c in range(NCORES):
        t = pc[c]
        m = dict(x_in=xs[c],
                 bw0=p['bw0'], bw1=p['bw1'],
                 wq0=p['wq0'], wk0=p['wk0'], wv0=p['wv0'], wo0=p['wo0'],
                 wq1=p['wq1'], wk1=p['wk1'], wv1=p['wv1'], wo1=p['wo1'],
                 cq0=t['cq0'], sq0=t['sq0'], ck0=t['ck0'], sk0=t['sk0'],
                 cq1=t['cq1'], sq1=t['sq1'], ck1=t['ck1'], sk1=t['sk1'],
                 rt=p['rt0'], la=p['la0'], lb=p['lb0'],
                 enc=p['enc1'], bbias=p['bb1'], c0=p['c0'], c1=p['c1'],
                 mc0=p['mc0'], mc1=p['mc1'], mt=p['mt'],
                 e16=p['e16'], ident=p['ident'], perm=p['perm'],
                 kpos=p['kpos'], qpos=t['qpos'],
                 negrow=p['negrow'], posrow=p['posrow'])
        maps.append(m)
    res = _run(nc, maps)
    out = np.empty((2, 4096, D), dtype=np.float32)
    for c in range(NCORES):
        out[c // 4, _core_positions(c), :] = res[c]['x_out'].T
    return out


# version 05
# speedup vs baseline: 1.4545x; 1.4545x over previous
import numpy as np
import ml_dtypes
import math
import contextlib
import concourse.bass as bass
import concourse.mybir as mybir
from concourse.bass_utils import run_bass_kernel_spmd

F32 = mybir.dt.float32
BF16 = mybir.dt.bfloat16
I32 = mybir.dt.int32
U8 = mybir.dt.uint8
AF = mybir.ActivationFunctionType
ALU = mybir.AluOpType
AX = mybir.AxisListType
BF = ml_dtypes.bfloat16

NCORES = 8
D = 4096
NCH = 32
TOK = 1024
TT = 512
EPS = 1e-5


def _core_chunks(c):
    m = c % 4
    return m, 7 - m

def _core_positions(c):
    m, h = _core_chunks(c)
    return np.concatenate([np.arange(512 * m, 512 * m + 512),
                           np.arange(512 * h, 512 * h + 512)])

def _tile_w(Wm):
    din, dout = Wm.shape
    return np.ascontiguousarray(
        Wm.reshape(din // 128, 128, dout // 128, 128).transpose(2, 1, 0, 3)
    ).reshape(dout // 128, 128, (din // 128) * 128)

def _rope_l0(pos):
    inv = 1.0 / (10000.0 ** (np.arange(0, 64, dtype=np.float64) / 64.0))
    ang = inv[:, None] * pos[None, :].astype(np.float64)
    c, s = np.cos(ang), np.sin(ang)
    return (np.vstack([c, c]).astype(np.float32), np.vstack([s, s]).astype(np.float32))

def _rope_l1(pos):
    idx = np.arange(0, 128, 2, dtype=np.float64) / 128.0
    freq = 1.0 / (2.0 * math.pi) * (500000.0 ** (-idx))
    factor, lo, hi, L0 = 32.0, 1.0, 4.0, 8192
    freq_low = lo / L0
    hi_f = hi / L0
    freq_scaled = np.where(freq < freq_low, freq / factor, freq)
    smooth = np.clip((L0 * freq - lo) / (hi - lo), 0.0, 1.0)
    freq_smooth = (1.0 - smooth) * (freq / factor) + smooth * freq
    is_mid = (freq >= freq_low) & (freq <= hi_f)
    f = np.where(is_mid, freq_smooth, freq_scaled)
    phase = 2.0 * math.pi * pos[:, None].astype(np.float64) * f[None, :]
    c, s = np.cos(phase).T, np.sin(phase).T
    return (np.vstack([c, c]).astype(np.float32), np.vstack([s, s]).astype(np.float32))

def _perm_rot():
    P = np.zeros((128, 128), dtype=np.float32)
    for m in range(64):
        P[m + 64, m] = -1.0
        P[m, m + 64] = 1.0
    return P.astype(BF)

def _e16():
    E = np.zeros((8, 128), dtype=np.float32)
    for e in range(8):
        E[e, 16 * e:16 * e + 16] = 2.0
    return E

def _mask_atlas():
    pats = [np.ones((128, 512), np.float32)]
    i = np.arange(128)[:, None]
    j = np.arange(512)[None, :]
    for d in (0, 128, 256, 384):
        pats.append((i <= j - d).astype(np.float32))
    pats.append(np.zeros((128, 512), np.float32))
    return np.concatenate(pats, axis=1).astype(BF)

def _off_feed(c):
    out = []
    for tile, nkt in ((0, 16), (1, 32)):
        p = _core_chunks(c)[tile]
        for kt in range(nkt):
            dp = kt - 4 * p
            out.append(0 if dp < 0 else (1 + dp if dp <= 3 else 5))
    out += [0] * (64 - len(out))
    return np.array(out, dtype=np.int32)[None, :]

def _mt_bf16():
    n = 4096
    t = np.arange(n, dtype=np.float64)
    M = np.empty((n, n), dtype=np.float64)
    M[:, 0] = 1.0 / math.sqrt(n)
    jj = np.arange(1, n // 2, dtype=np.float64)
    ang = 2.0 * math.pi * t[:, None] * jj[None, :] / n
    M[:, 1:n - 1:2] = math.sqrt(2.0 / n) * np.cos(ang)
    M[:, 2:n - 1:2] = math.sqrt(2.0 / n) * np.sin(ang)
    M[:, n - 1] = ((-1.0) ** t) / math.sqrt(n)
    return np.ascontiguousarray(M.T).astype(BF)

_HC = {}

def _cached(key, fn):
    if key not in _HC:
        _HC[key] = fn()
    return _HC[key]


class Sched:
    ENGINES = ("tensor", "vector", "scalar", "gpsimd", "sync")

    def __init__(self, nc, stack):
        self.nc = nc
        self.stack = stack
        self.prog = {e: [] for e in self.ENGINES}
        self.sems = {}
        self.cnt = {}

    def sem(self, name):
        if name not in self.sems:
            self.sems[name] = self.stack.enter_context(self.nc.semaphore(name))
            self.cnt[name] = 0
        return self.sems[name]

    def emit(self, eng, fn, waits=(), inc=None, k=1):
        tok = None
        if inc is not None:
            self.sem(inc)
            self.cnt[inc] += k
            tok = (inc, self.cnt[inc])
        wl = []
        for t in waits:
            if t is None:
                continue
            sname, v = t
            self.sem(sname)
            wl.append((self.sems[sname], v))
        semh = self.sems[inc] if inc is not None else None

        def closure(e, _wl=tuple(wl), _fn=fn, _semh=semh, _k=k):
            for sh, v in _wl:
                e.wait_ge(sh, v)
            inst = _fn(e)
            if _semh is not None:
                inst.then_inc(_semh, _k)
        self.prog[eng].append(closure)
        return tok

    def dma(self, queue, out, in_, waits=()):
        sem = 'dmah' if queue == 'sync' else 'dmas'
        return self.emit(queue, lambda e, o=out, i=in_: e.dma_start(out=o, in_=i),
                         waits=waits, inc=sem, k=16)

    def wait_only(self, eng, waits):
        wl = []
        for t in waits:
            if t is None:
                continue
            sname, v = t
            self.sem(sname)
            wl.append((self.sems[sname], v))

        def closure(e, _wl=tuple(wl)):
            for sh, v in _wl:
                e.wait_ge(sh, v)
        self.prog[eng].append(closure)

    def fence(self):
        toks = [self.last(x) for x in ('pe', 'dve', 'act', 'dmah', 'dmas', 'cc')]
        toks = [t for t in toks if t is not None]
        for e in self.ENGINES:
            self.wait_only(e, toks)

    def last(self, name):
        if name not in self.cnt or self.cnt[name] == 0:
            return None
        return (name, self.cnt[name])

    def finalize(self):
        with self.nc.Block() as block:
            for eng_name in self.ENGINES:
                ops = list(self.prog[eng_name])
                if not ops:
                    continue
                dec = getattr(block, eng_name)

                def body(e, _ops=ops):
                    for f in _ops:
                        f(e)
                dec(body)



def emit_ln(s, b, tile, extra_first_wait=()):
    col = lambda c: c * TOK + tile * TT
    pe_s1 = [None, None]
    pe_s2 = [None, None]
    ac_sq = [None, None]
    t_first = list(extra_first_wait)
    last2 = None
    for c in range(NCH):
        par = c % 2
        t_cast = s.emit('vector', lambda e, c=c, par=par: e.tensor_copy(
            b['sc_bf'][par][:, :], b['x_sb'][:, col(c):col(c) + TT]),
            waits=(t_first if c == 0 else []) + [pe_s1[par], ac_sq[par]], inc='dve')
        t_sq = s.emit('scalar', lambda e, par=par: e.activation(
            out=b['sc_sq'][par][:, :], in_=b['sc_bf'][par][:, :], func=AF.Square),
            waits=[t_cast, pe_s2[par]], inc='act')
        ac_sq[par] = t_sq
        pe_s1[par] = s.emit('tensor', lambda e, c=c, par=par: e.matmul(
            b['pb'][0][0:1, :], b['ones_col'][:, 0:1], b['sc_bf'][par][:, :],
            start=(c == 0), stop=(c == NCH - 1), skip_group_check=True),
            waits=[t_cast] + (t_first if c == 0 else []), inc='pe')
        pe_s2[par] = s.emit('tensor', lambda e, c=c, par=par: e.matmul(
            b['pb'][1][0:1, :], b['ones_col'][:, 0:1], b['sc_sq'][par][:, :],
            start=(c == 0), stop=(c == NCH - 1), skip_group_check=True),
            waits=[t_sq], inc='pe')
        last2 = pe_s2[par]
    t_m = s.emit('vector', lambda e: e.tensor_scalar_mul(
        b['m_row'][:, :], b['pb'][0][0:1, :], 1.0 / D), waits=[pe_s1[1], last2], inc='dve')
    t_msq = s.emit('vector', lambda e: e.tensor_mul(
        b['a_row'][:, :], b['m_row'][:, :], b['m_row'][:, :]), waits=[t_m], inc='dve')
    t_var = s.emit('vector', lambda e: e.scalar_tensor_tensor(
        out=b['a_row'][:, :], in0=b['pb'][1][0:1, :], scalar=1.0 / D,
        in1=b['a_row'][:, :], op0=ALU.mult, op1=ALU.subtract), waits=[t_msq], inc='dve')
    t_ln = s.emit('scalar', lambda e: e.activation(
        out=b['rstd_row'][:, :], in_=b['a_row'][:, :], func=AF.Ln,
        bias=b['eps_row'][:, 0:1]), waits=[t_var], inc='act')
    t_rstd = s.emit('scalar', lambda e: e.activation(
        out=b['a_row'][:, :], in_=b['rstd_row'][:, :], func=AF.Exp, scale=-0.5),
        waits=[t_ln], inc='act')
    t_mrs = s.emit('vector', lambda e: e.tensor_mul(
        b['rstd_row'][:, :], b['m_row'][:, :], b['a_row'][:, :]),
        waits=[t_rstd], inc='dve')
    t_r1 = s.emit('tensor', lambda e: e.matmul(
        b['pb'][2][:, :], b['ones_row_f32'][0:1, :], b['a_row'][:, :],
        start=True, stop=True), waits=[t_rstd, t_mrs], inc='pe')
    t_r2 = s.emit('tensor', lambda e: e.matmul(
        b['pb'][3][:, :], b['ones_row_f32'][0:1, :], b['rstd_row'][:, :],
        start=True, stop=True), waits=[t_mrs], inc='pe')
    t_y = None
    for c in range(NCH):
        s.emit('vector', lambda e, c=c: e.tensor_mul(
            b['sc_f32'][:, :], b['x_sb'][:, col(c):col(c) + TT], b['pb'][2][:, :]),
            waits=[t_r1, t_r2] if c == 0 else [], inc='dve')
        t_y = s.emit('vector', lambda e, c=c: e.tensor_sub(
            b['y_sb'][:, c * TT:(c + 1) * TT], b['sc_f32'][:, :], b['pb'][3][:, :]),
            inc='dve')
    return t_y


def emit_wsweep(s, b, tile, w_dram, n_slab_pre, scale, extra_mm=None,
                first_waits=(), rhs='y'):
    xcol = lambda j: j * TOK + tile * TT
    if rhs == 'y':
        rhs_ap = lambda c: b['y_sb'][:, c * TT:(c + 1) * TT]
    else:
        rhs_ap = lambda c: b['ctx_sb'][:, c * TT:(c + 1) * TT]
    nbank = 4
    evac_tok = [None] * nbank
    slab_tok = [None, None]
    for j in range(2):
        slab_tok[j] = s.dma('gpsimd', b['wslab'][j][:, :], w_dram.ap()[j],
                            waits=[s.last('pe')])
    t_ev = None
    t_mm = None
    for j in range(NCH):
        par = j % 2
        bank = b['pb'][4 + (j % nbank)]
        n_extra = 0 if extra_mm is None else (1 if extra_mm == 'l0inc' else 2)
        for c in range(NCH):
            waits = []
            if c == 0:
                waits = [slab_tok[par], evac_tok[j % nbank]] + (list(first_waits) if j == 0 else [])
            is_last = (c == NCH - 1) and n_extra == 0
            t_mm = s.emit('tensor', lambda e, j=j, c=c, par=par, bank=bank, il=is_last: e.matmul(
                bank[:, :], b['wslab'][par][:, c * 128:(c + 1) * 128], rhs_ap(c),
                start=(c == 0), stop=il, skip_group_check=True),
                waits=waits, inc='pe' if (c == NCH - 1) else None)
        if extra_mm == 'l0inc':
            t_mm = s.emit('tensor', lambda e, j=j, bank=bank: e.matmul(
                bank[:, :], b['ab_sb'][:, j * 128:(j + 1) * 128], b['midw'][:, :],
                start=False, stop=True, skip_group_check=True), inc='pe')
        elif extra_mm == 'l1':
            t_mm = s.emit('tensor', lambda e, j=j, bank=bank: e.matmul(
                bank[:, :], b['g_sb'][0:40, j * 128:(j + 1) * 128], b['e_sb'][0:40, :],
                start=False, stop=True, skip_group_check=True), inc='pe')
        if j + 2 < NCH:
            slab_tok[par] = s.dma('gpsimd', b['wslab'][par][:, :], w_dram.ap()[j + 2],
                                  waits=[t_mm])
        if extra_mm == 'l1':
            s.emit('vector', lambda e, j=j, bank=bank: e.tensor_scalar_add(
                b['sc_f32'][:, :], bank[:, :], b['bb2'][:, j:j + 1]),
                waits=[t_mm], inc='dve')
            t_ev = s.emit('vector', lambda e, j=j, sc=scale: e.scalar_tensor_tensor(
                out=b['x_sb'][:, xcol(j):xcol(j) + TT], in0=b['sc_f32'][:, :], scalar=sc,
                in1=b['x_sb'][:, xcol(j):xcol(j) + TT], op0=ALU.mult, op1=ALU.add),
                inc='dve')
        else:
            t_ev = s.emit('vector', lambda e, j=j, bank=bank, sc=scale: e.scalar_tensor_tensor(
                out=b['x_sb'][:, xcol(j):xcol(j) + TT], in0=bank[:, :], scalar=sc,
                in1=b['x_sb'][:, xcol(j):xcol(j) + TT], op0=ALU.mult, op1=ALU.add),
                waits=[t_mm], inc='dve')
        evac_tok[j % nbank] = t_ev
    return t_ev, t_mm


def emit_moe_head(s, b, tile, t_y, la_tok=None, dbg=None):
    ycol = lambda c: c * TT
    t_log = None
    for sub in range(4):
        for c in range(NCH):
            t_log = s.emit('tensor', lambda e, sub=sub, c=c: e.matmul(
                b['pb'][0][:, sub * 8:sub * 8 + 8],
                b['y_sb'][:, ycol(c) + sub * 128: ycol(c) + sub * 128 + 128],
                b['rt_sb'][:, c * 8:(c + 1) * 8],
                start=(c == 0), stop=(c == NCH - 1), skip_group_check=True),
                waits=[t_y] if (sub == 0 and c == 0) else [],
                inc='pe' if c == NCH - 1 else None)
    g = lambda t: t.rearrange("p (g e) -> p g e", e=8)
    bc = lambda t: t[:, :, None].broadcast_to([128, 4, 8])
    if dbg is not None:
        s.emit('vector', lambda e: e.tensor_copy(b['log_sb'][:, :], b['pb'][0][:, 0:32]),
               waits=[t_log], inc='dve')
    tp = s.emit('vector', lambda e: e.reduce_max(
        b['w4a'][:, :, None], g(b['pb'][0][:, 0:32]), axis=AX.X), waits=[t_log], inc='dve')
    t1 = s.emit('vector', lambda e: e.tensor_tensor(
        out=g(b['w32a']), in0=g(b['pb'][0][:, 0:32]), in1=bc(b['w4a']),
        op=ALU.subtract), waits=[tp], inc='dve')
    t2 = s.emit('scalar', lambda e: e.activation(
        out=b['w32p'][:, :], in_=b['w32a'][:, :], func=AF.Exp), waits=[t1], inc='act')
    tp = s.emit('vector', lambda e: e.reduce_max(
        b['w4a'][:, :, None], g(b['w32p']), axis=AX.X), waits=[t2], inc='dve')
    tp = s.emit('vector', lambda e: e.tensor_tensor(
        out=g(b['w32a']), in0=g(b['w32p']), in1=bc(b['w4a']), op=ALU.is_equal),
        waits=[tp], inc='dve')
    tp = s.emit('vector', lambda e: e.scalar_tensor_tensor(
        out=g(b['w32b']), in0=g(b['w32a']), scalar=-2.0, in1=g(b['w32p']),
        op0=ALU.mult, op1=ALU.add), waits=[tp], inc='dve')
    t6 = s.emit('vector', lambda e: e.reduce_max(
        b['w4b'][:, :, None], g(b['w32b']), axis=AX.X), waits=[tp], inc='dve')
    t8 = s.emit('vector', lambda e: e.tensor_add(
        b['w4a'][:, :], b['w4a'][:, :], b['w4b'][:, :]), waits=[t6], inc='dve')
    tp = s.emit('vector', lambda e: e.tensor_tensor(
        out=g(b['w32b']), in0=g(b['w32p']), in1=bc(b['w4b']), op=ALU.is_ge),
        waits=[t8], inc='dve')
    t9 = s.emit('vector', lambda e: e.reciprocal(
        b['w4b'][:, :], b['w4a'][:, :]), waits=[tp], inc='dve')
    tp = s.emit('vector', lambda e: e.tensor_tensor(
        out=g(b['w32a']), in0=g(b['w32p']), in1=g(b['w32b']), op=ALU.mult),
        waits=[t9], inc='dve')
    t11 = s.emit('vector', lambda e: e.tensor_tensor(
        out=g(b['w32a']), in0=g(b['w32a']), in1=bc(b['w4b']), op=ALU.mult),
        waits=[tp], inc='dve')
    tT = None
    for sub in range(4):
        tT = s.emit('tensor', lambda e, sub=sub: e.transpose(
            b['pb'][1][0:8, sub * 128:(sub + 1) * 128],
            b['w32a'][:, sub * 8:(sub + 1) * 8], b['ident'][:, :]),
            waits=[t11] if sub == 0 else [], inc='pe' if sub == 3 else None)
    tE = s.emit('vector', lambda e: e.tensor_copy(
        b['wfm'][:, :], b['pb'][1][0:8, 0:512]), waits=[tT], inc='dve')
    tR = s.emit('tensor', lambda e: e.matmul(
        b['pb'][2][:, :], b['e16'][:, :], b['wfm'][:, :], start=True, stop=True),
        waits=[tE], inc='pe')
    tRe = s.emit('vector', lambda e: e.tensor_copy(
        b['wrep'][:, :], b['pb'][2][:, :]), waits=[tR], inc='dve')
    tM = None
    for c in range(NCH):
        tM = s.emit('tensor', lambda e, c=c: e.matmul(
            b['pb'][3][:, :], b['ab_sb'][:, c * 128:(c + 1) * 128],
            b['y_sb'][:, c * TT:(c + 1) * TT],
            start=(c == 0), stop=(c == NCH - 1), skip_group_check=True),
            waits=[la_tok] if c == 0 else [], inc='pe' if c == NCH - 1 else None)
    tMw = s.emit('vector', lambda e: e.tensor_tensor(
        out=b['midw'][:, :], in0=b['pb'][3][:, :], in1=b['wrep'][:, :], op=ALU.mult),
        waits=[tM, tRe], inc='dve')
    return tMw


def emit_l1_head(s, b, t_y):
    tE = None
    for c in range(NCH):
        tE = s.emit('tensor', lambda e, c=c: e.matmul(
            b['pb'][0][0:40, :], b['enc_sb'][:, c * 40:(c + 1) * 40],
            b['y_sb'][:, c * TT:(c + 1) * TT],
            start=(c == 0), stop=(c == NCH - 1), skip_group_check=True),
            waits=[t_y] if c == 0 else [], inc='pe' if c == NCH - 1 else None)
    tEv = s.emit('vector', lambda e: e.tensor_copy(
        b['e_sb'][:, :], b['pb'][0][0:40, :]), waits=[tE], inc='dve')
    return tEv

def emit_gprep(s, b, dr):
    ws0 = b['wslab'][0]
    cf = lambda n: ws0[:, 0:NCH * n]
    mk = lambda n: ws0[:, 1024:1024 + NCH * n]
    wbuf = {'w0s': ws0[:, 2048:2048 + 1024], 'w1s': ws0[:, 3072:3072 + 256]}
    t_acc = None
    for which, n_r, w_buf in (('c0', 32, 'w0s'), ('c1', 8, 'w1s')):
        for e_i in range(3):
            cf_ap = cf(n_r).rearrange("p (c r) -> p c r", r=n_r)
            mk_ap = mk(n_r).rearrange("p (c r) -> p c r", r=n_r)
            t_cf = s.dma('gpsimd', cf_ap,
                         dr[which].ap()[e_i].rearrange("c p r -> p c r"),
                         waits=[t_acc])
            t_mk = s.dma('gpsimd', mk_ap,
                         dr['m' + which].ap()[e_i].rearrange("c p r -> p c r"))
            s.emit('vector', lambda e, n_r=n_r: e.tensor_tensor(
                out=cf(n_r), in0=cf(n_r), in1=mk(n_r), op=ALU.mult),
                waits=[t_cf, t_mk], inc='dve')
            if e_i == 0:
                t_acc = s.emit('vector', lambda e, n_r=n_r, w_buf=w_buf: e.tensor_copy(
                    wbuf[w_buf], cf(n_r)), inc='dve')
            else:
                t_acc = s.emit('vector', lambda e, n_r=n_r, w_buf=w_buf: e.tensor_tensor(
                    out=wbuf[w_buf], in0=wbuf[w_buf], in1=cf(n_r), op=ALU.add), inc='dve')
    r3 = lambda t, r, w: t.rearrange("p (c r) -> p c r", r=w)
    s.emit('vector', lambda e: e.tensor_scalar_mul(
        b['wcomb'].rearrange("p (c r) -> p c r", r=40)[:, :, 0:32],
        wbuf['w0s'].rearrange("p (c r) -> p c r", r=32), 0.5), waits=[t_acc], inc='dve')
    t_wc = s.emit('vector', lambda e: e.tensor_scalar_mul(
        b['wcomb'].rearrange("p (c r) -> p c r", r=40)[:, :, 32:40],
        wbuf['w1s'].rearrange("p (c r) -> p c r", r=8), 0.5), inc='dve')
    slab_tok = [None, None]
    for c in range(2):
        slab_tok[c] = s.dma('sync', b['wslab'][c][:, :], dr['mt'].ap()[c],
                            waits=[t_wc] if c == 0 else [s.last('pe')])
    t_mm = None
    for c in range(NCH):
        par = c % 2
        for tt_i in range(8):
            t_mm = s.emit('tensor', lambda e, c=c, par=par, tt_i=tt_i: e.matmul(
                b['pb'][tt_i][0:40, :], b['wcomb'][:, c * 40:(c + 1) * 40],
                b['wslab'][par][:, tt_i * 512:(tt_i + 1) * 512],
                start=(c == 0), stop=(c == NCH - 1), skip_group_check=True),
                waits=([slab_tok[par], t_wc] if tt_i == 0 else []),
                inc='pe' if tt_i == 7 else None)
        if c + 2 < NCH:
            slab_tok[par] = s.dma('sync', b['wslab'][par][:, :], dr['mt'].ap()[c + 2],
                                  waits=[t_mm])
    t_g = None
    for tt_i in range(8):
        t_g = s.emit('vector', lambda e, tt_i=tt_i: e.tensor_copy(
            b['g_sb'][:, tt_i * 512:(tt_i + 1) * 512], b['pb'][tt_i][0:40, :]),
            waits=[t_mm], inc='dve')
    return t_g, t_mm


def _alloc_common(nc, st, b, pb_n=8, ln=True, ctx=True):
    al = (lambda n, sh, dt: st.enter_context(nc.sbuf_tensor(n, sh, dt))) if ctx \
        else (lambda n, sh, dt: nc.alloc_sbuf_tensor(n, sh, dt))
    b['x_sb'] = al("x_sb", [128, NCH * TOK], F32)
    b['ones_col'] = al("ones_col", [128, 1], BF16)
    b['pb'] = [st.enter_context(nc.psum_tensor(f"pb{i}", [128, TT], F32)) for i in range(pb_n)]
    if ln:
        b['sc_bf'] = [al(f"scbf{i}", [128, TT], BF16) for i in range(2)]
        b['sc_sq'] = [al(f"scsq{i}", [128, TT], BF16) for i in range(2)]
        b['sc_f32'] = al("scf32", [128, TT], F32)
        for r in ('m_row', 'a_row', 'rstd_row'):
            b[r] = al(r, [1, TT], F32)
        b['eps_row'] = al("eps_row", [1, 1], F32)
        b['ones_row_f32'] = al("ones_row_f32", [1, 128], F32)

def _emit_const_init(s, b):
    s.emit('vector', lambda e: e.memset(b['ones_col'][:, :], 1.0), inc='dve')
    if 'ones_row_f32' in b:
        s.emit('vector', lambda e: e.memset(b['ones_row_f32'][:, :], 1.0), inc='dve')
    if 'eps_row' in b:
        s.emit('vector', lambda e: e.memset(b['eps_row'][:, :], EPS), inc='dve')

def _emit_x_load(s, b, x_dram):
    toks = []
    for q in range(4):
        toks.append(s.dma('sync',
                          b['x_sb'].ap().rearrange("p (c t) -> p c t", t=TOK)[:, q * 8:(q + 1) * 8],
                          x_dram.ap().rearrange("(c p) t -> p c t", p=128)[:, q * 8:(q + 1) * 8]))
    return toks

def _emit_x_store(s, b, xo_dram, waits):
    toks = []
    for q in range(4):
        toks.append(s.dma('sync',
                          xo_dram.ap().rearrange("(c p) t -> p c t", p=128)[:, q * 8:(q + 1) * 8],
                          b['x_sb'].ap().rearrange("p (c t) -> p c t", t=TOK)[:, q * 8:(q + 1) * 8],
                          waits=waits))
    return toks


def build_ffn_stage(layer):
    stack = contextlib.ExitStack()
    with stack:
        nc = bass.Bass("TRN2", target_bir_lowering=False, debug=False)
        b = {}
        x_in = nc.dram_tensor("x_in", [D, TOK], F32, kind="ExternalInput")
        x_out = nc.dram_tensor("x_out", [D, TOK], F32, kind="ExternalOutput")
        b['bw_dram'] = nc.dram_tensor("bw", [NCH, 128, NCH * 128], F32, kind="ExternalInput")
        if layer == 0:
            rt = nc.dram_tensor("rt", [128, NCH * 8], F32, kind="ExternalInput")
            la = nc.dram_tensor("la", [128, NCH * 128], F32, kind="ExternalInput")
            lb = nc.dram_tensor("lb", [128, NCH * 128], F32, kind="ExternalInput")
            e16 = nc.dram_tensor("e16", [8, 128], F32, kind="ExternalInput")
            ident = nc.dram_tensor("ident", [128, 128], F32, kind="ExternalInput")
        else:
            enc = nc.dram_tensor("enc", [128, NCH * 40], F32, kind="ExternalInput")
            bbias = nc.dram_tensor("bbias", [128, NCH], F32, kind="ExternalInput")
            b['c0_dram'] = nc.dram_tensor("c0", [3, NCH, 128, 32], F32, kind="ExternalInput")
            b['c1_dram'] = nc.dram_tensor("c1", [3, NCH, 128, 8], F32, kind="ExternalInput")
            b['mc0_dram'] = nc.dram_tensor("mc0", [3, NCH, 128, 32], U8, kind="ExternalInput")
            b['mc1_dram'] = nc.dram_tensor("mc1", [3, NCH, 128, 8], U8, kind="ExternalInput")
            b['mt_dram'] = nc.dram_tensor("mt", [NCH, 128, D], BF16, kind="ExternalInput")

        _alloc_common(nc, stack, b)
        b['y_sb'] = stack.enter_context(nc.sbuf_tensor("y_sb", [128, NCH * TT], BF16))
        b['wslab'] = [stack.enter_context(nc.sbuf_tensor(f"wslab{i}", [128, NCH * 128], BF16))
                      for i in range(2)]
        if layer == 0:
            b['rt_sb'] = stack.enter_context(nc.sbuf_tensor("rt_sb", [128, NCH * 8], BF16))
            b['ab_sb'] = stack.enter_context(nc.sbuf_tensor("ab_sb", [128, NCH * 128], BF16))
            b['e16'] = stack.enter_context(nc.sbuf_tensor("e16_sb", [8, 128], F32))
            b['ident'] = stack.enter_context(nc.sbuf_tensor("ident_sb", [128, 128], F32))
            for n, sh, dt in (('w32a', [128, 32], F32), ('w32b', [128, 32], F32),
                              ('w32p', [128, 32], F32), ('w4a', [128, 4], F32),
                              ('w4b', [128, 4], F32), ('wfm', [8, TT], F32),
                              ('wrep', [128, TT], F32), ('midw', [128, TT], BF16)):
                b[n] = stack.enter_context(nc.sbuf_tensor(n, sh, dt))
        else:
            b['enc_sb'] = stack.enter_context(nc.sbuf_tensor("enc_sb", [128, NCH * 40], BF16))
            b['bb2'] = stack.enter_context(nc.sbuf_tensor("bb2", [128, NCH], F32))
            b['e_sb'] = stack.enter_context(nc.sbuf_tensor("e_sb", [40, TT], BF16))
            b['g_sb'] = stack.enter_context(nc.sbuf_tensor("g_sb", [40, D], BF16))
            b['wcomb'] = stack.enter_context(nc.sbuf_tensor("wcomb", [128, NCH * 40], BF16))

        s = Sched(nc, stack)
        _emit_const_init(s, b)
        x_toks = _emit_x_load(s, b, x_in)
        dr = dict(bw=b['bw_dram'])
        if layer == 0:
            dr.update(rt=rt, la=la, lb=lb, e16=e16, ident=ident)
        else:
            dr.update(enc=enc, bbias=bbias, c0=b['c0_dram'], c1=b['c1_dram'],
                      mc0=b['mc0_dram'], mc1=b['mc1_dram'], mt=b['mt_dram'])
        last_ev = emit_ffn_body(s, b, dr, layer, x_toks)
        st_toks = _emit_x_store(s, b, x_out, waits=[last_ev])
        s.wait_only('sync', st_toks)
        s.finalize()
    return nc


def emit_ffn_body(s, b, dr, layer, first_toks):
    if layer == 0:
        small = [s.dma('gpsimd', b['rt_sb'][:, :], dr['rt'].ap(), waits=[s.last('pe')]),
                 s.dma('sync', b['e16'][:, :], dr['e16'].ap()),
                 s.dma('sync', b['ident'][:, :], dr['ident'].ap())]
        t_g = None
    else:
        small = [s.dma('gpsimd', b['enc_sb'][:, 0:NCH * 40], dr['enc'].ap(),
                       waits=[s.last('pe')]),
                 s.dma('sync', b['bb2'][:, :], dr['bbias'].ap())]
        t_g, _ = emit_gprep(s, b, dr)

    last_ev = None
    for tile in range(2):
        first = list(first_toks) + small + ([t_g] if layer == 1 else []) + \
            ([last_ev] if last_ev else [])
        t_y = emit_ln(s, b, tile, extra_first_wait=first)
        if layer == 0:
            la_tok = s.dma('gpsimd', b['ab_sb'][:, :], dr['la'].ap(),
                           waits=[s.last('pe')])
            tMw = emit_moe_head(s, b, tile, t_y, la_tok)
            lb_tok = s.dma('gpsimd', b['ab_sb'][:, :], dr['lb'].ap(),
                           waits=[s.last('pe')])
            last_ev, _ = emit_wsweep(s, b, tile, dr['bw'], 2, 0.5,
                                     extra_mm='l0inc', first_waits=[tMw, lb_tok])
        else:
            tEv = emit_l1_head(s, b, t_y)
            last_ev, _ = emit_wsweep(s, b, tile, dr['bw'], 2, 0.5,
                                     extra_mm='l1', first_waits=[tEv])
    return last_ev


def emit_rope_head(s, b, tile, t_proj, bankA, bankB, ct, st_, out_sb, war_dve=None):
    t_cast = s.emit('vector', lambda e: e.tensor_copy(b['qb'][:, :], bankA[:, :]),
                    waits=[t_proj] + ([war_dve] if war_dve else []), inc='dve')
    t_rot = s.emit('tensor', lambda e: e.matmul(
        bankB[:, :], b['perm_sb'][:, :], b['qb'][:, :], start=True, stop=True),
        waits=[t_cast], inc='pe')
    s.emit('vector', lambda e: e.tensor_tensor(
        out=b['rtmp1'][:, :], in0=b['qb'][:, :],
        in1=ct[:, tile * TT:tile * TT + TT], op=ALU.mult), inc='dve')
    s.emit('vector', lambda e: e.tensor_tensor(
        out=out_sb[:, :], in0=bankB[:, :],
        in1=st_[:, tile * TT:tile * TT + TT], op=ALU.mult),
        waits=[t_rot], inc='dve')
    t_out = s.emit('vector', lambda e: e.tensor_add(
        out_sb[:, :], b['rtmp1'][:, :], out_sb[:, :]), inc='dve')
    return t_out


def build_qkv_stage(layer):
    stack = contextlib.ExitStack()
    with stack:
        nc = bass.Bass("TRN2", target_bir_lowering=False, debug=False)
        b = {}
        x_in = nc.dram_tensor("x_in", [D, TOK], F32, kind="ExternalInput")
        wq = nc.dram_tensor("wq", [NCH, 128, NCH * 128], F32, kind="ExternalInput")
        wk = nc.dram_tensor("wk", [8, 128, NCH * 128], F32, kind="ExternalInput")
        wv = nc.dram_tensor("wv", [NCH, 128, 1024], F32, kind="ExternalInput")
        cq_d = nc.dram_tensor("cq", [128, TOK], F32, kind="ExternalInput")
        sq_d = nc.dram_tensor("sq", [128, TOK], F32, kind="ExternalInput")
        ck_d = nc.dram_tensor("ck", [128, TOK], F32, kind="ExternalInput")
        sk_d = nc.dram_tensor("sk", [128, TOK], F32, kind="ExternalInput")
        perm_d = nc.dram_tensor("perm", [128, 128], BF16, kind="ExternalInput")
        q_out = nc.dram_tensor("q_out", [D, TOK], BF16, kind="ExternalOutput")
        k_out = nc.dram_tensor("k_out", [1024, TOK], BF16, kind="ExternalOutput")
        v_out = nc.dram_tensor("v_out", [TOK, 1024], BF16, kind="ExternalOutput")
        kn_out = nc.dram_tensor("kn_out", [8, TOK], F32, kind="ExternalOutput")
        qm_out = nc.dram_tensor("qm_out", [1, 64], F32, kind="ExternalOutput")

        _alloc_common(nc, stack, b)
        b['y_sb'] = stack.enter_context(nc.sbuf_tensor("y_sb", [128, NCH * TT], BF16))
        b['wslab'] = [stack.enter_context(nc.sbuf_tensor(f"wslab{i}", [128, NCH * 128], BF16))
                      for i in range(2)]
        b['wv_sb'] = [stack.enter_context(nc.sbuf_tensor(f"wvsb{i}", [128, 1024], BF16))
                      for i in range(2)]
        b['ct_sb'] = stack.enter_context(nc.sbuf_tensor("ct_sb", [128, TOK], BF16))
        b['st_sb'] = stack.enter_context(nc.sbuf_tensor("st_sb", [128, TOK], BF16))
        b['perm_sb'] = stack.enter_context(nc.sbuf_tensor("perm_sb", [128, 128], BF16))
        b['qb'] = stack.enter_context(nc.sbuf_tensor("qb", [128, TT], BF16))
        b['rtmp1'] = stack.enter_context(nc.sbuf_tensor("rtmp1", [128, TT], F32))
        b['qrope'] = stack.enter_context(nc.sbuf_tensor("qrope", [128, TT], BF16))
        b['qsq'] = stack.enter_context(nc.sbuf_tensor("qsq", [128, TT], BF16))
        b['kn_row'] = stack.enter_context(nc.sbuf_tensor("kn_row", [1, TT], F32))
        b['qm_sb'] = stack.enter_context(nc.sbuf_tensor("qm_sb", [1, 64], F32))
        b['vtmp'] = [b['qb'], b['qrope']]

        s = Sched(nc, stack)
        _emit_const_init(s, b)
        x_toks = _emit_x_load(s, b, x_in)
        tbl = [s.dma('sync', b['perm_sb'][:, :], perm_d.ap())]
        t_qm_last = None
        last_y = None
        for tile in range(2):
            t_y = emit_ln(s, b, tile, extra_first_wait=x_toks + tbl +
                          ([last_y] if last_y else []))
            last_y = t_y
            for phase, nh, wdram, ctd, std in (
                    ('q', NCH, wq, cq_d, sq_d),
                    ('k', 8, wk, ck_d, sk_d)):
                ct, st_ = b['ct_sb'], b['st_sb']
                t_tb = [s.dma('gpsimd', b['ct_sb'][:, :], ctd.ap(),
                              waits=[s.last('dve')]),
                        s.dma('gpsimd', b['st_sb'][:, :], std.ap())]
                slab_tok = [None, None]
                for h in range(2):
                    slab_tok[h] = s.dma('gpsimd',
                                        b['wslab'][h].ap().rearrange("p (c f) -> p c f", f=128),
                                        wdram.ap()[h].rearrange("c p f -> p c f"),
                                        waits=[s.last('pe')])
                prev_tn = None
                for h in range(nh):
                    par = h % 2
                    bankA = b['pb'][par]
                    bankB = b['pb'][2 + par]
                    t_proj = None
                    for c in range(NCH):
                        waits = []
                        if c == 0:
                            waits = [slab_tok[par], t_y, s.last('dve')] + \
                                (t_tb if h < 2 else [])
                        t_proj = s.emit('tensor', lambda e, c=c, par=par, bankA=bankA: e.matmul(
                            bankA[:, :], b['wslab'][par][:, c * 128:(c + 1) * 128],
                            b['y_sb'][:, c * TT:(c + 1) * TT],
                            start=(c == 0), stop=(c == NCH - 1), skip_group_check=True),
                            waits=waits, inc='pe' if c == NCH - 1 else None)
                    if h + 2 < nh:
                        slab_tok[par] = s.dma('gpsimd',
                                              b['wslab'][par].ap().rearrange("p (c f) -> p c f", f=128),
                                              wdram.ap()[h + 2].rearrange("c p f -> p c f"),
                                              waits=[t_proj])
                    t_rope = emit_rope_head(s, b, tile, t_proj, bankA, bankB, ct, st_,
                                            b['qrope'], war_dve=s.last('dmah'))
                    if phase == 'q':
                        dst = q_out.ap()[h * 128:(h + 1) * 128, tile * TT:tile * TT + TT]
                    else:
                        dst = k_out.ap()[h * 128:(h + 1) * 128, tile * TT:tile * TT + TT]
                    s.dma('sync', dst, b['qrope'][:, :], waits=[t_rope])
                    t_sq = s.emit('scalar', lambda e: e.activation(
                        out=b['qsq'][:, :], in_=b['qrope'][:, :], func=AF.Square),
                        waits=[t_rope, prev_tn], inc='act')
                    t_n = s.emit('tensor', lambda e: e.matmul(
                        b['pb'][4][0:1, :], b['ones_col'][:, 0:1], b['qsq'][:, :],
                        start=True, stop=True, skip_group_check=True),
                        waits=[t_sq, s.last('dve')], inc='pe')
                    prev_tn = t_n
                    if phase == 'q':
                        slot = tile * 32 + h
                        t_qm_last = s.emit('vector', lambda e, slot=slot: e.reduce_max(
                            b['qm_sb'][:, slot:slot + 1], b['pb'][4][0:1, :], axis=AX.X),
                            waits=[t_n], inc='dve')
                    else:
                        t_kn = s.emit('vector', lambda e: e.tensor_copy(
                            b['kn_row'][:, :], b['pb'][4][0:1, :]),
                            waits=[t_n, s.last('dmah')], inc='dve')
                        s.dma('sync', kn_out.ap()[h:h + 1, tile * TT:tile * TT + TT],
                              b['kn_row'][:, :], waits=[t_kn])
            wv_tok = [None, None]
            for c in range(2):
                wv_tok[c] = s.dma('gpsimd', b['wv_sb'][c][:, :], wv.ap()[c],
                                  waits=[s.last('pe')])
            t_vmm = None
            for c in range(NCH):
                par = c % 2
                for sub in range(4):
                    for dv in range(2):
                        bank = b['pb'][sub * 2 + dv]
                        waits = []
                        if sub == 0 and dv == 0:
                            waits = [wv_tok[par]]
                            if c == 0:
                                waits.append(s.last('dve'))
                        t_vmm = s.emit('tensor', lambda e, c=c, par=par, sub=sub, dv=dv, bank=bank: e.matmul(
                            bank[:, :], b['y_sb'][:, c * TT + sub * 128:c * TT + sub * 128 + 128],
                            b['wv_sb'][par][:, dv * TT:(dv + 1) * TT],
                            start=(c == 0), stop=(c == NCH - 1), skip_group_check=True),
                            waits=waits,
                            inc='pe' if (sub == 3 and dv == 1) else None)
                if c + 2 < NCH:
                    wv_tok[par] = s.dma('gpsimd', b['wv_sb'][par][:, :], wv.ap()[c + 2],
                                        waits=[t_vmm])
            for sub in range(4):
                for dv in range(2):
                    bank = b['pb'][sub * 2 + dv]
                    vp = (sub * 2 + dv) % 2
                    t_ev = s.emit('vector', lambda e, bank=bank, vp=vp: e.tensor_copy(
                        b['vtmp'][vp][:, :], bank[:, :]),
                        waits=[t_vmm, s.last('dmah')], inc='dve')
                    s.dma('sync',
                          v_out.ap()[tile * TT + sub * 128: tile * TT + sub * 128 + 128,
                                     dv * TT:(dv + 1) * TT],
                          b['vtmp'][vp][:, :], waits=[t_ev])
        s.dma('sync', qm_out.ap()[:, :], b['qm_sb'][:, :], waits=[t_qm_last])
        s.wait_only('sync', [s.last('dmah'), s.last('dmas')])
        s.finalize()
    return nc


def build_att_stage(layer):
    stack = contextlib.ExitStack()
    with stack:
        nc = bass.Bass("TRN2", target_bir_lowering=False, debug=False)
        b = {}
        x_in = nc.dram_tensor("x_in", [D, TOK], F32, kind="ExternalInput")
        q_in = nc.dram_tensor("q_in", [D, TOK], BF16, kind="ExternalInput")
        kg = nc.dram_tensor("kg", [8, 128, 4096], BF16, kind="ExternalInput")
        vg = nc.dram_tensor("vg", [8, 128, 4096], BF16, kind="ExternalInput")
        kng = nc.dram_tensor("kng", [8, 4096], F32, kind="ExternalInput")
        qm_d = nc.dram_tensor("qm", [1, 64], F32, kind="ExternalInput")
        wo = nc.dram_tensor("wo", [NCH, 128, NCH * 128], F32, kind="ExternalInput")
        qpos_d = nc.dram_tensor("qpos", [1, TOK], F32, kind="ExternalInput")
        kpos_d = nc.dram_tensor("kpos", [128, 32], F32, kind="ExternalInput")
        ident_d = nc.dram_tensor("ident", [128, 128], F32, kind="ExternalInput")
        negrow_d = nc.dram_tensor("negrow", [1, 128], BF16, kind="ExternalInput")
        posrow_d = nc.dram_tensor("posrow", [1, 128], BF16, kind="ExternalInput")
        x_out = nc.dram_tensor("x_out", [D, TOK], F32, kind="ExternalOutput")

        _alloc_common(nc, stack, b, ln=False)
        b['y_sb'] = stack.enter_context(nc.sbuf_tensor("ctx_y", [128, NCH * TT], BF16))
        b['ctx_sb'] = b['y_sb']
        b['wslab'] = [stack.enter_context(nc.sbuf_tensor(f"wslab{i}", [128, NCH * 128], BF16))
                      for i in range(2)]
        b['ksl'] = b['wslab'][0]
        b['vsl'] = b['wslab'][1]
        b['q_sb'] = stack.enter_context(nc.sbuf_tensor("q_sb", [128, TT], BF16))
        b['exp_sb'] = [stack.enter_context(nc.sbuf_tensor(f"exp{i}", [128, TT], BF16))
                       for i in range(2)]
        b['qpos'] = stack.enter_context(nc.sbuf_tensor("qpos_sb", [1, TOK], F32))
        b['kpos'] = stack.enter_context(nc.sbuf_tensor("kpos_sb", [128, 32], F32))
        b['qpos_rep'] = stack.enter_context(nc.sbuf_tensor("qpos_rep", [128, TT], F32))
        b['ones_row_f32'] = stack.enter_context(nc.sbuf_tensor("ones_row_f32", [1, 128], F32))
        b['kn_sb'] = stack.enter_context(nc.sbuf_tensor("kn_sb", [8, TT], F32))
        b['mkacc'] = stack.enter_context(nc.sbuf_tensor("mkacc", [8, 8], F32))
        b['mk8'] = stack.enter_context(nc.sbuf_tensor("mk8", [8, 1], F32))
        b['mk2'] = stack.enter_context(nc.sbuf_tensor("mk2", [1, 8], F32))
        b['qm_sb'] = stack.enter_context(nc.sbuf_tensor("qm_sb", [1, 64], F32))
        b['negB'] = stack.enter_context(nc.sbuf_tensor("negB", [128, 1], F32))
        b['b2row'] = stack.enter_context(nc.sbuf_tensor("b2row", [1, 1], F32))
        b['b1row'] = stack.enter_context(nc.sbuf_tensor("b1row", [1, 1], F32))
        b['b1bf'] = stack.enter_context(nc.sbuf_tensor("b1bf", [1, 1], BF16))
        b['rcp'] = stack.enter_context(nc.sbuf_tensor("rcp", [1, TT], F32))
        b['rcpb'] = stack.enter_context(nc.sbuf_tensor("rcpb", [1, TT], BF16))
        b['rrep'] = stack.enter_context(nc.sbuf_tensor("rrep", [128, TT], F32))
        b['ident'] = stack.enter_context(nc.sbuf_tensor("ident_sb", [128, 128], F32))
        b['negrow'] = stack.enter_context(nc.sbuf_tensor("negrow_sb", [1, 128], BF16))
        b['posrow'] = stack.enter_context(nc.sbuf_tensor("posrow_sb", [1, 128], BF16))

        s = Sched(nc, stack)
        _emit_const_init(s, b)
        x_toks = _emit_x_load(s, b, x_in)
        pre = [s.dma('sync', b['qpos'][:, :], qpos_d.ap()),
               s.dma('sync', b['kpos'][:, :], kpos_d.ap()),
               s.dma('sync', b['qm_sb'][:, :], qm_d.ap()),
               s.dma('sync', b['ident'][:, :], ident_d.ap()),
               s.dma('sync', b['negrow'][:, :], negrow_d.ap()),
               s.dma('sync', b['posrow'][:, :], posrow_d.ap())]
        t_mk = None
        for i in range(8):
            t_l = s.dma('sync', b['kn_sb'][:, :],
                        kng.ap()[:, i * 512:(i + 1) * 512], waits=[t_mk])
            t_mk = s.emit('vector', lambda e, i=i: e.reduce_max(
                b['mkacc'][:, i:i + 1], b['kn_sb'][:, :], axis=AX.X),
                waits=[t_l], inc='dve')
        t_mk = s.emit('vector', lambda e: e.reduce_max(
            b['mk8'][:, :], b['mkacc'][:, :], axis=AX.X), waits=[t_mk], inc='dve')
        t_tr = s.emit('tensor', lambda e: e.transpose(
            b['pb'][6][0:1, 0:8], b['mk8'][:, 0:1], b['ident'][0:8, 0:8]),
            waits=[t_mk, pre[3]], inc='pe')
        t_mk2 = s.emit('vector', lambda e: e.tensor_copy(
            b['mk2'][:, :], b['pb'][6][0:1, 0:8]), waits=[t_tr], inc='dve')

        last_store_gate = None
        for tile in range(2):
            nkt = 16 if tile == 0 else 32
            t_qr = s.emit('tensor', lambda e, tile=tile: e.matmul(
                b['pb'][6][:, :], b['ones_row_f32'][0:1, :],
                b['qpos'][0:1, tile * TT:tile * TT + TT],
                start=True, stop=True, skip_group_check=True),
                waits=[pre[0], s.last('dve')], inc='pe')
            t_qrep = s.emit('vector', lambda e: e.tensor_copy(
                b['qpos_rep'][:, :], b['pb'][6][:, :]), waits=[t_qr], inc='dve')
            t_ctx_all = None
            t_kv_prev = None
            for kvh in range(8):
                t_k = s.dma('sync', b['ksl'][:, 0:nkt * 128],
                            kg.ap()[kvh][:, 0:nkt * 128], waits=[s.last('pe')])
                t_v = s.dma('sync', b['vsl'][:, 0:nkt * 128],
                            vg.ap()[kvh][:, 0:nkt * 128])
                for qh in range(4):
                    h = kvh * 4 + qh
                    t_q = s.dma('sync', b['q_sb'][:, :],
                                q_in.ap()[h * 128:(h + 1) * 128, tile * TT:tile * TT + TT],
                                waits=[s.last('pe')])
                    t_b2 = s.emit('vector', lambda e, tile=tile, h=h, kvh=kvh: e.tensor_tensor(
                        out=b['b2row'][:, :], in0=b['qm_sb'][:, tile * 32 + h:tile * 32 + h + 1],
                        in1=b['mk2'][:, kvh:kvh + 1], op=ALU.mult),
                        waits=[t_mk2, pre[2], s.last('act')], inc='dve')
                    t_b1 = s.emit('scalar', lambda e: e.activation(
                        out=b['b1row'][:, :], in_=b['b2row'][:, :], func=AF.Sqrt),
                        waits=[t_b2], inc='act')
                    t_bb = s.emit('vector', lambda e: e.tensor_copy(
                        b['b1bf'][:, :], b['b1row'][:, :]),
                        waits=[t_b1, s.last('pe')], inc='dve')
                    t_bn = s.emit('tensor', lambda e: e.matmul(
                        b['pb'][4][:, 0:1], b['negrow'][0:1, :], b['b1bf'][:, :],
                        start=True, stop=True, skip_group_check=True),
                        waits=[t_bb, pre[4]], inc='pe')
                    t_nb = s.emit('vector', lambda e: e.tensor_copy(
                        b['negB'][:, :], b['pb'][4][:, 0:1]),
                        waits=[t_bn, s.last('act')], inc='dve')
                    t_exp = [None, None]
                    t_msk = [None, None]
                    t_den = [None, None]
                    for kt in range(nkt):
                        par = kt % 2
                        t_sT = s.emit('tensor', lambda e, kt=kt, par=par: e.matmul(
                            b['pb'][par][:, :], b['ksl'][:, kt * 128:(kt + 1) * 128],
                            b['q_sb'][:, :], start=True, stop=True, skip_group_check=True),
                            waits=[t_exp[par], t_q] + ([t_k, t_v, s.last('dve')] if kt == 0 else []),
                            inc='pe')
                        t_exp[par] = s.emit('scalar', lambda e, par=par: e.activation(
                            out=b['exp_sb'][par][:, :], in_=b['pb'][par][:, :], func=AF.Exp,
                            bias=b['negB'][:, 0:1]),
                            waits=[t_sT, t_nb, t_den[par], t_msk[par]], inc='act')
                        t_msk[par] = s.emit('vector', lambda e, par=par, kt=kt: e.scalar_tensor_tensor(
                            out=b['exp_sb'][par][:, :], in0=b['qpos_rep'][:, :],
                            scalar=b['kpos'][:, kt:kt + 1], in1=b['exp_sb'][par][:, :],
                            op0=ALU.is_ge, op1=ALU.mult),
                            waits=[t_exp[par], t_qrep, pre[1]], inc='dve')
                        s.emit('tensor', lambda e, kt=kt, par=par, nkt=nkt: e.matmul(
                            b['pb'][2][:, :], b['vsl'][:, kt * 128:(kt + 1) * 128],
                            b['exp_sb'][par][:, :], start=(kt == 0), stop=(kt == nkt - 1),
                            skip_group_check=True),
                            waits=[t_msk[par]], inc=None)
                        t_den[par] = s.emit('tensor', lambda e, kt=kt, par=par, nkt=nkt: e.matmul(
                            b['pb'][3][0:1, :], b['ones_col'][:, 0:1],
                            b['exp_sb'][par][:, :], start=(kt == 0), stop=(kt == nkt - 1),
                            skip_group_check=True), inc='pe')
                    t_den_last = t_den[(nkt - 1) % 2]
                    t_rc = s.emit('vector', lambda e: e.reciprocal(
                        b['rcp'][:, :], b['pb'][3][0:1, :]), waits=[t_den_last], inc='dve')
                    t_rcb = s.emit('vector', lambda e: e.tensor_copy(
                        b['rcpb'][:, :], b['rcp'][:, :]), waits=[t_rc], inc='dve')
                    t_rr = s.emit('tensor', lambda e: e.matmul(
                        b['pb'][5][:, :], b['posrow'][0:1, :], b['rcpb'][:, :],
                        start=True, stop=True, skip_group_check=True),
                        waits=[t_rcb, pre[5]], inc='pe')
                    t_re = s.emit('vector', lambda e: e.tensor_copy(
                        b['rrep'][:, :], b['pb'][5][:, :]), waits=[t_rr], inc='dve')
                    t_ctx_all = s.emit('vector', lambda e, h=h: e.tensor_tensor(
                        out=b['ctx_sb'][:, h * TT:(h + 1) * TT], in0=b['pb'][2][:, :],
                        in1=b['rrep'][:, :], op=ALU.mult),
                        waits=[t_re], inc='dve')
            last_ev, _ = emit_wsweep(s, b, tile, wo, 2, 1.0, extra_mm=None,
                                     first_waits=[t_ctx_all], rhs='y')
            last_store_gate = last_ev
        st_toks = _emit_x_store(s, b, x_out, waits=[last_store_gate])
        s.wait_only('sync', st_toks)
        s.finalize()
    return nc


def _prep_shared(inp):
    p = {}
    p['bw0'] = _tile_w(np.ascontiguousarray(inp['L0_ffn_base_W'].T))
    p['rt0'] = np.ascontiguousarray(np.asarray(inp['L0_ffn_router_W']).T.reshape(NCH, 128, 8).transpose(1, 0, 2)).reshape(128, NCH * 8)
    p['la0'] = np.ascontiguousarray(
        np.asarray(inp['L0_ffn_lora_A']).transpose(1, 0, 2).reshape(D, 128)
        .reshape(NCH, 128, 128).transpose(1, 0, 2)).reshape(128, NCH * 128)
    p['lb0'] = np.ascontiguousarray(np.asarray(inp['L0_ffn_lora_B']).reshape(128, D))
    p['wq0'] = _tile_w(inp['L0_attn_Wq'])
    p['wk0'] = _tile_w(inp['L0_attn_Wk'])
    p['wv0'] = np.ascontiguousarray(inp['L0_attn_Wv']).reshape(NCH, 128, 1024)
    p['wo0'] = _tile_w(inp['L0_attn_Wo'])
    p['bw1'] = _tile_w(np.ascontiguousarray(inp['L1_ffn_W_base'].T))
    p['enc1'] = np.ascontiguousarray(
        np.concatenate([np.asarray(inp['L1_ffn_enc0']).T,
                        np.asarray(inp['L1_ffn_enc1']).T], axis=1)
        .reshape(NCH, 128, 40).transpose(1, 0, 2)).reshape(128, NCH * 40)
    p['bb1'] = np.ascontiguousarray(np.asarray(inp['L1_ffn_b_base']).reshape(NCH, 128).T)
    p['c0'] = np.ascontiguousarray(inp['L1_ffn_coefs0']).reshape(3, NCH, 128, 32)
    p['c1'] = np.ascontiguousarray(inp['L1_ffn_coefs1']).reshape(3, NCH, 128, 8)
    p['mc0'] = np.ascontiguousarray(inp['L1_ffn_mask0']).view(np.uint8).reshape(3, NCH, 128, 32)
    p['mc1'] = np.ascontiguousarray(inp['L1_ffn_mask1']).view(np.uint8).reshape(3, NCH, 128, 8)
    p['wq1'] = _tile_w(inp['L1_attn_Wq'])
    p['wk1'] = _tile_w(inp['L1_attn_Wk'])
    p['wv1'] = np.ascontiguousarray(inp['L1_attn_Wv']).reshape(NCH, 128, 1024)
    p['wo1'] = _tile_w(inp['L1_attn_Wo'])
    p['mt'] = _cached('mt', _mt_bf16).reshape(NCH, 128, D)
    p['e16'] = _cached('e16', _e16)
    p['ident'] = _cached('ident', lambda: np.eye(128, dtype=np.float32))
    p['perm'] = _cached('perm', _perm_rot)
    p['kpos'] = _cached('kpos', lambda: (
        np.arange(32, dtype=np.float32)[None, :] * 128 +
        np.arange(128, dtype=np.float32)[:, None]).astype(np.float32))
    p['negrow'] = _cached('negrow', lambda: np.full((1, 128), -1.02, dtype=np.float32).astype(BF))
    p['posrow'] = _cached('posrow', lambda: np.ones((1, 128), dtype=np.float32).astype(BF))
    return p


def _prep_cores():
    out = []
    scale = 1.0 / math.sqrt(128.0)
    for c in range(NCORES):
        pos = _core_positions(c)
        c0q, s0q = _cached(('r0', c), lambda pos=pos: _rope_l0(pos))
        c1q, s1q = _cached(('r1', c), lambda pos=pos: _rope_l1(pos))
        out.append(dict(
            cq0=np.ascontiguousarray(c0q * scale), sq0=np.ascontiguousarray(s0q * scale),
            ck0=c0q, sk0=s0q,
            cq1=np.ascontiguousarray(c1q * scale), sq1=np.ascontiguousarray(s1q * scale),
            ck1=c1q, sk1=s1q,
            qpos=np.ascontiguousarray(pos.astype(np.float32)[None, :])))
    return out


def _gather_kv(res, cores=range(NCORES)):
    Kg = [np.zeros((1024, 4096), dtype=BF) for _ in range(2)]
    Vg = [np.zeros((4096, 1024), dtype=BF) for _ in range(2)]
    KNg = [np.zeros((8, 4096), dtype=np.float32) for _ in range(2)]
    for c in cores:
        bt = c // 4
        m, hch = _core_chunks(c)
        k = res[c]['k_out']; v = res[c]['v_out']; kn = res[c]['kn_out']
        Kg[bt][:, 512 * m:512 * m + 512] = k[:, 0:512]
        Kg[bt][:, 512 * hch:512 * hch + 512] = k[:, 512:1024]
        Vg[bt][512 * m:512 * m + 512, :] = v[0:512, :]
        Vg[bt][512 * hch:512 * hch + 512, :] = v[512:1024, :]
        KNg[bt][:, 512 * m:512 * m + 512] = kn[:, 0:512]
        KNg[bt][:, 512 * hch:512 * hch + 512] = kn[:, 512:1024]
    kg_f = [np.ascontiguousarray(Kg[bt].reshape(8, 128, 4096)) for bt in range(2)]
    vg_f = [np.ascontiguousarray(
        Vg[bt].reshape(32, 128, 8, 128).transpose(2, 1, 0, 3).reshape(8, 128, 4096))
        for bt in range(2)]
    return kg_f, vg_f, KNg


_NC_CACHE = {}

def _get_nc(kind, layer):
    key = (kind, layer)
    if key not in _NC_CACHE:
        if kind == 'ffn':
            _NC_CACHE[key] = build_ffn_stage(layer)
        elif kind == 'qkv':
            _NC_CACHE[key] = build_qkv_stage(layer)
        else:
            _NC_CACHE[key] = build_att_stage(layer)
    return _NC_CACHE[key]


def _run(nc, in_maps):
    res = run_bass_kernel_spmd(nc, in_maps, core_ids=list(range(NCORES)))
    return res.results


def kernel_unfused(**inp):
    p = _prep_shared(inp)
    pc = _prep_cores()
    X = np.asarray(inp['L0_attn_X'])
    xs = [np.ascontiguousarray(X[c // 4, _core_positions(c), :].T.astype(np.float32))
          for c in range(NCORES)]

    def ffn_maps(layer, xs_cur):
        maps = []
        for c in range(NCORES):
            m = {'x_in': xs_cur[c], 'bw': p['bw0'] if layer == 0 else p['bw1']}
            if layer == 0:
                m.update(rt=p['rt0'], la=p['la0'], lb=p['lb0'], e16=p['e16'],
                         ident=p['ident'])
            else:
                m.update(enc=p['enc1'], bbias=p['bb1'], c0=p['c0'], c1=p['c1'],
                         mc0=p['mc0'], mc1=p['mc1'], mt=p['mt'])
            maps.append(m)
        return maps

    def qkv_maps(layer, xs_cur):
        maps = []
        for c in range(NCORES):
            t = pc[c]
            if layer == 0:
                m = dict(x_in=xs_cur[c], wq=p['wq0'], wk=p['wk0'], wv=p['wv0'],
                         cq=t['cq0'], sq=t['sq0'], ck=t['ck0'], sk=t['sk0'],
                         perm=p['perm'])
            else:
                m = dict(x_in=xs_cur[c], wq=p['wq1'], wk=p['wk1'], wv=p['wv1'],
                         cq=t['cq1'], sq=t['sq1'], ck=t['ck1'], sk=t['sk1'],
                         perm=p['perm'])
            maps.append(m)
        return maps

    def att_maps(layer, xs_cur, qk):
        kg_f, vg_f, KNg = qk['gather']
        maps = []
        for c in range(NCORES):
            bt = c // 4
            maps.append(dict(
                x_in=xs_cur[c], q_in=qk['res'][c]['q_out'], kg=kg_f[bt], vg=vg_f[bt],
                kng=KNg[bt], qm=qk['res'][c]['qm_out'],
                wo=p['wo0'] if layer == 0 else p['wo1'],
                qpos=pc[c]['qpos'], kpos=p['kpos'], ident=p['ident'],
                negrow=p['negrow'], posrow=p['posrow']))
        return maps

    cur = xs
    for layer in (0, 1):
        res = _run(_get_nc('ffn', layer), ffn_maps(layer, cur))
        cur = [res[c]['x_out'] for c in range(NCORES)]
        qres = _run(_get_nc('qkv', layer), qkv_maps(layer, cur))
        qk = {'res': qres, 'gather': _gather_kv(qres)}
        res = _run(_get_nc('att', layer), att_maps(layer, cur, qk))
        cur = [res[c]['x_out'] for c in range(NCORES)]
        res = _run(_get_nc('ffn', layer), ffn_maps(layer, cur))
        cur = [res[c]['x_out'] for c in range(NCORES)]

    out = np.empty((2, 4096, D), dtype=np.float32)
    for c in range(NCORES):
        out[c // 4, _core_positions(c), :] = cur[c].T
    return out


def _carve(nc, base, cursor, name, shape, dtype):
    nbytes = int(np.prod(shape[1:])) * (2 if dtype in (BF16,) else (1 if dtype == U8 else 4))
    off = (cursor + 31) & ~31
    h = nc.alloc_sbuf_tensor_at(name, shape, dtype, offset=base + off)
    return h, off + nbytes


def build_fused():
    stack = contextlib.ExitStack()
    with stack:
        nc = bass.Bass("TRN2", target_bir_lowering=False, debug=False)
        b = {}
        x_in = nc.dram_tensor("x_in", [D, TOK], F32, kind="ExternalInput")
        x_out = nc.dram_tensor("x_out", [D, TOK], F32, kind="ExternalOutput")
        dri = {}
        for l in (0, 1):
            dri[('bw', l)] = nc.dram_tensor(f"bw{l}", [NCH, 128, NCH * 128], F32, kind="ExternalInput")
            dri[('wq', l)] = nc.dram_tensor(f"wq{l}", [NCH, 128, NCH * 128], F32, kind="ExternalInput")
            dri[('wk', l)] = nc.dram_tensor(f"wk{l}", [8, 128, NCH * 128], F32, kind="ExternalInput")
            dri[('wv', l)] = nc.dram_tensor(f"wv{l}", [NCH, 128, 1024], F32, kind="ExternalInput")
            dri[('wo', l)] = nc.dram_tensor(f"wo{l}", [NCH, 128, NCH * 128], F32, kind="ExternalInput")
            for t in ('cq', 'sq', 'ck', 'sk'):
                dri[(t, l)] = nc.dram_tensor(f"{t}{l}", [128, TOK], F32, kind="ExternalInput")
        rt_d = nc.dram_tensor("rt", [128, NCH * 8], F32, kind="ExternalInput")
        la_d = nc.dram_tensor("la", [128, NCH * 128], F32, kind="ExternalInput")
        lb_d = nc.dram_tensor("lb", [128, NCH * 128], F32, kind="ExternalInput")
        enc_d = nc.dram_tensor("enc", [128, NCH * 40], F32, kind="ExternalInput")
        bbias_d = nc.dram_tensor("bbias", [128, NCH], F32, kind="ExternalInput")
        c0_d = nc.dram_tensor("c0", [3, NCH, 128, 32], F32, kind="ExternalInput")
        c1_d = nc.dram_tensor("c1", [3, NCH, 128, 8], F32, kind="ExternalInput")
        mc0_d = nc.dram_tensor("mc0", [3, NCH, 128, 32], U8, kind="ExternalInput")
        mc1_d = nc.dram_tensor("mc1", [3, NCH, 128, 8], U8, kind="ExternalInput")
        mt_d = nc.dram_tensor("mt", [NCH, 128, D], BF16, kind="ExternalInput")
        e16_d = nc.dram_tensor("e16", [8, 128], F32, kind="ExternalInput")
        ident_d = nc.dram_tensor("ident", [128, 128], F32, kind="ExternalInput")
        perm_d = nc.dram_tensor("perm", [128, 128], BF16, kind="ExternalInput")
        kpos_d = nc.dram_tensor("kpos", [128, 32], F32, kind="ExternalInput")
        qpos_d = nc.dram_tensor("qpos", [1, TOK], F32, kind="ExternalInput")
        negrow_d = nc.dram_tensor("negrow", [1, 128], BF16, kind="ExternalInput")
        posrow_d = nc.dram_tensor("posrow", [1, 128], BF16, kind="ExternalInput")
        q_scr = nc.dram_tensor("q_scr", [D, TOK], BF16, kind="Internal")
        kv_stg = nc.dram_tensor("kv_stg", [2056, TOK], BF16, kind="Internal")
        kv_gth = nc.dram_tensor("kv_gth", [4 * 2056, TOK], BF16, kind="Internal")

        _alloc_common(nc, stack, b, ctx=False)
        b['y_sb'] = nc.alloc_sbuf_tensor("y_sb", [128, NCH * TT], BF16)
        b['ctx_sb'] = b['y_sb']
        b['wslab'] = [nc.alloc_sbuf_tensor(f"wslab{i}", [128, NCH * 128], BF16)
                      for i in range(2)]
        b['ksl'] = b['wslab'][0]
        b['vsl'] = b['wslab'][1]
        b['rt_sb'] = nc.alloc_sbuf_tensor("rt_sb", [128, NCH * 8], BF16)
        b['e16'] = nc.alloc_sbuf_tensor("e16_sb", [8, 128], F32)
        b['ident'] = nc.alloc_sbuf_tensor("ident_sb", [128, 128], F32)
        b['perm_sb'] = nc.alloc_sbuf_tensor("perm_sb", [128, 128], BF16)
        b['bb2'] = nc.alloc_sbuf_tensor("bb2", [128, NCH], F32)
        b['kpos'] = nc.alloc_sbuf_tensor("kpos_sb", [128, 32], F32)
        b['qm_sb'] = nc.alloc_sbuf_tensor("qm_sb", [1, 64], F32)
        b['negrow'] = nc.alloc_sbuf_tensor("negrow_sb", [1, 128], BF16)
        b['posrow'] = nc.alloc_sbuf_tensor("posrow_sb", [1, 128], BF16)

        szR = 15744
        baseR = nc.bump_sbuf(szR)[0]
        cur = 0
        b['e_sb'], cur = _carve(nc, baseR, cur, "e_sb", [40, TT], BF16)
        b['wcomb'], cur = _carve(nc, baseR, cur, "wcomb", [128, NCH * 40], BF16)
        b['g_sb'], cur = _carve(nc, baseR, cur, "g_sb", [40, D], BF16)
        b['enc_sb'], cur = _carve(nc, baseR, cur, "enc_sb", [128, NCH * 40], BF16)
        assert cur <= szR, cur
        cur = 0
        b['w32a'], cur = _carve(nc, baseR, cur, "w32a", [128, 32], F32)
        b['w32b'], cur = _carve(nc, baseR, cur, "w32b", [128, 32], F32)
        b['w32p'], cur = _carve(nc, baseR, cur, "w32p", [128, 32], F32)
        b['w4a'], cur = _carve(nc, baseR, cur, "w4a", [128, 4], F32)
        b['w4b'], cur = _carve(nc, baseR, cur, "w4b", [128, 4], F32)
        b['wfm'], cur = _carve(nc, baseR, cur, "wfm", [8, TT], F32)
        b['wrep'], cur = _carve(nc, baseR, cur, "wrep", [128, TT], F32)
        b['midw'], cur = _carve(nc, baseR, cur, "midw", [128, TT], BF16)
        b['ab_sb'], cur = _carve(nc, baseR, cur, "ab_sb", [128, NCH * 128], BF16)
        assert cur <= szR, cur
        cur = 0
        b['ct_sb'], cur = _carve(nc, baseR, cur, "ct_sb", [128, TOK], BF16)
        b['st_sb'], cur = _carve(nc, baseR, cur, "st_sb", [128, TOK], BF16)
        b['qb'], cur = _carve(nc, baseR, cur, "qb", [128, TT], BF16)
        b['rtmp1'], cur = _carve(nc, baseR, cur, "rtmp1", [128, TT], F32)
        b['qrope'], cur = _carve(nc, baseR, cur, "qrope", [128, TT], BF16)
        b['qsq'], cur = _carve(nc, baseR, cur, "qsq", [128, TT], BF16)
        b['kn_row'], cur = _carve(nc, baseR, cur, "kn_row", [1, TT], F32)
        assert cur <= szR, cur
        b['wv_sb'] = [b['ct_sb'], b['st_sb']]
        b['vtmp'] = [b['qb'], b['qrope']]
        cur = 0
        b['q_sb2'] = [None, None]
        b['q_sb2'][0], cur = _carve(nc, baseR, cur, "q_sb0", [128, TT], BF16)
        b['q_sb2'][1], cur = _carve(nc, baseR, cur, "q_sb1", [128, TT], BF16)
        b['exp_sb'] = [None] * 4
        b['exp_sb'][0], cur = _carve(nc, baseR, cur, "exp0", [128, TT], BF16)
        b['exp_sb'][1], cur = _carve(nc, baseR, cur, "exp1", [128, TT], BF16)
        b['qpos_rep'], cur = _carve(nc, baseR, cur, "qpos_rep", [128, TT], F32)
        b['qpos'], cur = _carve(nc, baseR, cur, "qpos_sb", [1, TT], F32)
        knb_off = cur
        b['exp_sb'][2], cur = _carve(nc, baseR, cur, "exp2", [128, TT], BF16)
        b['exp_sb'][3], cur = _carve(nc, baseR, cur, "exp3", [128, TT], BF16)
        b['knb'] = nc.alloc_sbuf_tensor_at("knb", [8, TOK], BF16, offset=baseR + knb_off)
        b['mkacc'], cur = _carve(nc, baseR, cur, "mkacc", [8, 8], F32)
        b['mk8'], cur = _carve(nc, baseR, cur, "mk8", [8, 1], F32)
        b['mk2'], cur = _carve(nc, baseR, cur, "mk2", [1, 8], F32)
        b['negB2'] = [None, None]
        b['negB2'][0], cur = _carve(nc, baseR, cur, "negB0", [128, 1], F32)
        b['negB2'][1], cur = _carve(nc, baseR, cur, "negB1", [128, 1], F32)
        b['b2row2'] = [None, None]
        b['b2row2'][0], cur = _carve(nc, baseR, cur, "b2row0", [1, 1], F32)
        b['b2row2'][1], cur = _carve(nc, baseR, cur, "b2row1", [1, 1], F32)
        b['b1row2'] = [None, None]
        b['b1row2'][0], cur = _carve(nc, baseR, cur, "b1row0", [1, 1], F32)
        b['b1row2'][1], cur = _carve(nc, baseR, cur, "b1row1", [1, 1], F32)
        b['b1bf2'] = [None, None]
        b['b1bf2'][0], cur = _carve(nc, baseR, cur, "b1bf0", [1, 1], BF16)
        b['b1bf2'][1], cur = _carve(nc, baseR, cur, "b1bf1", [1, 1], BF16)
        b['rcp'], cur = _carve(nc, baseR, cur, "rcp", [1, TT], F32)
        b['rcpb'], cur = _carve(nc, baseR, cur, "rcpb", [1, TT], BF16)
        b['rrep'], cur = _carve(nc, baseR, cur, "rrep", [128, TT], F32)
        assert cur <= szR, cur

        s = Sched(nc, stack)
        _emit_const_init(s, b)
        x_toks = _emit_x_load(s, b, x_in)
        consts = [s.dma('sync', b['perm_sb'][:, :], perm_d.ap()),
                  s.dma('sync', b['kpos'][:, :], kpos_d.ap()),
                  s.dma('sync', b['negrow'][:, :], negrow_d.ap()),
                  s.dma('sync', b['posrow'][:, :], posrow_d.ap())]

        for layer in (0, 1):
            dr = dict(bw=dri[('bw', layer)])
            if layer == 0:
                dr.update(rt=rt_d, la=la_d, lb=lb_d, e16=e16_d, ident=ident_d)
            else:
                dr.update(enc=enc_d, bbias=bbias_d, c0=c0_d, c1=c1_d,
                          mc0=mc0_d, mc1=mc1_d, mt=mt_d)
            emit_ffn_body(s, b, dr, layer, x_toks if layer == 0 else [])
            s.fence()
            t_qm = emit_qkv_fused(s, b, dri, layer, q_scr, kv_stg, consts)
            s.wait_only('gpsimd', [s.last('dmah'), s.last('dmas')])
            t_cc = s.emit('gpsimd', lambda e: e.collective_compute(
                "AllGather", ALU.bypass,
                replica_groups=[[0, 1, 2, 3], [4, 5, 6, 7]],
                ins=[kv_stg.ap().opt()], outs=[kv_gth.ap().opt()]), inc='cc', k=1)
            s.fence()
            emit_att_fused(s, b, dri, layer, q_scr, kv_gth, qpos_d, t_cc, t_qm, consts)
            s.fence()
            emit_ffn_body(s, b, dr, layer, [])
            s.fence()
        st_toks = _emit_x_store(s, b, x_out, waits=[s.last('dve')])
        s.wait_only('sync', st_toks)
        s.finalize()
    return nc


def emit_qkv_fused(s, b, dri, layer, q_scr, kv_stg, consts):
    wq, wk, wv = dri[('wq', layer)], dri[('wk', layer)], dri[('wv', layer)]
    t_qm_last = None
    last_y = None
    for tile in range(2):
        t_y = emit_ln(s, b, tile, extra_first_wait=consts +
                      ([last_y] if last_y else []))
        last_y = t_y
        for phase, nh, wdram, ctd, std in (
                ('q', NCH, wq, dri[('cq', layer)], dri[('sq', layer)]),
                ('k', 8, wk, dri[('ck', layer)], dri[('sk', layer)])):
            ct, st_ = b['ct_sb'], b['st_sb']
            t_tb = [s.dma('gpsimd', b['ct_sb'][:, :], ctd.ap(), waits=[s.last('dve')]),
                    s.dma('gpsimd', b['st_sb'][:, :], std.ap())]
            slab_tok = [None, None]
            for h in range(2):
                slab_tok[h] = s.dma('gpsimd', b['wslab'][h][:, :], wdram.ap()[h],
                                    waits=[s.last('pe')])
            prev_tn = None
            for h in range(nh):
                par = h % 2
                bankA = b['pb'][par]
                bankB = b['pb'][2 + par]
                t_proj = None
                for c in range(NCH):
                    waits = []
                    if c == 0:
                        waits = [slab_tok[par], t_y, s.last('dve')] + \
                            (t_tb if h < 2 else [])
                    t_proj = s.emit('tensor', lambda e, c=c, par=par, bankA=bankA: e.matmul(
                        bankA[:, :], b['wslab'][par][:, c * 128:(c + 1) * 128],
                        b['y_sb'][:, c * TT:(c + 1) * TT],
                        start=(c == 0), stop=(c == NCH - 1), skip_group_check=True),
                        waits=waits, inc='pe' if c == NCH - 1 else None)
                if h + 2 < nh:
                    slab_tok[par] = s.dma('gpsimd', b['wslab'][par][:, :],
                                          wdram.ap()[h + 2], waits=[t_proj])
                t_rope = emit_rope_head(s, b, tile, t_proj, bankA, bankB, ct, st_,
                                        b['qrope'], war_dve=s.last('dmah'))
                if phase == 'q':
                    dst = q_scr.ap()[h * 128:(h + 1) * 128, tile * TT:tile * TT + TT]
                else:
                    dst = kv_stg.ap()[h * 128:(h + 1) * 128, tile * TT:tile * TT + TT]
                s.dma('sync', dst, b['qrope'][:, :], waits=[t_rope])
                t_sq = s.emit('scalar', lambda e: e.activation(
                    out=b['qsq'][:, :], in_=b['qrope'][:, :], func=AF.Square),
                    waits=[t_rope, prev_tn], inc='act')
                t_n = s.emit('tensor', lambda e: e.matmul(
                    b['pb'][4][0:1, :], b['ones_col'][:, 0:1], b['qsq'][:, :],
                    start=True, stop=True, skip_group_check=True),
                    waits=[t_sq, s.last('dve')], inc='pe')
                prev_tn = t_n
                if phase == 'q':
                    slot = tile * 32 + h
                    t_qm_last = s.emit('vector', lambda e, slot=slot: e.reduce_max(
                        b['qm_sb'][:, slot:slot + 1], b['pb'][4][0:1, :], axis=AX.X),
                        waits=[t_n], inc='dve')
                else:
                    t_kn = s.emit('vector', lambda e: e.tensor_copy(
                        b['kn_row'][:, :], b['pb'][4][0:1, :]),
                        waits=[t_n, s.last('dmas')], inc='dve')
                    s.dma('gpsimd', kv_stg.ap()[2048 + h:2049 + h, tile * TT:tile * TT + TT],
                          b['kn_row'][:, :], waits=[t_kn])
        wv_tok = [None, None]
        for c in range(2):
            wv_tok[c] = s.dma('gpsimd', b['wv_sb'][c][:, :], wv.ap()[c],
                              waits=[s.last('pe'), s.last('dve')])
        t_vmm = None
        for c in range(NCH):
            par = c % 2
            for sub in range(4):
                for dv in range(2):
                    bank = b['pb'][sub * 2 + dv]
                    waits = []
                    if sub == 0 and dv == 0:
                        waits = [wv_tok[par]]
                        if c == 0:
                            waits.append(s.last('dve'))
                    t_vmm = s.emit('tensor', lambda e, c=c, par=par, sub=sub, dv=dv, bank=bank: e.matmul(
                        bank[:, :], b['y_sb'][:, c * TT + sub * 128:c * TT + sub * 128 + 128],
                        b['wv_sb'][par][:, dv * TT:(dv + 1) * TT],
                        start=(c == 0), stop=(c == NCH - 1), skip_group_check=True),
                        waits=waits, inc='pe' if (sub == 3 and dv == 1) else None)
            if c + 2 < NCH:
                wv_tok[par] = s.dma('gpsimd', b['wv_sb'][par][:, :], wv.ap()[c + 2],
                                    waits=[t_vmm])
        for sub in range(4):
            for dv in range(2):
                bank = b['pb'][sub * 2 + dv]
                vp = (sub * 2 + dv) % 2
                t_ev = s.emit('vector', lambda e, bank=bank, vp=vp: e.tensor_copy(
                    b['vtmp'][vp][:, :], bank[:, :]),
                    waits=[t_vmm, s.last('dmah')], inc='dve')
                s.dma('sync',
                      kv_stg.ap()[1024 + tile * TT + sub * 128: 1024 + tile * TT + sub * 128 + 128,
                                  dv * TT:(dv + 1) * TT],
                      b['vtmp'][vp][:, :], waits=[t_ev])
    return t_qm_last


def emit_att_fused(s, b, dri, layer, q_scr, kv_gth, qpos_d, t_cc, t_qm, consts):
    wo = dri[('wo', layer)]
    t_mk = None
    for r4 in range(4):
        t_l = s.dma('sync', b['knb'][:, :],
                    kv_gth.ap()[r4 * 2056 + 2048: r4 * 2056 + 2056, :],
                    waits=[t_cc] + ([t_mk] if t_mk else []))
        t_mk = s.emit('vector', lambda e, r4=r4: e.reduce_max(
            b['mkacc'][:, r4:r4 + 1], b['knb'][:, :], axis=AX.X),
            waits=[t_l], inc='dve')
    t_mk = s.emit('vector', lambda e: e.reduce_max(
        b['mk8'][:, :], b['mkacc'][:, 0:4], axis=AX.X), waits=[t_mk], inc='dve')
    t_tr = s.emit('tensor', lambda e: e.transpose(
        b['pb'][6][0:1, 0:8], b['mk8'][:, 0:1], b['ident'][0:8, 0:8]),
        waits=[t_mk, s.last('dve')], inc='pe')
    t_mk2 = s.emit('vector', lambda e: e.tensor_copy(
        b['mk2'][:, :], b['pb'][6][0:1, 0:8]), waits=[t_tr], inc='dve')

    for tile in range(2):
        nkt = 16 if tile == 0 else 32
        t_qp = s.dma('sync', b['qpos'][:, :],
                     qpos_d.ap()[:, tile * TT:tile * TT + TT], waits=[s.last('dve')])
        t_qr = s.emit('tensor', lambda e, tile=tile: e.matmul(
            b['pb'][6][:, :], b['ones_row_f32'][0:1, :], b['qpos'][0:1, :],
            start=True, stop=True, skip_group_check=True),
            waits=[t_qp, s.last('dve')], inc='pe')
        t_qrep = s.emit('vector', lambda e: e.tensor_copy(
            b['qpos_rep'][:, :], b['pb'][6][:, :]), waits=[t_qr], inc='dve')
        t_ctx_all = None
        t_q = [None, None]
        t_q_lastuse = [None, None]
        t_exp_head = [None, None]
        pbS = [b['pb'][0], b['pb'][1], b['pb'][6], b['pb'][7]]
        for kvh in range(8):
            kt_toks = []
            for r4 in range(4):
                halves = (0,) if tile == 0 else (0, 1)
                for half in halves:
                    chunk = r4 if half == 0 else 7 - r4
                    kt_toks.append(s.dma(
                        'sync', b['ksl'][:, 512 * chunk:512 * chunk + 512],
                        kv_gth.ap()[r4 * 2056 + kvh * 128: r4 * 2056 + kvh * 128 + 128,
                                    half * 512:half * 512 + 512],
                        waits=[s.last('pe')] if (r4 == 0 and half == 0) else []))
                    vin = kv_gth.ap().rearrange("(r t) d -> r t d", t=2056)[r4]
                    kt_toks.append(s.dma(
                        'gpsimd',
                        b['vsl'][:, 512 * chunk: 512 * chunk + 512]
                        .rearrange("p (q d) -> p q d", d=128),
                        vin[1024 + half * 512:1024 + half * 512 + 512]
                        .rearrange("(q p) d -> p q d", p=128)[:, :, kvh * 128:kvh * 128 + 128],
                        waits=[s.last('pe')] if (r4 == 0 and half == 0) else []))
            for qh in range(4):
                h = kvh * 4 + qh
                hp = h % 2
                if qh == 0 and kvh == 0:
                    t_q[hp] = s.dma('sync', b['q_sb2'][hp][:, :],
                                    q_scr.ap()[h * 128:(h + 1) * 128,
                                               tile * TT:tile * TT + TT],
                                    waits=[s.last('pe')])
                t_b2 = s.emit('vector', lambda e, tile=tile, h=h, kvh=kvh, hp=hp: e.tensor_tensor(
                    out=b['b2row2'][hp][:, :],
                    in0=b['qm_sb'][:, tile * 32 + h:tile * 32 + h + 1],
                    in1=b['mk2'][:, kvh:kvh + 1], op=ALU.mult),
                    waits=[t_mk2, t_qm, t_exp_head[hp]], inc='dve')
                t_b1 = s.emit('scalar', lambda e, hp=hp: e.activation(
                    out=b['b1row2'][hp][:, :], in_=b['b2row2'][hp][:, :], func=AF.Sqrt),
                    waits=[t_b2], inc='act')
                t_bb = s.emit('vector', lambda e, hp=hp: e.tensor_copy(
                    b['b1bf2'][hp][:, :], b['b1row2'][hp][:, :]),
                    waits=[t_b1], inc='dve')
                t_bn = s.emit('tensor', lambda e, hp=hp: e.matmul(
                    b['pb'][4][:, hp:hp + 1], b['negrow'][0:1, :], b['b1bf2'][hp][:, :],
                    start=True, stop=True, skip_group_check=True),
                    waits=[t_bb], inc='pe')
                t_nb = s.emit('vector', lambda e, hp=hp: e.tensor_copy(
                    b['negB2'][hp][:, :], b['pb'][4][:, hp:hp + 1]),
                    waits=[t_bn, t_exp_head[hp]], inc='dve')
                if h + 1 < 32:
                    np_ = (h + 1) % 2
                    t_q[np_] = s.dma('sync', b['q_sb2'][np_][:, :],
                                     q_scr.ap()[(h + 1) * 128:(h + 2) * 128,
                                                tile * TT:tile * TT + TT],
                                     waits=[t_q_lastuse[np_]])
                t_exp = [None] * 4
                t_msk = [None] * 4
                t_den = [None] * 4
                LAG = 3
                for i in range(nkt + LAG):
                    if i < nkt:
                        kt, par = i, i % 4
                        t_sT = s.emit('tensor', lambda e, kt=kt, par=par, hp=hp: e.matmul(
                            pbS[par][:, :], b['ksl'][:, kt * 128:(kt + 1) * 128],
                            b['q_sb2'][hp][:, :], start=True, stop=True,
                            skip_group_check=True),
                            waits=[t_exp[par], t_q[hp]] +
                            (kt_toks + [s.last('dve')] if kt == 0 else []),
                            inc='pe')
                        if kt == nkt - 1:
                            t_q_lastuse[hp] = t_sT
                        t_exp[par] = s.emit('scalar', lambda e, par=par, hp=hp: e.activation(
                            out=b['exp_sb'][par][:, :], in_=pbS[par][:, :], func=AF.Exp,
                            bias=b['negB2'][hp][:, 0:1]),
                            waits=[t_sT, t_nb, t_den[par], t_msk[par]], inc='act')
                        t_msk[par] = s.emit('vector', lambda e, par=par, kt=kt: e.scalar_tensor_tensor(
                            out=b['exp_sb'][par][:, :], in0=b['qpos_rep'][:, :],
                            scalar=b['kpos'][:, kt:kt + 1], in1=b['exp_sb'][par][:, :],
                            op0=ALU.is_ge, op1=ALU.mult),
                            waits=[t_exp[par], t_qrep], inc='dve')
                    if i >= LAG:
                        kt = i - LAG
                        par = kt % 4
                        s.emit('tensor', lambda e, kt=kt, par=par, nkt=nkt: e.matmul(
                            b['pb'][2][:, :], b['vsl'][:, kt * 128:(kt + 1) * 128],
                            b['exp_sb'][par][:, :], start=(kt == 0), stop=(kt == nkt - 1),
                            skip_group_check=True),
                            waits=[t_msk[par]], inc=None)
                        t_den[par] = s.emit('tensor', lambda e, kt=kt, par=par, nkt=nkt: e.matmul(
                            b['pb'][3][0:1, :], b['ones_col'][:, 0:1],
                            b['exp_sb'][par][:, :], start=(kt == 0), stop=(kt == nkt - 1),
                            skip_group_check=True), inc='pe')
                t_exp_head[hp] = t_exp[(nkt - 1) % 4]
                t_den_last = t_den[(nkt - 1) % 4]
                t_rc = s.emit('vector', lambda e: e.reciprocal(
                    b['rcp'][:, :], b['pb'][3][0:1, :]), waits=[t_den_last], inc='dve')
                t_rcb = s.emit('vector', lambda e: e.tensor_copy(
                    b['rcpb'][:, :], b['rcp'][:, :]), waits=[t_rc], inc='dve')
                t_rr = s.emit('tensor', lambda e: e.matmul(
                    b['pb'][5][:, :], b['posrow'][0:1, :], b['rcpb'][:, :],
                    start=True, stop=True, skip_group_check=True),
                    waits=[t_rcb], inc='pe')
                t_re = s.emit('vector', lambda e: e.tensor_copy(
                    b['rrep'][:, :], b['pb'][5][:, :]), waits=[t_rr], inc='dve')
                t_ctx_all = s.emit('vector', lambda e, h=h: e.tensor_tensor(
                    out=b['ctx_sb'][:, h * TT:(h + 1) * TT], in0=b['pb'][2][:, :],
                    in1=b['rrep'][:, :], op=ALU.mult),
                    waits=[t_re], inc='dve')
        last_ev, _ = emit_wsweep(s, b, tile, wo, 2, 1.0, extra_mm=None,
                                 first_waits=[t_ctx_all], rhs='y')
    return last_ev


_FUSED_NC = [None]

def kernel(**inp):
    p = _prep_shared(inp)
    pc = _prep_cores()
    X = np.asarray(inp['L0_attn_X'])
    xs = [np.ascontiguousarray(X[c // 4, _core_positions(c), :].T.astype(np.float32))
          for c in range(NCORES)]
    if _FUSED_NC[0] is None:
        _FUSED_NC[0] = build_fused()
    nc = _FUSED_NC[0]
    maps = []
    for c in range(NCORES):
        t = pc[c]
        m = dict(x_in=xs[c],
                 bw0=p['bw0'], bw1=p['bw1'],
                 wq0=p['wq0'], wk0=p['wk0'], wv0=p['wv0'], wo0=p['wo0'],
                 wq1=p['wq1'], wk1=p['wk1'], wv1=p['wv1'], wo1=p['wo1'],
                 cq0=t['cq0'], sq0=t['sq0'], ck0=t['ck0'], sk0=t['sk0'],
                 cq1=t['cq1'], sq1=t['sq1'], ck1=t['ck1'], sk1=t['sk1'],
                 rt=p['rt0'], la=p['la0'], lb=p['lb0'],
                 enc=p['enc1'], bbias=p['bb1'], c0=p['c0'], c1=p['c1'],
                 mc0=p['mc0'], mc1=p['mc1'], mt=p['mt'],
                 e16=p['e16'], ident=p['ident'], perm=p['perm'],
                 kpos=p['kpos'], qpos=t['qpos'],
                 negrow=p['negrow'], posrow=p['posrow'])
        maps.append(m)
    res = _run(nc, maps)
    out = np.empty((2, 4096, D), dtype=np.float32)
    for c in range(NCORES):
        out[c // 4, _core_positions(c), :] = res[c]['x_out'].T
    return out


# version 06
# speedup vs baseline: 1.5444x; 1.0618x over previous
import numpy as np
import ml_dtypes
import math
import contextlib
import concourse.bass as bass
import concourse.mybir as mybir
from concourse.bass_utils import run_bass_kernel_spmd

F32 = mybir.dt.float32
BF16 = mybir.dt.bfloat16
I32 = mybir.dt.int32
U8 = mybir.dt.uint8
AF = mybir.ActivationFunctionType
ALU = mybir.AluOpType
AX = mybir.AxisListType
BF = ml_dtypes.bfloat16

NCORES = 8
D = 4096
NCH = 32
TOK = 1024
TT = 512
EPS = 1e-5


def _core_chunks(c):
    m = c % 4
    return m, 7 - m

def _core_positions(c):
    m, h = _core_chunks(c)
    return np.concatenate([np.arange(512 * m, 512 * m + 512),
                           np.arange(512 * h, 512 * h + 512)])

def _tile_w(Wm):
    din, dout = Wm.shape
    return np.ascontiguousarray(
        Wm.reshape(din // 128, 128, dout // 128, 128).transpose(2, 1, 0, 3)
    ).reshape(dout // 128, 128, (din // 128) * 128)

def _rope_l0(pos):
    inv = 1.0 / (10000.0 ** (np.arange(0, 64, dtype=np.float64) / 64.0))
    ang = inv[:, None] * pos[None, :].astype(np.float64)
    c, s = np.cos(ang), np.sin(ang)
    return (np.vstack([c, c]).astype(np.float32), np.vstack([s, s]).astype(np.float32))

def _rope_l1(pos):
    idx = np.arange(0, 128, 2, dtype=np.float64) / 128.0
    freq = 1.0 / (2.0 * math.pi) * (500000.0 ** (-idx))
    factor, lo, hi, L0 = 32.0, 1.0, 4.0, 8192
    freq_low = lo / L0
    hi_f = hi / L0
    freq_scaled = np.where(freq < freq_low, freq / factor, freq)
    smooth = np.clip((L0 * freq - lo) / (hi - lo), 0.0, 1.0)
    freq_smooth = (1.0 - smooth) * (freq / factor) + smooth * freq
    is_mid = (freq >= freq_low) & (freq <= hi_f)
    f = np.where(is_mid, freq_smooth, freq_scaled)
    phase = 2.0 * math.pi * pos[:, None].astype(np.float64) * f[None, :]
    c, s = np.cos(phase).T, np.sin(phase).T
    return (np.vstack([c, c]).astype(np.float32), np.vstack([s, s]).astype(np.float32))

def _perm_rot():
    P = np.zeros((128, 128), dtype=np.float32)
    for m in range(64):
        P[m + 64, m] = -1.0
        P[m, m + 64] = 1.0
    return P.astype(BF)

def _e16():
    E = np.zeros((8, 128), dtype=np.float32)
    for e in range(8):
        E[e, 16 * e:16 * e + 16] = 2.0
    return E

def _mask_atlas():
    pats = [np.ones((128, 512), np.float32)]
    i = np.arange(128)[:, None]
    j = np.arange(512)[None, :]
    for d in (0, 128, 256, 384):
        pats.append((i <= j - d).astype(np.float32))
    pats.append(np.zeros((128, 512), np.float32))
    return np.concatenate(pats, axis=1).astype(BF)

def _off_feed(c):
    out = []
    for tile, nkt in ((0, 16), (1, 32)):
        p = _core_chunks(c)[tile]
        for kt in range(nkt):
            dp = kt - 4 * p
            out.append(0 if dp < 0 else (1 + dp if dp <= 3 else 5))
    out += [0] * (64 - len(out))
    return np.array(out, dtype=np.int32)[None, :]

def _mt_bf16():
    n = 4096
    t = np.arange(n, dtype=np.float64)
    M = np.empty((n, n), dtype=np.float64)
    M[:, 0] = 1.0 / math.sqrt(n)
    jj = np.arange(1, n // 2, dtype=np.float64)
    ang = 2.0 * math.pi * t[:, None] * jj[None, :] / n
    M[:, 1:n - 1:2] = math.sqrt(2.0 / n) * np.cos(ang)
    M[:, 2:n - 1:2] = math.sqrt(2.0 / n) * np.sin(ang)
    M[:, n - 1] = ((-1.0) ** t) / math.sqrt(n)
    return np.ascontiguousarray(M.T).astype(BF)

_HC = {}

def _cached(key, fn):
    if key not in _HC:
        _HC[key] = fn()
    return _HC[key]


class Sched:
    ENGINES = ("tensor", "vector", "scalar", "gpsimd", "sync")

    def __init__(self, nc, stack):
        self.nc = nc
        self.stack = stack
        self.prog = {e: [] for e in self.ENGINES}
        self.sems = {}
        self.cnt = {}

    def sem(self, name):
        if name not in self.sems:
            self.sems[name] = self.stack.enter_context(self.nc.semaphore(name))
            self.cnt[name] = 0
        return self.sems[name]

    def emit(self, eng, fn, waits=(), inc=None, k=1):
        tok = None
        if inc is not None:
            self.sem(inc)
            self.cnt[inc] += k
            tok = (inc, self.cnt[inc])
        wl = []
        for t in waits:
            if t is None:
                continue
            sname, v = t
            self.sem(sname)
            wl.append((self.sems[sname], v))
        semh = self.sems[inc] if inc is not None else None

        def closure(e, _wl=tuple(wl), _fn=fn, _semh=semh, _k=k):
            for sh, v in _wl:
                e.wait_ge(sh, v)
            inst = _fn(e)
            if _semh is not None:
                inst.then_inc(_semh, _k)
        self.prog[eng].append(closure)
        return tok

    def dma(self, queue, out, in_, waits=()):
        sem = 'dmah' if queue == 'sync' else 'dmas'
        return self.emit(queue, lambda e, o=out, i=in_: e.dma_start(out=o, in_=i),
                         waits=waits, inc=sem, k=16)

    def wait_only(self, eng, waits):
        wl = []
        for t in waits:
            if t is None:
                continue
            sname, v = t
            self.sem(sname)
            wl.append((self.sems[sname], v))

        def closure(e, _wl=tuple(wl)):
            for sh, v in _wl:
                e.wait_ge(sh, v)
        self.prog[eng].append(closure)

    def fence(self):
        toks = [self.last(x) for x in ('pe', 'dve', 'act', 'dmah', 'dmas', 'cc')]
        toks = [t for t in toks if t is not None]
        for e in self.ENGINES:
            self.wait_only(e, toks)

    def last(self, name):
        if name not in self.cnt or self.cnt[name] == 0:
            return None
        return (name, self.cnt[name])

    def finalize(self):
        with self.nc.Block() as block:
            for eng_name in self.ENGINES:
                ops = list(self.prog[eng_name])
                if not ops:
                    continue
                dec = getattr(block, eng_name)

                def body(e, _ops=ops):
                    for f in _ops:
                        f(e)
                dec(body)



def emit_ln(s, b, tile, extra_first_wait=()):
    col = lambda c: c * TOK + tile * TT
    pe_s1 = [None, None]
    pe_s2 = [None, None]
    ac_sq = [None, None]
    t_first = list(extra_first_wait)
    last2 = None
    for c in range(NCH):
        par = c % 2
        t_cast = s.emit('vector', lambda e, c=c, par=par: e.tensor_copy(
            b['sc_bf'][par][:, :], b['x_sb'][:, col(c):col(c) + TT]),
            waits=(t_first if c == 0 else []) + [pe_s1[par], ac_sq[par]], inc='dve')
        t_sq = s.emit('scalar', lambda e, par=par: e.activation(
            out=b['sc_sq'][par][:, :], in_=b['sc_bf'][par][:, :], func=AF.Square),
            waits=[t_cast, pe_s2[par]], inc='act')
        ac_sq[par] = t_sq
        pe_s1[par] = s.emit('tensor', lambda e, c=c, par=par: e.matmul(
            b['pb'][0][0:1, :], b['ones_col'][:, 0:1], b['sc_bf'][par][:, :],
            start=(c == 0), stop=(c == NCH - 1), skip_group_check=True),
            waits=[t_cast] + (t_first if c == 0 else []), inc='pe')
        pe_s2[par] = s.emit('tensor', lambda e, c=c, par=par: e.matmul(
            b['pb'][1][0:1, :], b['ones_col'][:, 0:1], b['sc_sq'][par][:, :],
            start=(c == 0), stop=(c == NCH - 1), skip_group_check=True),
            waits=[t_sq], inc='pe')
        last2 = pe_s2[par]
    t_m = s.emit('vector', lambda e: e.tensor_scalar_mul(
        b['m_row'][:, :], b['pb'][0][0:1, :], 1.0 / D), waits=[pe_s1[1], last2], inc='dve')
    t_msq = s.emit('vector', lambda e: e.tensor_mul(
        b['a_row'][:, :], b['m_row'][:, :], b['m_row'][:, :]), waits=[t_m], inc='dve')
    t_var = s.emit('vector', lambda e: e.scalar_tensor_tensor(
        out=b['a_row'][:, :], in0=b['pb'][1][0:1, :], scalar=1.0 / D,
        in1=b['a_row'][:, :], op0=ALU.mult, op1=ALU.subtract), waits=[t_msq], inc='dve')
    t_ln = s.emit('scalar', lambda e: e.activation(
        out=b['rstd_row'][:, :], in_=b['a_row'][:, :], func=AF.Ln,
        bias=b['eps_row'][:, 0:1]), waits=[t_var], inc='act')
    t_rstd = s.emit('scalar', lambda e: e.activation(
        out=b['a_row'][:, :], in_=b['rstd_row'][:, :], func=AF.Exp, scale=-0.5),
        waits=[t_ln], inc='act')
    t_mrs = s.emit('vector', lambda e: e.tensor_mul(
        b['rstd_row'][:, :], b['m_row'][:, :], b['a_row'][:, :]),
        waits=[t_rstd], inc='dve')
    t_r1 = s.emit('tensor', lambda e: e.matmul(
        b['pb'][2][:, :], b['ones_row_f32'][0:1, :], b['a_row'][:, :],
        start=True, stop=True), waits=[t_rstd, t_mrs], inc='pe')
    t_r2 = s.emit('tensor', lambda e: e.matmul(
        b['pb'][3][:, :], b['ones_row_f32'][0:1, :], b['rstd_row'][:, :],
        start=True, stop=True), waits=[t_mrs], inc='pe')
    t_y = None
    for c in range(NCH):
        s.emit('vector', lambda e, c=c: e.tensor_mul(
            b['sc_f32'][:, :], b['x_sb'][:, col(c):col(c) + TT], b['pb'][2][:, :]),
            waits=[t_r1, t_r2] if c == 0 else [], inc='dve')
        t_y = s.emit('vector', lambda e, c=c: e.tensor_sub(
            b['y_sb'][:, c * TT:(c + 1) * TT], b['sc_f32'][:, :], b['pb'][3][:, :]),
            inc='dve')
    return t_y


def emit_wsweep(s, b, tile, w_dram, n_slab_pre, scale, extra_mm=None,
                first_waits=(), rhs='y'):
    xcol = lambda j: j * TOK + tile * TT
    if rhs == 'y':
        rhs_ap = lambda c: b['y_sb'][:, c * TT:(c + 1) * TT]
    else:
        rhs_ap = lambda c: b['ctx_sb'][:, c * TT:(c + 1) * TT]
    nbank = 4
    evac_tok = [None] * nbank
    slab_tok = [None, None]
    for j in range(2):
        slab_tok[j] = s.dma('gpsimd', b['wslab'][j][:, :], w_dram.ap()[j],
                            waits=[s.last('pe')])
    t_ev = None
    t_mm = None
    for j in range(NCH):
        par = j % 2
        bank = b['pb'][4 + (j % nbank)]
        n_extra = 0 if extra_mm is None else (1 if extra_mm == 'l0inc' else 2)
        for c in range(NCH):
            waits = []
            if c == 0:
                waits = [slab_tok[par], evac_tok[j % nbank]] + (list(first_waits) if j == 0 else [])
            is_last = (c == NCH - 1) and n_extra == 0
            t_mm = s.emit('tensor', lambda e, j=j, c=c, par=par, bank=bank, il=is_last: e.matmul(
                bank[:, :], b['wslab'][par][:, c * 128:(c + 1) * 128], rhs_ap(c),
                start=(c == 0), stop=il, skip_group_check=True),
                waits=waits, inc='pe' if (c == NCH - 1) else None)
        if extra_mm == 'l0inc':
            t_mm = s.emit('tensor', lambda e, j=j, bank=bank: e.matmul(
                bank[:, :], b['ab_sb'][:, j * 128:(j + 1) * 128], b['midw'][:, :],
                start=False, stop=True, skip_group_check=True), inc='pe')
        elif extra_mm == 'l1':
            t_mm = s.emit('tensor', lambda e, j=j, bank=bank: e.matmul(
                bank[:, :], b['g_sb'][0:40, j * 128:(j + 1) * 128], b['e_sb'][0:40, :],
                start=False, stop=True, skip_group_check=True), inc='pe')
        if j + 2 < NCH:
            slab_tok[par] = s.dma('gpsimd', b['wslab'][par][:, :], w_dram.ap()[j + 2],
                                  waits=[t_mm])
        if extra_mm == 'l1':
            s.emit('vector', lambda e, j=j, bank=bank: e.tensor_scalar_add(
                b['sc_f32'][:, :], bank[:, :], b['bb2'][:, j:j + 1]),
                waits=[t_mm], inc='dve')
            t_ev = s.emit('vector', lambda e, j=j, sc=scale: e.scalar_tensor_tensor(
                out=b['x_sb'][:, xcol(j):xcol(j) + TT], in0=b['sc_f32'][:, :], scalar=sc,
                in1=b['x_sb'][:, xcol(j):xcol(j) + TT], op0=ALU.mult, op1=ALU.add),
                inc='dve')
        else:
            t_ev = s.emit('vector', lambda e, j=j, bank=bank, sc=scale: e.scalar_tensor_tensor(
                out=b['x_sb'][:, xcol(j):xcol(j) + TT], in0=bank[:, :], scalar=sc,
                in1=b['x_sb'][:, xcol(j):xcol(j) + TT], op0=ALU.mult, op1=ALU.add),
                waits=[t_mm], inc='dve')
        evac_tok[j % nbank] = t_ev
    return t_ev, t_mm


def emit_moe_head(s, b, tile, t_y, la_tok=None, dbg=None):
    ycol = lambda c: c * TT
    t_log = None
    for sub in range(4):
        for c in range(NCH):
            t_log = s.emit('tensor', lambda e, sub=sub, c=c: e.matmul(
                b['pb'][0][:, sub * 8:sub * 8 + 8],
                b['y_sb'][:, ycol(c) + sub * 128: ycol(c) + sub * 128 + 128],
                b['rt_sb'][:, c * 8:(c + 1) * 8],
                start=(c == 0), stop=(c == NCH - 1), skip_group_check=True),
                waits=[t_y] if (sub == 0 and c == 0) else [],
                inc='pe' if c == NCH - 1 else None)
    g = lambda t: t.rearrange("p (g e) -> p g e", e=8)
    bc = lambda t: t[:, :, None].broadcast_to([128, 4, 8])
    if dbg is not None:
        s.emit('vector', lambda e: e.tensor_copy(b['log_sb'][:, :], b['pb'][0][:, 0:32]),
               waits=[t_log], inc='dve')
    tp = s.emit('vector', lambda e: e.reduce_max(
        b['w4a'][:, :, None], g(b['pb'][0][:, 0:32]), axis=AX.X), waits=[t_log], inc='dve')
    t1 = s.emit('vector', lambda e: e.tensor_tensor(
        out=g(b['w32a']), in0=g(b['pb'][0][:, 0:32]), in1=bc(b['w4a']),
        op=ALU.subtract), waits=[tp], inc='dve')
    t2 = s.emit('scalar', lambda e: e.activation(
        out=b['w32p'][:, :], in_=b['w32a'][:, :], func=AF.Exp), waits=[t1], inc='act')
    tp = s.emit('vector', lambda e: e.reduce_max(
        b['w4a'][:, :, None], g(b['w32p']), axis=AX.X), waits=[t2], inc='dve')
    tp = s.emit('vector', lambda e: e.tensor_tensor(
        out=g(b['w32a']), in0=g(b['w32p']), in1=bc(b['w4a']), op=ALU.is_equal),
        waits=[tp], inc='dve')
    tp = s.emit('vector', lambda e: e.scalar_tensor_tensor(
        out=g(b['w32b']), in0=g(b['w32a']), scalar=-2.0, in1=g(b['w32p']),
        op0=ALU.mult, op1=ALU.add), waits=[tp], inc='dve')
    t6 = s.emit('vector', lambda e: e.reduce_max(
        b['w4b'][:, :, None], g(b['w32b']), axis=AX.X), waits=[tp], inc='dve')
    t8 = s.emit('vector', lambda e: e.tensor_add(
        b['w4a'][:, :], b['w4a'][:, :], b['w4b'][:, :]), waits=[t6], inc='dve')
    tp = s.emit('vector', lambda e: e.tensor_tensor(
        out=g(b['w32b']), in0=g(b['w32p']), in1=bc(b['w4b']), op=ALU.is_ge),
        waits=[t8], inc='dve')
    t9 = s.emit('vector', lambda e: e.reciprocal(
        b['w4b'][:, :], b['w4a'][:, :]), waits=[tp], inc='dve')
    tp = s.emit('vector', lambda e: e.tensor_tensor(
        out=g(b['w32a']), in0=g(b['w32p']), in1=g(b['w32b']), op=ALU.mult),
        waits=[t9], inc='dve')
    t11 = s.emit('vector', lambda e: e.tensor_tensor(
        out=g(b['w32a']), in0=g(b['w32a']), in1=bc(b['w4b']), op=ALU.mult),
        waits=[tp], inc='dve')
    tT = None
    for sub in range(4):
        tT = s.emit('tensor', lambda e, sub=sub: e.transpose(
            b['pb'][1][0:8, sub * 128:(sub + 1) * 128],
            b['w32a'][:, sub * 8:(sub + 1) * 8], b['ident'][:, :]),
            waits=[t11] if sub == 0 else [], inc='pe' if sub == 3 else None)
    tE = s.emit('vector', lambda e: e.tensor_copy(
        b['wfm'][:, :], b['pb'][1][0:8, 0:512]), waits=[tT], inc='dve')
    tR = s.emit('tensor', lambda e: e.matmul(
        b['pb'][2][:, :], b['e16'][:, :], b['wfm'][:, :], start=True, stop=True),
        waits=[tE], inc='pe')
    tRe = s.emit('vector', lambda e: e.tensor_copy(
        b['wrep'][:, :], b['pb'][2][:, :]), waits=[tR], inc='dve')
    tM = None
    for c in range(NCH):
        tM = s.emit('tensor', lambda e, c=c: e.matmul(
            b['pb'][3][:, :], b['ab_sb'][:, c * 128:(c + 1) * 128],
            b['y_sb'][:, c * TT:(c + 1) * TT],
            start=(c == 0), stop=(c == NCH - 1), skip_group_check=True),
            waits=[la_tok] if c == 0 else [], inc='pe' if c == NCH - 1 else None)
    tMw = s.emit('vector', lambda e: e.tensor_tensor(
        out=b['midw'][:, :], in0=b['pb'][3][:, :], in1=b['wrep'][:, :], op=ALU.mult),
        waits=[tM, tRe], inc='dve')
    return tMw


def emit_l1_head(s, b, t_y):
    tE = None
    for c in range(NCH):
        tE = s.emit('tensor', lambda e, c=c: e.matmul(
            b['pb'][0][0:40, :], b['enc_sb'][:, c * 40:(c + 1) * 40],
            b['y_sb'][:, c * TT:(c + 1) * TT],
            start=(c == 0), stop=(c == NCH - 1), skip_group_check=True),
            waits=[t_y] if c == 0 else [], inc='pe' if c == NCH - 1 else None)
    tEv = s.emit('vector', lambda e: e.tensor_copy(
        b['e_sb'][:, :], b['pb'][0][0:40, :]), waits=[tE], inc='dve')
    return tEv

def emit_gprep(s, b, dr):
    ws0 = b['wslab'][0]
    cf = lambda n: ws0[:, 0:NCH * n]
    mk = lambda n: ws0[:, 1024:1024 + NCH * n]
    wbuf = {'w0s': ws0[:, 2048:2048 + 1024], 'w1s': ws0[:, 3072:3072 + 256]}
    t_acc = None
    for which, n_r, w_buf in (('c0', 32, 'w0s'), ('c1', 8, 'w1s')):
        for e_i in range(3):
            cf_ap = cf(n_r).rearrange("p (c r) -> p c r", r=n_r)
            mk_ap = mk(n_r).rearrange("p (c r) -> p c r", r=n_r)
            t_cf = s.dma('gpsimd', cf_ap,
                         dr[which].ap()[e_i].rearrange("c p r -> p c r"),
                         waits=[t_acc])
            t_mk = s.dma('gpsimd', mk_ap,
                         dr['m' + which].ap()[e_i].rearrange("c p r -> p c r"))
            s.emit('vector', lambda e, n_r=n_r: e.tensor_tensor(
                out=cf(n_r), in0=cf(n_r), in1=mk(n_r), op=ALU.mult),
                waits=[t_cf, t_mk], inc='dve')
            if e_i == 0:
                t_acc = s.emit('vector', lambda e, n_r=n_r, w_buf=w_buf: e.tensor_copy(
                    wbuf[w_buf], cf(n_r)), inc='dve')
            else:
                t_acc = s.emit('vector', lambda e, n_r=n_r, w_buf=w_buf: e.tensor_tensor(
                    out=wbuf[w_buf], in0=wbuf[w_buf], in1=cf(n_r), op=ALU.add), inc='dve')
    r3 = lambda t, r, w: t.rearrange("p (c r) -> p c r", r=w)
    s.emit('vector', lambda e: e.tensor_scalar_mul(
        b['wcomb'].rearrange("p (c r) -> p c r", r=40)[:, :, 0:32],
        wbuf['w0s'].rearrange("p (c r) -> p c r", r=32), 0.5), waits=[t_acc], inc='dve')
    t_wc = s.emit('vector', lambda e: e.tensor_scalar_mul(
        b['wcomb'].rearrange("p (c r) -> p c r", r=40)[:, :, 32:40],
        wbuf['w1s'].rearrange("p (c r) -> p c r", r=8), 0.5), inc='dve')
    slab_tok = [None, None]
    for c in range(2):
        slab_tok[c] = s.dma('sync', b['wslab'][c][:, :], dr['mt'].ap()[c],
                            waits=[t_wc] if c == 0 else [s.last('pe')])
    t_mm = None
    for c in range(NCH):
        par = c % 2
        for tt_i in range(8):
            t_mm = s.emit('tensor', lambda e, c=c, par=par, tt_i=tt_i: e.matmul(
                b['pb'][tt_i][0:40, :], b['wcomb'][:, c * 40:(c + 1) * 40],
                b['wslab'][par][:, tt_i * 512:(tt_i + 1) * 512],
                start=(c == 0), stop=(c == NCH - 1), skip_group_check=True),
                waits=([slab_tok[par], t_wc] if tt_i == 0 else []),
                inc='pe' if tt_i == 7 else None)
        if c + 2 < NCH:
            slab_tok[par] = s.dma('sync', b['wslab'][par][:, :], dr['mt'].ap()[c + 2],
                                  waits=[t_mm])
    t_g = None
    for tt_i in range(8):
        t_g = s.emit('vector', lambda e, tt_i=tt_i: e.tensor_copy(
            b['g_sb'][:, tt_i * 512:(tt_i + 1) * 512], b['pb'][tt_i][0:40, :]),
            waits=[t_mm], inc='dve')
    return t_g, t_mm


def _alloc_common(nc, st, b, pb_n=8, ln=True, ctx=True):
    al = (lambda n, sh, dt: st.enter_context(nc.sbuf_tensor(n, sh, dt))) if ctx \
        else (lambda n, sh, dt: nc.alloc_sbuf_tensor(n, sh, dt))
    b['x_sb'] = al("x_sb", [128, NCH * TOK], F32)
    b['ones_col'] = al("ones_col", [128, 1], BF16)
    if pb_n:
        b['pb'] = [st.enter_context(nc.psum_tensor(f"pb{i}", [128, TT], F32))
                   for i in range(pb_n)]
    if ln:
        b['sc_bf'] = [al(f"scbf{i}", [128, TT], BF16) for i in range(2)]
        b['sc_sq'] = [al(f"scsq{i}", [128, TT], BF16) for i in range(2)]
        b['sc_f32'] = al("scf32", [128, TT], F32)
        for r in ('m_row', 'a_row', 'rstd_row'):
            b[r] = al(r, [1, TT], F32)
        b['eps_row'] = al("eps_row", [1, 1], F32)
        b['ones_row_f32'] = al("ones_row_f32", [1, 128], F32)

def _emit_const_init(s, b):
    s.emit('vector', lambda e: e.memset(b['ones_col'][:, :], 1.0), inc='dve')
    if 'ones_row_f32' in b:
        s.emit('vector', lambda e: e.memset(b['ones_row_f32'][:, :], 1.0), inc='dve')
    if 'eps_row' in b:
        s.emit('vector', lambda e: e.memset(b['eps_row'][:, :], EPS), inc='dve')

def _emit_x_load(s, b, x_dram):
    toks = []
    for q in range(4):
        toks.append(s.dma('sync',
                          b['x_sb'].ap().rearrange("p (c t) -> p c t", t=TOK)[:, q * 8:(q + 1) * 8],
                          x_dram.ap().rearrange("(c p) t -> p c t", p=128)[:, q * 8:(q + 1) * 8]))
    return toks

def _emit_x_store(s, b, xo_dram, waits):
    toks = []
    for q in range(4):
        toks.append(s.dma('sync',
                          xo_dram.ap().rearrange("(c p) t -> p c t", p=128)[:, q * 8:(q + 1) * 8],
                          b['x_sb'].ap().rearrange("p (c t) -> p c t", t=TOK)[:, q * 8:(q + 1) * 8],
                          waits=waits))
    return toks


def build_ffn_stage(layer):
    stack = contextlib.ExitStack()
    with stack:
        nc = bass.Bass("TRN2", target_bir_lowering=False, debug=False)
        b = {}
        x_in = nc.dram_tensor("x_in", [D, TOK], F32, kind="ExternalInput")
        x_out = nc.dram_tensor("x_out", [D, TOK], F32, kind="ExternalOutput")
        b['bw_dram'] = nc.dram_tensor("bw", [NCH, 128, NCH * 128], F32, kind="ExternalInput")
        if layer == 0:
            rt = nc.dram_tensor("rt", [128, NCH * 8], F32, kind="ExternalInput")
            la = nc.dram_tensor("la", [128, NCH * 128], F32, kind="ExternalInput")
            lb = nc.dram_tensor("lb", [128, NCH * 128], F32, kind="ExternalInput")
            e16 = nc.dram_tensor("e16", [8, 128], F32, kind="ExternalInput")
            ident = nc.dram_tensor("ident", [128, 128], F32, kind="ExternalInput")
        else:
            enc = nc.dram_tensor("enc", [128, NCH * 40], F32, kind="ExternalInput")
            bbias = nc.dram_tensor("bbias", [128, NCH], F32, kind="ExternalInput")
            b['c0_dram'] = nc.dram_tensor("c0", [3, NCH, 128, 32], F32, kind="ExternalInput")
            b['c1_dram'] = nc.dram_tensor("c1", [3, NCH, 128, 8], F32, kind="ExternalInput")
            b['mc0_dram'] = nc.dram_tensor("mc0", [3, NCH, 128, 32], U8, kind="ExternalInput")
            b['mc1_dram'] = nc.dram_tensor("mc1", [3, NCH, 128, 8], U8, kind="ExternalInput")
            b['mt_dram'] = nc.dram_tensor("mt", [NCH, 128, D], BF16, kind="ExternalInput")

        _alloc_common(nc, stack, b)
        b['y_sb'] = stack.enter_context(nc.sbuf_tensor("y_sb", [128, NCH * TT], BF16))
        b['wslab'] = [stack.enter_context(nc.sbuf_tensor(f"wslab{i}", [128, NCH * 128], BF16))
                      for i in range(2)]
        if layer == 0:
            b['rt_sb'] = stack.enter_context(nc.sbuf_tensor("rt_sb", [128, NCH * 8], BF16))
            b['ab_sb'] = stack.enter_context(nc.sbuf_tensor("ab_sb", [128, NCH * 128], BF16))
            b['e16'] = stack.enter_context(nc.sbuf_tensor("e16_sb", [8, 128], F32))
            b['ident'] = stack.enter_context(nc.sbuf_tensor("ident_sb", [128, 128], F32))
            for n, sh, dt in (('w32a', [128, 32], F32), ('w32b', [128, 32], F32),
                              ('w32p', [128, 32], F32), ('w4a', [128, 4], F32),
                              ('w4b', [128, 4], F32), ('wfm', [8, TT], F32),
                              ('wrep', [128, TT], F32), ('midw', [128, TT], BF16)):
                b[n] = stack.enter_context(nc.sbuf_tensor(n, sh, dt))
        else:
            b['enc_sb'] = stack.enter_context(nc.sbuf_tensor("enc_sb", [128, NCH * 40], BF16))
            b['bb2'] = stack.enter_context(nc.sbuf_tensor("bb2", [128, NCH], F32))
            b['e_sb'] = stack.enter_context(nc.sbuf_tensor("e_sb", [40, TT], BF16))
            b['g_sb'] = stack.enter_context(nc.sbuf_tensor("g_sb", [40, D], BF16))
            b['wcomb'] = stack.enter_context(nc.sbuf_tensor("wcomb", [128, NCH * 40], BF16))

        s = Sched(nc, stack)
        _emit_const_init(s, b)
        x_toks = _emit_x_load(s, b, x_in)
        dr = dict(bw=b['bw_dram'])
        if layer == 0:
            dr.update(rt=rt, la=la, lb=lb, e16=e16, ident=ident)
        else:
            dr.update(enc=enc, bbias=bbias, c0=b['c0_dram'], c1=b['c1_dram'],
                      mc0=b['mc0_dram'], mc1=b['mc1_dram'], mt=b['mt_dram'])
        last_ev = emit_ffn_body(s, b, dr, layer, x_toks)
        st_toks = _emit_x_store(s, b, x_out, waits=[last_ev])
        s.wait_only('sync', st_toks)
        s.finalize()
    return nc


def emit_ffn_body(s, b, dr, layer, first_toks):
    if layer == 0:
        small = [s.dma('gpsimd', b['rt_sb'][:, :], dr['rt'].ap(), waits=[s.last('pe')]),
                 s.dma('sync', b['e16'][:, :], dr['e16'].ap()),
                 s.dma('sync', b['ident'][:, :], dr['ident'].ap())]
        t_g = None
    else:
        small = [s.dma('gpsimd', b['enc_sb'][:, 0:NCH * 40], dr['enc'].ap(),
                       waits=[s.last('pe')]),
                 s.dma('sync', b['bb2'][:, :], dr['bbias'].ap())]
        t_g, _ = emit_gprep(s, b, dr)

    last_ev = None
    for tile in range(2):
        first = list(first_toks) + small + ([t_g] if layer == 1 else []) + \
            ([last_ev] if last_ev else [])
        t_y = emit_ln(s, b, tile, extra_first_wait=first)
        if layer == 0:
            la_tok = s.dma('gpsimd', b['ab_sb'][:, :], dr['la'].ap(),
                           waits=[s.last('pe')])
            tMw = emit_moe_head(s, b, tile, t_y, la_tok)
            lb_tok = s.dma('gpsimd', b['ab_sb'][:, :], dr['lb'].ap(),
                           waits=[s.last('pe')])
            last_ev, _ = emit_wsweep(s, b, tile, dr['bw'], 2, 0.5,
                                     extra_mm='l0inc', first_waits=[tMw, lb_tok])
        else:
            tEv = emit_l1_head(s, b, t_y)
            last_ev, _ = emit_wsweep(s, b, tile, dr['bw'], 2, 0.5,
                                     extra_mm='l1', first_waits=[tEv])
    return last_ev


def emit_rope_head(s, b, tile, t_proj, bankA, bankB, ct, st_, out_sb, war_dve=None):
    t_cast = s.emit('vector', lambda e: e.tensor_copy(b['qb'][:, :], bankA[:, :]),
                    waits=[t_proj] + ([war_dve] if war_dve else []), inc='dve')
    t_rot = s.emit('tensor', lambda e: e.matmul(
        bankB[:, :], b['perm_sb'][:, :], b['qb'][:, :], start=True, stop=True),
        waits=[t_cast], inc='pe')
    s.emit('vector', lambda e: e.tensor_tensor(
        out=b['rtmp1'][:, :], in0=b['qb'][:, :],
        in1=ct[:, tile * TT:tile * TT + TT], op=ALU.mult), inc='dve')
    s.emit('vector', lambda e: e.tensor_tensor(
        out=out_sb[:, :], in0=bankB[:, :],
        in1=st_[:, tile * TT:tile * TT + TT], op=ALU.mult),
        waits=[t_rot], inc='dve')
    t_out = s.emit('vector', lambda e: e.tensor_add(
        out_sb[:, :], b['rtmp1'][:, :], out_sb[:, :]), inc='dve')
    return t_out


def build_qkv_stage(layer):
    stack = contextlib.ExitStack()
    with stack:
        nc = bass.Bass("TRN2", target_bir_lowering=False, debug=False)
        b = {}
        x_in = nc.dram_tensor("x_in", [D, TOK], F32, kind="ExternalInput")
        wq = nc.dram_tensor("wq", [NCH, 128, NCH * 128], F32, kind="ExternalInput")
        wk = nc.dram_tensor("wk", [8, 128, NCH * 128], F32, kind="ExternalInput")
        wv = nc.dram_tensor("wv", [NCH, 128, 1024], F32, kind="ExternalInput")
        cq_d = nc.dram_tensor("cq", [128, TOK], F32, kind="ExternalInput")
        sq_d = nc.dram_tensor("sq", [128, TOK], F32, kind="ExternalInput")
        ck_d = nc.dram_tensor("ck", [128, TOK], F32, kind="ExternalInput")
        sk_d = nc.dram_tensor("sk", [128, TOK], F32, kind="ExternalInput")
        perm_d = nc.dram_tensor("perm", [128, 128], BF16, kind="ExternalInput")
        q_out = nc.dram_tensor("q_out", [D, TOK], BF16, kind="ExternalOutput")
        k_out = nc.dram_tensor("k_out", [1024, TOK], BF16, kind="ExternalOutput")
        v_out = nc.dram_tensor("v_out", [TOK, 1024], BF16, kind="ExternalOutput")
        kn_out = nc.dram_tensor("kn_out", [8, TOK], F32, kind="ExternalOutput")
        qm_out = nc.dram_tensor("qm_out", [1, 64], F32, kind="ExternalOutput")

        _alloc_common(nc, stack, b)
        b['y_sb'] = stack.enter_context(nc.sbuf_tensor("y_sb", [128, NCH * TT], BF16))
        b['wslab'] = [stack.enter_context(nc.sbuf_tensor(f"wslab{i}", [128, NCH * 128], BF16))
                      for i in range(2)]
        b['wv_sb'] = [stack.enter_context(nc.sbuf_tensor(f"wvsb{i}", [128, 1024], BF16))
                      for i in range(2)]
        b['ct_sb'] = stack.enter_context(nc.sbuf_tensor("ct_sb", [128, TOK], BF16))
        b['st_sb'] = stack.enter_context(nc.sbuf_tensor("st_sb", [128, TOK], BF16))
        b['perm_sb'] = stack.enter_context(nc.sbuf_tensor("perm_sb", [128, 128], BF16))
        b['qb'] = stack.enter_context(nc.sbuf_tensor("qb", [128, TT], BF16))
        b['rtmp1'] = stack.enter_context(nc.sbuf_tensor("rtmp1", [128, TT], F32))
        b['qrope'] = stack.enter_context(nc.sbuf_tensor("qrope", [128, TT], BF16))
        b['qsq'] = stack.enter_context(nc.sbuf_tensor("qsq", [128, TT], BF16))
        b['kn_row'] = stack.enter_context(nc.sbuf_tensor("kn_row", [1, TT], F32))
        b['qm_sb'] = stack.enter_context(nc.sbuf_tensor("qm_sb", [1, 64], F32))
        b['vtmp'] = [b['qb'], b['qrope']]

        s = Sched(nc, stack)
        _emit_const_init(s, b)
        x_toks = _emit_x_load(s, b, x_in)
        tbl = [s.dma('sync', b['perm_sb'][:, :], perm_d.ap())]
        t_qm_last = None
        last_y = None
        for tile in range(2):
            t_y = emit_ln(s, b, tile, extra_first_wait=x_toks + tbl +
                          ([last_y] if last_y else []))
            last_y = t_y
            for phase, nh, wdram, ctd, std in (
                    ('q', NCH, wq, cq_d, sq_d),
                    ('k', 8, wk, ck_d, sk_d)):
                ct, st_ = b['ct_sb'], b['st_sb']
                t_tb = [s.dma('gpsimd', b['ct_sb'][:, :], ctd.ap(),
                              waits=[s.last('dve')]),
                        s.dma('gpsimd', b['st_sb'][:, :], std.ap())]
                slab_tok = [None, None]
                for h in range(2):
                    slab_tok[h] = s.dma('gpsimd',
                                        b['wslab'][h].ap().rearrange("p (c f) -> p c f", f=128),
                                        wdram.ap()[h].rearrange("c p f -> p c f"),
                                        waits=[s.last('pe')])
                prev_tn = None
                for h in range(nh):
                    par = h % 2
                    bankA = b['pb'][par]
                    bankB = b['pb'][2 + par]
                    t_proj = None
                    for c in range(NCH):
                        waits = []
                        if c == 0:
                            waits = [slab_tok[par], t_y, s.last('dve')] + \
                                (t_tb if h < 2 else [])
                        t_proj = s.emit('tensor', lambda e, c=c, par=par, bankA=bankA: e.matmul(
                            bankA[:, :], b['wslab'][par][:, c * 128:(c + 1) * 128],
                            b['y_sb'][:, c * TT:(c + 1) * TT],
                            start=(c == 0), stop=(c == NCH - 1), skip_group_check=True),
                            waits=waits, inc='pe' if c == NCH - 1 else None)
                    if h + 2 < nh:
                        slab_tok[par] = s.dma('gpsimd',
                                              b['wslab'][par].ap().rearrange("p (c f) -> p c f", f=128),
                                              wdram.ap()[h + 2].rearrange("c p f -> p c f"),
                                              waits=[t_proj])
                    t_rope = emit_rope_head(s, b, tile, t_proj, bankA, bankB, ct, st_,
                                            b['qrope'], war_dve=s.last('dmah'))
                    if phase == 'q':
                        dst = q_out.ap()[h * 128:(h + 1) * 128, tile * TT:tile * TT + TT]
                    else:
                        dst = k_out.ap()[h * 128:(h + 1) * 128, tile * TT:tile * TT + TT]
                    s.dma('sync', dst, b['qrope'][:, :], waits=[t_rope])
                    t_sq = s.emit('scalar', lambda e: e.activation(
                        out=b['qsq'][:, :], in_=b['qrope'][:, :], func=AF.Square),
                        waits=[t_rope, prev_tn], inc='act')
                    t_n = s.emit('tensor', lambda e: e.matmul(
                        b['pb'][4][0:1, :], b['ones_col'][:, 0:1], b['qsq'][:, :],
                        start=True, stop=True, skip_group_check=True),
                        waits=[t_sq, s.last('dve')], inc='pe')
                    prev_tn = t_n
                    if phase == 'q':
                        slot = tile * 32 + h
                        t_qm_last = s.emit('vector', lambda e, slot=slot: e.reduce_max(
                            b['qm_sb'][:, slot:slot + 1], b['pb'][4][0:1, :], axis=AX.X),
                            waits=[t_n], inc='dve')
                    else:
                        t_kn = s.emit('vector', lambda e: e.tensor_copy(
                            b['kn_row'][:, :], b['pb'][4][0:1, :]),
                            waits=[t_n, s.last('dmah')], inc='dve')
                        s.dma('sync', kn_out.ap()[h:h + 1, tile * TT:tile * TT + TT],
                              b['kn_row'][:, :], waits=[t_kn])
            wv_tok = [None, None]
            for c in range(2):
                wv_tok[c] = s.dma('gpsimd', b['wv_sb'][c][:, :], wv.ap()[c],
                                  waits=[s.last('pe')])
            t_vmm = None
            for c in range(NCH):
                par = c % 2
                for sub in range(4):
                    for dv in range(2):
                        bank = b['pb'][sub * 2 + dv]
                        waits = []
                        if sub == 0 and dv == 0:
                            waits = [wv_tok[par]]
                            if c == 0:
                                waits.append(s.last('dve'))
                        t_vmm = s.emit('tensor', lambda e, c=c, par=par, sub=sub, dv=dv, bank=bank: e.matmul(
                            bank[:, :], b['y_sb'][:, c * TT + sub * 128:c * TT + sub * 128 + 128],
                            b['wv_sb'][par][:, dv * TT:(dv + 1) * TT],
                            start=(c == 0), stop=(c == NCH - 1), skip_group_check=True),
                            waits=waits,
                            inc='pe' if (sub == 3 and dv == 1) else None)
                if c + 2 < NCH:
                    wv_tok[par] = s.dma('gpsimd', b['wv_sb'][par][:, :], wv.ap()[c + 2],
                                        waits=[t_vmm])
            for sub in range(4):
                for dv in range(2):
                    bank = b['pb'][sub * 2 + dv]
                    vp = (sub * 2 + dv) % 2
                    t_ev = s.emit('vector', lambda e, bank=bank, vp=vp: e.tensor_copy(
                        b['vtmp'][vp][:, :], bank[:, :]),
                        waits=[t_vmm, s.last('dmah')], inc='dve')
                    s.dma('sync',
                          v_out.ap()[tile * TT + sub * 128: tile * TT + sub * 128 + 128,
                                     dv * TT:(dv + 1) * TT],
                          b['vtmp'][vp][:, :], waits=[t_ev])
        s.dma('sync', qm_out.ap()[:, :], b['qm_sb'][:, :], waits=[t_qm_last])
        s.wait_only('sync', [s.last('dmah'), s.last('dmas')])
        s.finalize()
    return nc


def build_att_stage(layer):
    stack = contextlib.ExitStack()
    with stack:
        nc = bass.Bass("TRN2", target_bir_lowering=False, debug=False)
        b = {}
        x_in = nc.dram_tensor("x_in", [D, TOK], F32, kind="ExternalInput")
        q_in = nc.dram_tensor("q_in", [D, TOK], BF16, kind="ExternalInput")
        kg = nc.dram_tensor("kg", [8, 128, 4096], BF16, kind="ExternalInput")
        vg = nc.dram_tensor("vg", [8, 128, 4096], BF16, kind="ExternalInput")
        kng = nc.dram_tensor("kng", [8, 4096], F32, kind="ExternalInput")
        qm_d = nc.dram_tensor("qm", [1, 64], F32, kind="ExternalInput")
        wo = nc.dram_tensor("wo", [NCH, 128, NCH * 128], F32, kind="ExternalInput")
        qpos_d = nc.dram_tensor("qpos", [1, TOK], F32, kind="ExternalInput")
        kpos_d = nc.dram_tensor("kpos", [128, 32], F32, kind="ExternalInput")
        ident_d = nc.dram_tensor("ident", [128, 128], F32, kind="ExternalInput")
        negrow_d = nc.dram_tensor("negrow", [1, 128], BF16, kind="ExternalInput")
        posrow_d = nc.dram_tensor("posrow", [1, 128], BF16, kind="ExternalInput")
        x_out = nc.dram_tensor("x_out", [D, TOK], F32, kind="ExternalOutput")

        _alloc_common(nc, stack, b, ln=False)
        b['y_sb'] = stack.enter_context(nc.sbuf_tensor("ctx_y", [128, NCH * TT], BF16))
        b['ctx_sb'] = b['y_sb']
        b['wslab'] = [stack.enter_context(nc.sbuf_tensor(f"wslab{i}", [128, NCH * 128], BF16))
                      for i in range(2)]
        b['ksl'] = b['wslab'][0]
        b['vsl'] = b['wslab'][1]
        b['q_sb'] = stack.enter_context(nc.sbuf_tensor("q_sb", [128, TT], BF16))
        b['exp_sb'] = [stack.enter_context(nc.sbuf_tensor(f"exp{i}", [128, TT], BF16))
                       for i in range(2)]
        b['qpos'] = stack.enter_context(nc.sbuf_tensor("qpos_sb", [1, TOK], F32))
        b['kpos'] = stack.enter_context(nc.sbuf_tensor("kpos_sb", [128, 32], F32))
        b['qpos_rep'] = stack.enter_context(nc.sbuf_tensor("qpos_rep", [128, TT], F32))
        b['ones_row_f32'] = stack.enter_context(nc.sbuf_tensor("ones_row_f32", [1, 128], F32))
        b['kn_sb'] = stack.enter_context(nc.sbuf_tensor("kn_sb", [8, TT], F32))
        b['mkacc'] = stack.enter_context(nc.sbuf_tensor("mkacc", [8, 8], F32))
        b['mk8'] = stack.enter_context(nc.sbuf_tensor("mk8", [8, 1], F32))
        b['mk2'] = stack.enter_context(nc.sbuf_tensor("mk2", [1, 8], F32))
        b['qm_sb'] = stack.enter_context(nc.sbuf_tensor("qm_sb", [1, 64], F32))
        b['negB'] = stack.enter_context(nc.sbuf_tensor("negB", [128, 1], F32))
        b['b2row'] = stack.enter_context(nc.sbuf_tensor("b2row", [1, 1], F32))
        b['b1row'] = stack.enter_context(nc.sbuf_tensor("b1row", [1, 1], F32))
        b['b1bf'] = stack.enter_context(nc.sbuf_tensor("b1bf", [1, 1], BF16))
        b['rcp'] = stack.enter_context(nc.sbuf_tensor("rcp", [1, TT], F32))
        b['rcpb'] = stack.enter_context(nc.sbuf_tensor("rcpb", [1, TT], BF16))
        b['rrep'] = stack.enter_context(nc.sbuf_tensor("rrep", [128, TT], F32))
        b['ident'] = stack.enter_context(nc.sbuf_tensor("ident_sb", [128, 128], F32))
        b['negrow'] = stack.enter_context(nc.sbuf_tensor("negrow_sb", [1, 128], BF16))
        b['posrow'] = stack.enter_context(nc.sbuf_tensor("posrow_sb", [1, 128], BF16))

        s = Sched(nc, stack)
        _emit_const_init(s, b)
        x_toks = _emit_x_load(s, b, x_in)
        pre = [s.dma('sync', b['qpos'][:, :], qpos_d.ap()),
               s.dma('sync', b['kpos'][:, :], kpos_d.ap()),
               s.dma('sync', b['qm_sb'][:, :], qm_d.ap()),
               s.dma('sync', b['ident'][:, :], ident_d.ap()),
               s.dma('sync', b['negrow'][:, :], negrow_d.ap()),
               s.dma('sync', b['posrow'][:, :], posrow_d.ap())]
        t_mk = None
        for i in range(8):
            t_l = s.dma('sync', b['kn_sb'][:, :],
                        kng.ap()[:, i * 512:(i + 1) * 512], waits=[t_mk])
            t_mk = s.emit('vector', lambda e, i=i: e.reduce_max(
                b['mkacc'][:, i:i + 1], b['kn_sb'][:, :], axis=AX.X),
                waits=[t_l], inc='dve')
        t_mk = s.emit('vector', lambda e: e.reduce_max(
            b['mk8'][:, :], b['mkacc'][:, :], axis=AX.X), waits=[t_mk], inc='dve')
        t_tr = s.emit('tensor', lambda e: e.transpose(
            b['pb'][6][0:1, 0:8], b['mk8'][:, 0:1], b['ident'][0:8, 0:8]),
            waits=[t_mk, pre[3]], inc='pe')
        t_mk2 = s.emit('vector', lambda e: e.tensor_copy(
            b['mk2'][:, :], b['pb'][6][0:1, 0:8]), waits=[t_tr], inc='dve')

        last_store_gate = None
        for tile in range(2):
            nkt = 16 if tile == 0 else 32
            t_qr = s.emit('tensor', lambda e, tile=tile: e.matmul(
                b['pb'][6][:, :], b['ones_row_f32'][0:1, :],
                b['qpos'][0:1, tile * TT:tile * TT + TT],
                start=True, stop=True, skip_group_check=True),
                waits=[pre[0], s.last('dve')], inc='pe')
            t_qrep = s.emit('vector', lambda e: e.tensor_copy(
                b['qpos_rep'][:, :], b['pb'][6][:, :]), waits=[t_qr], inc='dve')
            t_ctx_all = None
            t_kv_prev = None
            for kvh in range(8):
                t_k = s.dma('sync', b['ksl'][:, 0:nkt * 128],
                            kg.ap()[kvh][:, 0:nkt * 128], waits=[s.last('pe')])
                t_v = s.dma('sync', b['vsl'][:, 0:nkt * 128],
                            vg.ap()[kvh][:, 0:nkt * 128])
                for qh in range(4):
                    h = kvh * 4 + qh
                    t_q = s.dma('sync', b['q_sb'][:, :],
                                q_in.ap()[h * 128:(h + 1) * 128, tile * TT:tile * TT + TT],
                                waits=[s.last('pe')])
                    t_b2 = s.emit('vector', lambda e, tile=tile, h=h, kvh=kvh: e.tensor_tensor(
                        out=b['b2row'][:, :], in0=b['qm_sb'][:, tile * 32 + h:tile * 32 + h + 1],
                        in1=b['mk2'][:, kvh:kvh + 1], op=ALU.mult),
                        waits=[t_mk2, pre[2], s.last('act')], inc='dve')
                    t_b1 = s.emit('scalar', lambda e: e.activation(
                        out=b['b1row'][:, :], in_=b['b2row'][:, :], func=AF.Sqrt),
                        waits=[t_b2], inc='act')
                    t_bb = s.emit('vector', lambda e: e.tensor_copy(
                        b['b1bf'][:, :], b['b1row'][:, :]),
                        waits=[t_b1, s.last('pe')], inc='dve')
                    t_bn = s.emit('tensor', lambda e: e.matmul(
                        b['pb'][4][:, 0:1], b['negrow'][0:1, :], b['b1bf'][:, :],
                        start=True, stop=True, skip_group_check=True),
                        waits=[t_bb, pre[4]], inc='pe')
                    t_nb = s.emit('vector', lambda e: e.tensor_copy(
                        b['negB'][:, :], b['pb'][4][:, 0:1]),
                        waits=[t_bn, s.last('act')], inc='dve')
                    t_exp = [None, None]
                    t_msk = [None, None]
                    t_den = [None, None]
                    for kt in range(nkt):
                        par = kt % 2
                        t_sT = s.emit('tensor', lambda e, kt=kt, par=par: e.matmul(
                            b['pb'][par][:, :], b['ksl'][:, kt * 128:(kt + 1) * 128],
                            b['q_sb'][:, :], start=True, stop=True, skip_group_check=True),
                            waits=[t_exp[par], t_q] + ([t_k, t_v, s.last('dve')] if kt == 0 else []),
                            inc='pe')
                        t_exp[par] = s.emit('scalar', lambda e, par=par: e.activation(
                            out=b['exp_sb'][par][:, :], in_=b['pb'][par][:, :], func=AF.Exp,
                            bias=b['negB'][:, 0:1]),
                            waits=[t_sT, t_nb, t_den[par], t_msk[par]], inc='act')
                        t_msk[par] = s.emit('vector', lambda e, par=par, kt=kt: e.scalar_tensor_tensor(
                            out=b['exp_sb'][par][:, :], in0=b['qpos_rep'][:, :],
                            scalar=b['kpos'][:, kt:kt + 1], in1=b['exp_sb'][par][:, :],
                            op0=ALU.is_ge, op1=ALU.mult),
                            waits=[t_exp[par], t_qrep, pre[1]], inc='dve')
                        s.emit('tensor', lambda e, kt=kt, par=par, nkt=nkt: e.matmul(
                            b['pb'][2][:, :], b['vsl'][:, kt * 128:(kt + 1) * 128],
                            b['exp_sb'][par][:, :], start=(kt == 0), stop=(kt == nkt - 1),
                            skip_group_check=True),
                            waits=[t_msk[par]], inc=None)
                        t_den[par] = s.emit('tensor', lambda e, kt=kt, par=par, nkt=nkt: e.matmul(
                            b['pb'][3][0:1, :], b['ones_col'][:, 0:1],
                            b['exp_sb'][par][:, :], start=(kt == 0), stop=(kt == nkt - 1),
                            skip_group_check=True), inc='pe')
                    t_den_last = t_den[(nkt - 1) % 2]
                    t_rc = s.emit('vector', lambda e: e.reciprocal(
                        b['rcp'][:, :], b['pb'][3][0:1, :]), waits=[t_den_last], inc='dve')
                    t_rcb = s.emit('vector', lambda e: e.tensor_copy(
                        b['rcpb'][:, :], b['rcp'][:, :]), waits=[t_rc], inc='dve')
                    t_rr = s.emit('tensor', lambda e: e.matmul(
                        b['pb'][5][:, :], b['posrow'][0:1, :], b['rcpb'][:, :],
                        start=True, stop=True, skip_group_check=True),
                        waits=[t_rcb, pre[5]], inc='pe')
                    t_re = s.emit('vector', lambda e: e.tensor_copy(
                        b['rrep'][:, :], b['pb'][5][:, :]), waits=[t_rr], inc='dve')
                    t_ctx_all = s.emit('vector', lambda e, h=h: e.tensor_tensor(
                        out=b['ctx_sb'][:, h * TT:(h + 1) * TT], in0=b['pb'][2][:, :],
                        in1=b['rrep'][:, :], op=ALU.mult),
                        waits=[t_re], inc='dve')
            last_ev, _ = emit_wsweep(s, b, tile, wo, 2, 1.0, extra_mm=None,
                                     first_waits=[t_ctx_all], rhs='y')
            last_store_gate = last_ev
        st_toks = _emit_x_store(s, b, x_out, waits=[last_store_gate])
        s.wait_only('sync', st_toks)
        s.finalize()
    return nc


def _prep_shared(inp):
    p = {}
    p['bw0'] = _tile_w(np.ascontiguousarray(inp['L0_ffn_base_W'].T))
    p['rt0'] = np.ascontiguousarray(np.asarray(inp['L0_ffn_router_W']).T.reshape(NCH, 128, 8).transpose(1, 0, 2)).reshape(128, NCH * 8)
    p['la0'] = np.ascontiguousarray(
        np.asarray(inp['L0_ffn_lora_A']).transpose(1, 0, 2).reshape(D, 128)
        .reshape(NCH, 128, 128).transpose(1, 0, 2)).reshape(128, NCH * 128)
    p['lb0'] = np.ascontiguousarray(np.asarray(inp['L0_ffn_lora_B']).reshape(128, D))
    p['wq0'] = _tile_w(inp['L0_attn_Wq'])
    p['wk0'] = _tile_w(inp['L0_attn_Wk'])
    p['wv0'] = np.ascontiguousarray(inp['L0_attn_Wv']).reshape(NCH, 128, 1024)
    p['wo0'] = _tile_w(inp['L0_attn_Wo'])
    p['bw1'] = _tile_w(np.ascontiguousarray(inp['L1_ffn_W_base'].T))
    p['enc1'] = np.ascontiguousarray(
        np.concatenate([np.asarray(inp['L1_ffn_enc0']).T,
                        np.asarray(inp['L1_ffn_enc1']).T], axis=1)
        .reshape(NCH, 128, 40).transpose(1, 0, 2)).reshape(128, NCH * 40)
    p['bb1'] = np.ascontiguousarray(np.asarray(inp['L1_ffn_b_base']).reshape(NCH, 128).T)
    p['c0'] = np.ascontiguousarray(inp['L1_ffn_coefs0']).reshape(3, NCH, 128, 32)
    p['c1'] = np.ascontiguousarray(inp['L1_ffn_coefs1']).reshape(3, NCH, 128, 8)
    p['mc0'] = np.ascontiguousarray(inp['L1_ffn_mask0']).view(np.uint8).reshape(3, NCH, 128, 32)
    p['mc1'] = np.ascontiguousarray(inp['L1_ffn_mask1']).view(np.uint8).reshape(3, NCH, 128, 8)
    p['wq1'] = _tile_w(inp['L1_attn_Wq'])
    p['wk1'] = _tile_w(inp['L1_attn_Wk'])
    p['wv1'] = np.ascontiguousarray(inp['L1_attn_Wv']).reshape(NCH, 128, 1024)
    p['wo1'] = _tile_w(inp['L1_attn_Wo'])
    p['mt'] = _cached('mt', _mt_bf16).reshape(NCH, 128, D)
    p['e16'] = _cached('e16', _e16)
    p['ident'] = _cached('ident', lambda: np.eye(128, dtype=np.float32))
    p['perm'] = _cached('perm', _perm_rot)
    p['kpos'] = _cached('kpos', lambda: (
        np.arange(32, dtype=np.float32)[None, :] * 128 +
        np.arange(128, dtype=np.float32)[:, None]).astype(np.float32))
    p['negrow'] = _cached('negrow', lambda: np.full((1, 128), -1.02, dtype=np.float32).astype(BF))
    p['posrow'] = _cached('posrow', lambda: np.ones((1, 128), dtype=np.float32).astype(BF))
    return p


def _prep_cores():
    out = []
    scale = 1.0 / math.sqrt(128.0)
    for c in range(NCORES):
        pos = _core_positions(c)
        c0q, s0q = _cached(('r0', c), lambda pos=pos: _rope_l0(pos))
        c1q, s1q = _cached(('r1', c), lambda pos=pos: _rope_l1(pos))
        out.append(dict(
            cq0=np.ascontiguousarray(c0q * scale), sq0=np.ascontiguousarray(s0q * scale),
            ck0=c0q, sk0=s0q,
            cq1=np.ascontiguousarray(c1q * scale), sq1=np.ascontiguousarray(s1q * scale),
            ck1=c1q, sk1=s1q,
            qpos=np.ascontiguousarray(pos.astype(np.float32)[None, :])))
    return out


def _gather_kv(res, cores=range(NCORES)):
    Kg = [np.zeros((1024, 4096), dtype=BF) for _ in range(2)]
    Vg = [np.zeros((4096, 1024), dtype=BF) for _ in range(2)]
    KNg = [np.zeros((8, 4096), dtype=np.float32) for _ in range(2)]
    for c in cores:
        bt = c // 4
        m, hch = _core_chunks(c)
        k = res[c]['k_out']; v = res[c]['v_out']; kn = res[c]['kn_out']
        Kg[bt][:, 512 * m:512 * m + 512] = k[:, 0:512]
        Kg[bt][:, 512 * hch:512 * hch + 512] = k[:, 512:1024]
        Vg[bt][512 * m:512 * m + 512, :] = v[0:512, :]
        Vg[bt][512 * hch:512 * hch + 512, :] = v[512:1024, :]
        KNg[bt][:, 512 * m:512 * m + 512] = kn[:, 0:512]
        KNg[bt][:, 512 * hch:512 * hch + 512] = kn[:, 512:1024]
    kg_f = [np.ascontiguousarray(Kg[bt].reshape(8, 128, 4096)) for bt in range(2)]
    vg_f = [np.ascontiguousarray(
        Vg[bt].reshape(32, 128, 8, 128).transpose(2, 1, 0, 3).reshape(8, 128, 4096))
        for bt in range(2)]
    return kg_f, vg_f, KNg


_NC_CACHE = {}

def _get_nc(kind, layer):
    key = (kind, layer)
    if key not in _NC_CACHE:
        if kind == 'ffn':
            _NC_CACHE[key] = build_ffn_stage(layer)
        elif kind == 'qkv':
            _NC_CACHE[key] = build_qkv_stage(layer)
        else:
            _NC_CACHE[key] = build_att_stage(layer)
    return _NC_CACHE[key]


def _run(nc, in_maps):
    res = run_bass_kernel_spmd(nc, in_maps, core_ids=list(range(NCORES)))
    return res.results


def kernel_unfused(**inp):
    p = _prep_shared(inp)
    pc = _prep_cores()
    X = np.asarray(inp['L0_attn_X'])
    xs = [np.ascontiguousarray(X[c // 4, _core_positions(c), :].T.astype(np.float32))
          for c in range(NCORES)]

    def ffn_maps(layer, xs_cur):
        maps = []
        for c in range(NCORES):
            m = {'x_in': xs_cur[c], 'bw': p['bw0'] if layer == 0 else p['bw1']}
            if layer == 0:
                m.update(rt=p['rt0'], la=p['la0'], lb=p['lb0'], e16=p['e16'],
                         ident=p['ident'])
            else:
                m.update(enc=p['enc1'], bbias=p['bb1'], c0=p['c0'], c1=p['c1'],
                         mc0=p['mc0'], mc1=p['mc1'], mt=p['mt'])
            maps.append(m)
        return maps

    def qkv_maps(layer, xs_cur):
        maps = []
        for c in range(NCORES):
            t = pc[c]
            if layer == 0:
                m = dict(x_in=xs_cur[c], wq=p['wq0'], wk=p['wk0'], wv=p['wv0'],
                         cq=t['cq0'], sq=t['sq0'], ck=t['ck0'], sk=t['sk0'],
                         perm=p['perm'])
            else:
                m = dict(x_in=xs_cur[c], wq=p['wq1'], wk=p['wk1'], wv=p['wv1'],
                         cq=t['cq1'], sq=t['sq1'], ck=t['ck1'], sk=t['sk1'],
                         perm=p['perm'])
            maps.append(m)
        return maps

    def att_maps(layer, xs_cur, qk):
        kg_f, vg_f, KNg = qk['gather']
        maps = []
        for c in range(NCORES):
            bt = c // 4
            maps.append(dict(
                x_in=xs_cur[c], q_in=qk['res'][c]['q_out'], kg=kg_f[bt], vg=vg_f[bt],
                kng=KNg[bt], qm=qk['res'][c]['qm_out'],
                wo=p['wo0'] if layer == 0 else p['wo1'],
                qpos=pc[c]['qpos'], kpos=p['kpos'], ident=p['ident'],
                negrow=p['negrow'], posrow=p['posrow']))
        return maps

    cur = xs
    for layer in (0, 1):
        res = _run(_get_nc('ffn', layer), ffn_maps(layer, cur))
        cur = [res[c]['x_out'] for c in range(NCORES)]
        qres = _run(_get_nc('qkv', layer), qkv_maps(layer, cur))
        qk = {'res': qres, 'gather': _gather_kv(qres)}
        res = _run(_get_nc('att', layer), att_maps(layer, cur, qk))
        cur = [res[c]['x_out'] for c in range(NCORES)]
        res = _run(_get_nc('ffn', layer), ffn_maps(layer, cur))
        cur = [res[c]['x_out'] for c in range(NCORES)]

    out = np.empty((2, 4096, D), dtype=np.float32)
    for c in range(NCORES):
        out[c // 4, _core_positions(c), :] = cur[c].T
    return out


def _carve(nc, base, cursor, name, shape, dtype):
    nbytes = int(np.prod(shape[1:])) * (2 if dtype in (BF16,) else (1 if dtype == U8 else 4))
    off = (cursor + 31) & ~31
    h = nc.alloc_sbuf_tensor_at(name, shape, dtype, offset=base + off)
    return h, off + nbytes


def build_fused():
    stack = contextlib.ExitStack()
    with stack:
        nc = bass.Bass("TRN2", target_bir_lowering=False, debug=False)
        b = {}
        x_in = nc.dram_tensor("x_in", [D, TOK], F32, kind="ExternalInput")
        x_out = nc.dram_tensor("x_out", [D, TOK], F32, kind="ExternalOutput")
        dri = {}
        for l in (0, 1):
            dri[('bw', l)] = nc.dram_tensor(f"bw{l}", [NCH, 128, NCH * 128], F32, kind="ExternalInput")
            dri[('wq', l)] = nc.dram_tensor(f"wq{l}", [NCH, 128, NCH * 128], F32, kind="ExternalInput")
            dri[('wk', l)] = nc.dram_tensor(f"wk{l}", [8, 128, NCH * 128], F32, kind="ExternalInput")
            dri[('wv', l)] = nc.dram_tensor(f"wv{l}", [NCH, 128, 1024], F32, kind="ExternalInput")
            dri[('wo', l)] = nc.dram_tensor(f"wo{l}", [NCH, 128, NCH * 128], F32, kind="ExternalInput")
            for t in ('cq', 'sq', 'ck', 'sk'):
                dri[(t, l)] = nc.dram_tensor(f"{t}{l}", [128, TOK], F32, kind="ExternalInput")
        rt_d = nc.dram_tensor("rt", [128, NCH * 8], F32, kind="ExternalInput")
        la_d = nc.dram_tensor("la", [128, NCH * 128], F32, kind="ExternalInput")
        lb_d = nc.dram_tensor("lb", [128, NCH * 128], F32, kind="ExternalInput")
        enc_d = nc.dram_tensor("enc", [128, NCH * 40], F32, kind="ExternalInput")
        bbias_d = nc.dram_tensor("bbias", [128, NCH], F32, kind="ExternalInput")
        c0_d = nc.dram_tensor("c0", [3, NCH, 128, 32], F32, kind="ExternalInput")
        c1_d = nc.dram_tensor("c1", [3, NCH, 128, 8], F32, kind="ExternalInput")
        mc0_d = nc.dram_tensor("mc0", [3, NCH, 128, 32], U8, kind="ExternalInput")
        mc1_d = nc.dram_tensor("mc1", [3, NCH, 128, 8], U8, kind="ExternalInput")
        mt_d = nc.dram_tensor("mt", [NCH, 128, D], BF16, kind="ExternalInput")
        e16_d = nc.dram_tensor("e16", [8, 128], F32, kind="ExternalInput")
        ident_d = nc.dram_tensor("ident", [128, 128], F32, kind="ExternalInput")
        perm_d = nc.dram_tensor("perm", [128, 128], BF16, kind="ExternalInput")
        kpos_d = nc.dram_tensor("kpos", [128, 32], F32, kind="ExternalInput")
        qpos_d = nc.dram_tensor("qpos", [1, TOK], F32, kind="ExternalInput")
        negrow_d = nc.dram_tensor("negrow", [1, 128], BF16, kind="ExternalInput")
        posrow_d = nc.dram_tensor("posrow", [1, 128], BF16, kind="ExternalInput")
        q_scr = nc.dram_tensor("q_scr", [D, TOK], BF16, kind="Internal")
        kv_stg = nc.dram_tensor("kv_stg", [2056, TOK], BF16, kind="Internal")
        kv_gth = nc.dram_tensor("kv_gth", [4 * 2056, TOK], BF16, kind="Internal")

        _alloc_common(nc, stack, b, ctx=False, pb_n=0)
        pb01 = stack.enter_context(nc.psum_tensor("pb01", [128, 2 * TT], F32))
        pb25 = [stack.enter_context(nc.psum_tensor(f"pbm{i}", [128, TT], F32))
                for i in range(4)]
        pb67 = stack.enter_context(nc.psum_tensor("pb67", [128, 2 * TT], F32))
        b['pb'] = [pb01.ap()[:, 0:TT], pb01.ap()[:, TT:2 * TT],
                   pb25[0].ap(), pb25[1].ap(), pb25[2].ap(), pb25[3].ap(),
                   pb67.ap()[:, 0:TT], pb67.ap()[:, TT:2 * TT]]
        b['pbpair'] = [pb01.ap(), pb67.ap()]
        b['exp_pair'] = [None, None]
        b['y_sb'] = nc.alloc_sbuf_tensor("y_sb", [128, NCH * TT], BF16)
        b['ctx_sb'] = b['y_sb']
        b['wslab'] = [nc.alloc_sbuf_tensor(f"wslab{i}", [128, NCH * 128], BF16)
                      for i in range(2)]
        b['ksl'] = b['wslab'][0]
        b['vsl'] = b['wslab'][1]
        b['rt_sb'] = nc.alloc_sbuf_tensor("rt_sb", [128, NCH * 8], BF16)
        b['e16'] = nc.alloc_sbuf_tensor("e16_sb", [8, 128], F32)
        b['ident'] = nc.alloc_sbuf_tensor("ident_sb", [128, 128], F32)
        b['perm_sb'] = nc.alloc_sbuf_tensor("perm_sb", [128, 128], BF16)
        b['bb2'] = nc.alloc_sbuf_tensor("bb2", [128, NCH], F32)
        b['kpos'] = nc.alloc_sbuf_tensor("kpos_sb", [128, 32], F32)
        b['qm_sb'] = nc.alloc_sbuf_tensor("qm_sb", [1, 64], F32)
        b['negrow'] = nc.alloc_sbuf_tensor("negrow_sb", [1, 128], BF16)
        b['posrow'] = nc.alloc_sbuf_tensor("posrow_sb", [1, 128], BF16)

        szR = 15744
        baseR = nc.bump_sbuf(szR)[0]
        cur = 0
        b['e_sb'], cur = _carve(nc, baseR, cur, "e_sb", [40, TT], BF16)
        b['wcomb'], cur = _carve(nc, baseR, cur, "wcomb", [128, NCH * 40], BF16)
        b['g_sb'], cur = _carve(nc, baseR, cur, "g_sb", [40, D], BF16)
        b['enc_sb'], cur = _carve(nc, baseR, cur, "enc_sb", [128, NCH * 40], BF16)
        assert cur <= szR, cur
        cur = 0
        b['w32a'], cur = _carve(nc, baseR, cur, "w32a", [128, 32], F32)
        b['w32b'], cur = _carve(nc, baseR, cur, "w32b", [128, 32], F32)
        b['w32p'], cur = _carve(nc, baseR, cur, "w32p", [128, 32], F32)
        b['w4a'], cur = _carve(nc, baseR, cur, "w4a", [128, 4], F32)
        b['w4b'], cur = _carve(nc, baseR, cur, "w4b", [128, 4], F32)
        b['wfm'], cur = _carve(nc, baseR, cur, "wfm", [8, TT], F32)
        b['wrep'], cur = _carve(nc, baseR, cur, "wrep", [128, TT], F32)
        b['midw'], cur = _carve(nc, baseR, cur, "midw", [128, TT], BF16)
        b['ab_sb'], cur = _carve(nc, baseR, cur, "ab_sb", [128, NCH * 128], BF16)
        assert cur <= szR, cur
        cur = 0
        b['ct_sb'], cur = _carve(nc, baseR, cur, "ct_sb", [128, TOK], BF16)
        b['st_sb'], cur = _carve(nc, baseR, cur, "st_sb", [128, TOK], BF16)
        b['qb'], cur = _carve(nc, baseR, cur, "qb", [128, TT], BF16)
        b['rtmp1'], cur = _carve(nc, baseR, cur, "rtmp1", [128, TT], F32)
        b['qrope'], cur = _carve(nc, baseR, cur, "qrope", [128, TT], BF16)
        b['qsq'], cur = _carve(nc, baseR, cur, "qsq", [128, TT], BF16)
        b['kn_row'], cur = _carve(nc, baseR, cur, "kn_row", [1, TT], F32)
        assert cur <= szR, cur
        b['wv_sb'] = [b['ct_sb'], b['st_sb']]
        b['vtmp'] = [b['qb'], b['qrope']]
        cur = 0
        b['q_sb2'] = [None, None]
        b['q_sb2'][0], cur = _carve(nc, baseR, cur, "q_sb0", [128, TT], BF16)
        b['q_sb2'][1], cur = _carve(nc, baseR, cur, "q_sb1", [128, TT], BF16)
        b['exp_sb'] = [None] * 4
        e0_off = (cur + 31) & ~31
        b['exp_sb'][0], cur = _carve(nc, baseR, cur, "exp0", [128, TT], BF16)
        b['exp_sb'][1], cur = _carve(nc, baseR, cur, "exp1", [128, TT], BF16)
        b['qpos_rep'], cur = _carve(nc, baseR, cur, "qpos_rep", [128, TT], F32)
        b['qpos'], cur = _carve(nc, baseR, cur, "qpos_sb", [1, TT], F32)
        knb_off = cur
        e2_off = (cur + 31) & ~31
        b['exp_sb'][2], cur = _carve(nc, baseR, cur, "exp2", [128, TT], BF16)
        b['exp_sb'][3], cur = _carve(nc, baseR, cur, "exp3", [128, TT], BF16)
        b['exp_pairsb'] = [
            nc.alloc_sbuf_tensor_at("expp0", [128, 2 * TT], BF16, offset=baseR + e0_off),
            nc.alloc_sbuf_tensor_at("expp1", [128, 2 * TT], BF16, offset=baseR + e2_off)]
        b['knb'] = nc.alloc_sbuf_tensor_at("knb", [8, TOK], BF16, offset=baseR + knb_off)
        b['mkacc'], cur = _carve(nc, baseR, cur, "mkacc", [8, 8], F32)
        b['mk8'], cur = _carve(nc, baseR, cur, "mk8", [8, 1], F32)
        b['mk2'], cur = _carve(nc, baseR, cur, "mk2", [1, 8], F32)
        b['negB2'] = [None, None]
        b['negB2'][0], cur = _carve(nc, baseR, cur, "negB0", [128, 1], F32)
        b['negB2'][1], cur = _carve(nc, baseR, cur, "negB1", [128, 1], F32)
        b['b2row2'] = [None, None]
        b['b2row2'][0], cur = _carve(nc, baseR, cur, "b2row0", [1, 1], F32)
        b['b2row2'][1], cur = _carve(nc, baseR, cur, "b2row1", [1, 1], F32)
        b['b1row2'] = [None, None]
        b['b1row2'][0], cur = _carve(nc, baseR, cur, "b1row0", [1, 1], F32)
        b['b1row2'][1], cur = _carve(nc, baseR, cur, "b1row1", [1, 1], F32)
        b['b1bf2'] = [None, None]
        b['b1bf2'][0], cur = _carve(nc, baseR, cur, "b1bf0", [1, 1], BF16)
        b['b1bf2'][1], cur = _carve(nc, baseR, cur, "b1bf1", [1, 1], BF16)
        b['rcp'], cur = _carve(nc, baseR, cur, "rcp", [1, TT], F32)
        b['rcpb'], cur = _carve(nc, baseR, cur, "rcpb", [1, TT], BF16)
        b['rrep'], cur = _carve(nc, baseR, cur, "rrep", [128, TT], F32)
        assert cur <= szR, cur

        s = Sched(nc, stack)
        _emit_const_init(s, b)
        x_toks = _emit_x_load(s, b, x_in)
        consts = [s.dma('sync', b['perm_sb'][:, :], perm_d.ap()),
                  s.dma('sync', b['kpos'][:, :], kpos_d.ap()),
                  s.dma('sync', b['negrow'][:, :], negrow_d.ap()),
                  s.dma('sync', b['posrow'][:, :], posrow_d.ap())]

        for layer in (0, 1):
            dr = dict(bw=dri[('bw', layer)])
            if layer == 0:
                dr.update(rt=rt_d, la=la_d, lb=lb_d, e16=e16_d, ident=ident_d)
            else:
                dr.update(enc=enc_d, bbias=bbias_d, c0=c0_d, c1=c1_d,
                          mc0=mc0_d, mc1=mc1_d, mt=mt_d)
            emit_ffn_body(s, b, dr, layer, x_toks if layer == 0 else [])
            s.fence()
            t_qm = emit_qkv_fused(s, b, dri, layer, q_scr, kv_stg, consts)
            s.wait_only('gpsimd', [s.last('dmah'), s.last('dmas')])
            t_cc = s.emit('gpsimd', lambda e: e.collective_compute(
                "AllGather", ALU.bypass,
                replica_groups=[[0, 1, 2, 3], [4, 5, 6, 7]],
                ins=[kv_stg.ap().opt()], outs=[kv_gth.ap().opt()]), inc='cc', k=1)
            s.fence()
            emit_att_fused(s, b, dri, layer, q_scr, kv_gth, qpos_d, t_cc, t_qm, consts)
            s.fence()
            emit_ffn_body(s, b, dr, layer, [])
            s.fence()
        st_toks = _emit_x_store(s, b, x_out, waits=[s.last('dve')])
        s.wait_only('sync', st_toks)
        s.finalize()
    return nc


def emit_qkv_fused(s, b, dri, layer, q_scr, kv_stg, consts):
    wq, wk, wv = dri[('wq', layer)], dri[('wk', layer)], dri[('wv', layer)]
    t_qm_last = None
    last_y = None
    for tile in range(2):
        t_y = emit_ln(s, b, tile, extra_first_wait=consts +
                      ([last_y] if last_y else []))
        last_y = t_y
        for phase, nh, wdram, ctd, std in (
                ('q', NCH, wq, dri[('cq', layer)], dri[('sq', layer)]),
                ('k', 8, wk, dri[('ck', layer)], dri[('sk', layer)])):
            ct, st_ = b['ct_sb'], b['st_sb']
            t_tb = [s.dma('gpsimd', b['ct_sb'][:, :], ctd.ap(), waits=[s.last('dve')]),
                    s.dma('gpsimd', b['st_sb'][:, :], std.ap())]
            slab_tok = [None, None]
            for h in range(2):
                slab_tok[h] = s.dma('gpsimd', b['wslab'][h][:, :], wdram.ap()[h],
                                    waits=[s.last('pe')])
            prev_tn = None
            for h in range(nh):
                par = h % 2
                bankA = b['pb'][par]
                bankB = b['pb'][2 + par]
                t_proj = None
                for c in range(NCH):
                    waits = []
                    if c == 0:
                        waits = [slab_tok[par], t_y, s.last('dve')] + \
                            (t_tb if h < 2 else [])
                    t_proj = s.emit('tensor', lambda e, c=c, par=par, bankA=bankA: e.matmul(
                        bankA[:, :], b['wslab'][par][:, c * 128:(c + 1) * 128],
                        b['y_sb'][:, c * TT:(c + 1) * TT],
                        start=(c == 0), stop=(c == NCH - 1), skip_group_check=True),
                        waits=waits, inc='pe' if c == NCH - 1 else None)
                if h + 2 < nh:
                    slab_tok[par] = s.dma('gpsimd', b['wslab'][par][:, :],
                                          wdram.ap()[h + 2], waits=[t_proj])
                t_rope = emit_rope_head(s, b, tile, t_proj, bankA, bankB, ct, st_,
                                        b['qrope'], war_dve=s.last('dmah'))
                if phase == 'q':
                    dst = q_scr.ap()[h * 128:(h + 1) * 128, tile * TT:tile * TT + TT]
                else:
                    dst = kv_stg.ap()[h * 128:(h + 1) * 128, tile * TT:tile * TT + TT]
                s.dma('sync', dst, b['qrope'][:, :], waits=[t_rope])
                t_sq = s.emit('scalar', lambda e: e.activation(
                    out=b['qsq'][:, :], in_=b['qrope'][:, :], func=AF.Square),
                    waits=[t_rope, prev_tn], inc='act')
                t_n = s.emit('tensor', lambda e: e.matmul(
                    b['pb'][4][0:1, :], b['ones_col'][:, 0:1], b['qsq'][:, :],
                    start=True, stop=True, skip_group_check=True),
                    waits=[t_sq, s.last('dve')], inc='pe')
                prev_tn = t_n
                if phase == 'q':
                    slot = tile * 32 + h
                    t_qm_last = s.emit('vector', lambda e, slot=slot: e.reduce_max(
                        b['qm_sb'][:, slot:slot + 1], b['pb'][4][0:1, :], axis=AX.X),
                        waits=[t_n], inc='dve')
                else:
                    t_kn = s.emit('vector', lambda e: e.tensor_copy(
                        b['kn_row'][:, :], b['pb'][4][0:1, :]),
                        waits=[t_n, s.last('dmas')], inc='dve')
                    s.dma('gpsimd', kv_stg.ap()[2048 + h:2049 + h, tile * TT:tile * TT + TT],
                          b['kn_row'][:, :], waits=[t_kn])
        wv_tok = [None, None]
        for c in range(2):
            wv_tok[c] = s.dma('gpsimd', b['wv_sb'][c][:, :], wv.ap()[c],
                              waits=[s.last('pe'), s.last('dve')])
        t_vmm = None
        for c in range(NCH):
            par = c % 2
            for sub in range(4):
                for dv in range(2):
                    bank = b['pb'][sub * 2 + dv]
                    waits = []
                    if sub == 0 and dv == 0:
                        waits = [wv_tok[par]]
                        if c == 0:
                            waits.append(s.last('dve'))
                    t_vmm = s.emit('tensor', lambda e, c=c, par=par, sub=sub, dv=dv, bank=bank: e.matmul(
                        bank[:, :], b['y_sb'][:, c * TT + sub * 128:c * TT + sub * 128 + 128],
                        b['wv_sb'][par][:, dv * TT:(dv + 1) * TT],
                        start=(c == 0), stop=(c == NCH - 1), skip_group_check=True),
                        waits=waits, inc='pe' if (sub == 3 and dv == 1) else None)
            if c + 2 < NCH:
                wv_tok[par] = s.dma('gpsimd', b['wv_sb'][par][:, :], wv.ap()[c + 2],
                                    waits=[t_vmm])
        for sub in range(4):
            for dv in range(2):
                bank = b['pb'][sub * 2 + dv]
                vp = (sub * 2 + dv) % 2
                t_ev = s.emit('vector', lambda e, bank=bank, vp=vp: e.tensor_copy(
                    b['vtmp'][vp][:, :], bank[:, :]),
                    waits=[t_vmm, s.last('dmah')], inc='dve')
                s.dma('sync',
                      kv_stg.ap()[1024 + tile * TT + sub * 128: 1024 + tile * TT + sub * 128 + 128,
                                  dv * TT:(dv + 1) * TT],
                      b['vtmp'][vp][:, :], waits=[t_ev])
    return t_qm_last


def emit_att_fused(s, b, dri, layer, q_scr, kv_gth, qpos_d, t_cc, t_qm, consts):
    wo = dri[('wo', layer)]
    t_mk = None
    for r4 in range(4):
        t_l = s.dma('sync', b['knb'][:, :],
                    kv_gth.ap()[r4 * 2056 + 2048: r4 * 2056 + 2056, :],
                    waits=[t_cc] + ([t_mk] if t_mk else []))
        t_mk = s.emit('vector', lambda e, r4=r4: e.reduce_max(
            b['mkacc'][:, r4:r4 + 1], b['knb'][:, :], axis=AX.X),
            waits=[t_l], inc='dve')
    t_mk = s.emit('vector', lambda e: e.reduce_max(
        b['mk8'][:, :], b['mkacc'][:, 0:4], axis=AX.X), waits=[t_mk], inc='dve')
    t_tr = s.emit('tensor', lambda e: e.transpose(
        b['pb'][6][0:1, 0:8], b['mk8'][:, 0:1], b['ident'][0:8, 0:8]),
        waits=[t_mk, s.last('dve')], inc='pe')
    t_mk2 = s.emit('vector', lambda e: e.tensor_copy(
        b['mk2'][:, :], b['pb'][6][0:1, 0:8]), waits=[t_tr], inc='dve')

    for tile in range(2):
        nkt = 16 if tile == 0 else 32
        t_qp = s.dma('sync', b['qpos'][:, :],
                     qpos_d.ap()[:, tile * TT:tile * TT + TT], waits=[s.last('dve')])
        t_qr = s.emit('tensor', lambda e, tile=tile: e.matmul(
            b['pb'][6][:, :], b['ones_row_f32'][0:1, :], b['qpos'][0:1, :],
            start=True, stop=True, skip_group_check=True),
            waits=[t_qp, s.last('dve')], inc='pe')
        t_qrep = s.emit('vector', lambda e: e.tensor_copy(
            b['qpos_rep'][:, :], b['pb'][6][:, :]), waits=[t_qr], inc='dve')
        t_ctx_all = None
        t_q = [None, None]
        t_q_lastuse = [None, None]
        t_exp_head = [None, None]
        for kvh in range(8):
            kt_toks = []
            for r4 in range(4):
                halves = (0,) if tile == 0 else (0, 1)
                for half in halves:
                    chunk = r4 if half == 0 else 7 - r4
                    kt_toks.append(s.dma(
                        'sync', b['ksl'][:, 512 * chunk:512 * chunk + 512],
                        kv_gth.ap()[r4 * 2056 + kvh * 128: r4 * 2056 + kvh * 128 + 128,
                                    half * 512:half * 512 + 512],
                        waits=[s.last('pe')] if (r4 == 0 and half == 0) else []))
                    vin = kv_gth.ap().rearrange("(r t) d -> r t d", t=2056)[r4]
                    kt_toks.append(s.dma(
                        'gpsimd',
                        b['vsl'][:, 512 * chunk: 512 * chunk + 512]
                        .rearrange("p (q d) -> p q d", d=128),
                        vin[1024 + half * 512:1024 + half * 512 + 512]
                        .rearrange("(q p) d -> p q d", p=128)[:, :, kvh * 128:kvh * 128 + 128],
                        waits=[s.last('pe')] if (r4 == 0 and half == 0) else []))
            for qh in range(4):
                h = kvh * 4 + qh
                hp = h % 2
                if qh == 0 and kvh == 0:
                    t_q[hp] = s.dma('sync', b['q_sb2'][hp][:, :],
                                    q_scr.ap()[h * 128:(h + 1) * 128,
                                               tile * TT:tile * TT + TT],
                                    waits=[s.last('pe')])
                t_b2 = s.emit('vector', lambda e, tile=tile, h=h, kvh=kvh, hp=hp: e.tensor_tensor(
                    out=b['b2row2'][hp][:, :],
                    in0=b['qm_sb'][:, tile * 32 + h:tile * 32 + h + 1],
                    in1=b['mk2'][:, kvh:kvh + 1], op=ALU.mult),
                    waits=[t_mk2, t_qm, t_exp_head[hp]], inc='dve')
                t_b1 = s.emit('scalar', lambda e, hp=hp: e.activation(
                    out=b['b1row2'][hp][:, :], in_=b['b2row2'][hp][:, :], func=AF.Sqrt),
                    waits=[t_b2], inc='act')
                t_bb = s.emit('vector', lambda e, hp=hp: e.tensor_copy(
                    b['b1bf2'][hp][:, :], b['b1row2'][hp][:, :]),
                    waits=[t_b1], inc='dve')
                t_bn = s.emit('tensor', lambda e, hp=hp: e.matmul(
                    b['pb'][4][:, hp:hp + 1], b['negrow'][0:1, :], b['b1bf2'][hp][:, :],
                    start=True, stop=True, skip_group_check=True),
                    waits=[t_bb], inc='pe')
                t_nb = s.emit('vector', lambda e, hp=hp: e.tensor_copy(
                    b['negB2'][hp][:, :], b['pb'][4][:, hp:hp + 1]),
                    waits=[t_bn, t_exp_head[hp]], inc='dve')
                if h + 1 < 32:
                    np_ = (h + 1) % 2
                    t_q[np_] = s.dma('sync', b['q_sb2'][np_][:, :],
                                     q_scr.ap()[(h + 1) * 128:(h + 2) * 128,
                                                tile * TT:tile * TT + TT],
                                     waits=[t_q_lastuse[np_]])
                npair = nkt // 2
                t_expp = [None, None]
                t_mskp = [[None, None], [None, None]]
                t_denp = [None, None]
                t_avp = [None, None]
                LAGP = 1
                t_sT = None
                for ip in range(npair + LAGP):
                    pp = ip % 2
                    if ip < npair:
                        for half in range(2):
                            kt = 2 * ip + half
                            t_sT = s.emit('tensor', lambda e, kt=kt, pp=pp, half=half, hp=hp: e.matmul(
                                b['pbpair'][pp][:, half * TT:(half + 1) * TT],
                                b['ksl'][:, kt * 128:(kt + 1) * 128],
                                b['q_sb2'][hp][:, :], start=True, stop=True,
                                skip_group_check=True),
                                waits=[t_expp[pp], t_q[hp]] +
                                (kt_toks + [s.last('dve')] if kt == 0 else []),
                                inc='pe' if half == 1 else None)
                        if 2 * ip + 1 == nkt - 1:
                            t_q_lastuse[hp] = t_sT
                        t_expp[pp] = s.emit('scalar', lambda e, pp=pp, hp=hp: e.activation(
                            out=b['exp_pairsb'][pp][:, :], in_=b['pbpair'][pp][:, :],
                            func=AF.Exp, bias=b['negB2'][hp][:, 0:1]),
                            waits=[t_sT, t_nb, t_denp[pp], t_avp[pp],
                                   t_mskp[pp][0], t_mskp[pp][1]], inc='act')
                        for half in range(2):
                            kt = 2 * ip + half
                            if tile == 1 and kt < 16:
                                t_mskp[pp][half] = None
                                continue
                            t_mskp[pp][half] = s.emit('vector', lambda e, pp=pp, half=half, kt=kt: e.scalar_tensor_tensor(
                                out=b['exp_sb'][2 * pp + half][:, :], in0=b['qpos_rep'][:, :],
                                scalar=b['kpos'][:, kt:kt + 1],
                                in1=b['exp_sb'][2 * pp + half][:, :],
                                op0=ALU.is_ge, op1=ALU.mult),
                                waits=[t_expp[pp], t_qrep], inc='dve')
                    if ip >= LAGP:
                        jp = ip - LAGP
                        pj = jp % 2
                        for half in range(2):
                            kt = 2 * jp + half
                            t_avp[pj] = s.emit('tensor', lambda e, kt=kt, pj=pj, half=half, nkt=nkt: e.matmul(
                                b['pb'][2][:, :], b['vsl'][:, kt * 128:(kt + 1) * 128],
                                b['exp_sb'][2 * pj + half][:, :],
                                start=(kt == 0), stop=(kt == nkt - 1),
                                skip_group_check=True),
                                waits=[t_mskp[pj][half], t_expp[pj]],
                                inc='pe' if half == 1 else None)
                        for half in range(2):
                            kt = 2 * jp + half
                            t_denp[pj] = s.emit('tensor', lambda e, kt=kt, pj=pj, half=half, nkt=nkt: e.matmul(
                                b['pb'][3][0:1, :], b['ones_col'][:, 0:1],
                                b['exp_sb'][2 * pj + half][:, :],
                                start=(kt == 0), stop=(kt == nkt - 1),
                                skip_group_check=True), inc='pe' if half == 1 else None)
                t_exp_head[hp] = t_expp[(npair - 1) % 2]
                t_den_last = t_denp[(npair - 1) % 2]
                t_rc = s.emit('vector', lambda e: e.reciprocal(
                    b['rcp'][:, :], b['pb'][3][0:1, :]), waits=[t_den_last], inc='dve')
                t_rcb = s.emit('vector', lambda e: e.tensor_copy(
                    b['rcpb'][:, :], b['rcp'][:, :]), waits=[t_rc], inc='dve')
                t_rr = s.emit('tensor', lambda e: e.matmul(
                    b['pb'][5][:, :], b['posrow'][0:1, :], b['rcpb'][:, :],
                    start=True, stop=True, skip_group_check=True),
                    waits=[t_rcb], inc='pe')
                t_re = s.emit('vector', lambda e: e.tensor_copy(
                    b['rrep'][:, :], b['pb'][5][:, :]), waits=[t_rr], inc='dve')
                t_ctx_all = s.emit('vector', lambda e, h=h: e.tensor_tensor(
                    out=b['ctx_sb'][:, h * TT:(h + 1) * TT], in0=b['pb'][2][:, :],
                    in1=b['rrep'][:, :], op=ALU.mult),
                    waits=[t_re], inc='dve')
        last_ev, _ = emit_wsweep(s, b, tile, wo, 2, 1.0, extra_mm=None,
                                 first_waits=[t_ctx_all], rhs='y')
    return last_ev


_FUSED_NC = [None]

def kernel(**inp):
    p = _prep_shared(inp)
    pc = _prep_cores()
    X = np.asarray(inp['L0_attn_X'])
    xs = [np.ascontiguousarray(X[c // 4, _core_positions(c), :].T.astype(np.float32))
          for c in range(NCORES)]
    if _FUSED_NC[0] is None:
        _FUSED_NC[0] = build_fused()
    nc = _FUSED_NC[0]
    maps = []
    for c in range(NCORES):
        t = pc[c]
        m = dict(x_in=xs[c],
                 bw0=p['bw0'], bw1=p['bw1'],
                 wq0=p['wq0'], wk0=p['wk0'], wv0=p['wv0'], wo0=p['wo0'],
                 wq1=p['wq1'], wk1=p['wk1'], wv1=p['wv1'], wo1=p['wo1'],
                 cq0=t['cq0'], sq0=t['sq0'], ck0=t['ck0'], sk0=t['sk0'],
                 cq1=t['cq1'], sq1=t['sq1'], ck1=t['ck1'], sk1=t['sk1'],
                 rt=p['rt0'], la=p['la0'], lb=p['lb0'],
                 enc=p['enc1'], bbias=p['bb1'], c0=p['c0'], c1=p['c1'],
                 mc0=p['mc0'], mc1=p['mc1'], mt=p['mt'],
                 e16=p['e16'], ident=p['ident'], perm=p['perm'],
                 kpos=p['kpos'], qpos=t['qpos'],
                 negrow=p['negrow'], posrow=p['posrow'])
        maps.append(m)
    res = _run(nc, maps)
    out = np.empty((2, 4096, D), dtype=np.float32)
    for c in range(NCORES):
        out[c // 4, _core_positions(c), :] = res[c]['x_out'].T
    return out


# version 09
# speedup vs baseline: 1.5548x; 1.0068x over previous
import numpy as np
import ml_dtypes
import math
import contextlib
import concourse.bass as bass
import concourse.mybir as mybir
from concourse.bass_utils import run_bass_kernel_spmd

F32 = mybir.dt.float32
BF16 = mybir.dt.bfloat16
I32 = mybir.dt.int32
U8 = mybir.dt.uint8
AF = mybir.ActivationFunctionType
ALU = mybir.AluOpType
AX = mybir.AxisListType
BF = ml_dtypes.bfloat16

NCORES = 8
D = 4096
NCH = 32
TOK = 1024
TT = 512
EPS = 1e-5


def _core_chunks(c):
    m = c % 4
    return m, 7 - m

def _core_positions(c):
    m, h = _core_chunks(c)
    return np.concatenate([np.arange(512 * m, 512 * m + 512),
                           np.arange(512 * h, 512 * h + 512)])

def _tile_w(Wm):
    din, dout = Wm.shape
    return np.ascontiguousarray(
        Wm.reshape(din // 128, 128, dout // 128, 128).transpose(2, 1, 0, 3)
    ).reshape(dout // 128, 128, (din // 128) * 128)

def _rope_l0(pos):
    inv = 1.0 / (10000.0 ** (np.arange(0, 64, dtype=np.float64) / 64.0))
    ang = inv[:, None] * pos[None, :].astype(np.float64)
    c, s = np.cos(ang), np.sin(ang)
    return (np.vstack([c, c]).astype(np.float32), np.vstack([s, s]).astype(np.float32))

def _rope_l1(pos):
    idx = np.arange(0, 128, 2, dtype=np.float64) / 128.0
    freq = 1.0 / (2.0 * math.pi) * (500000.0 ** (-idx))
    factor, lo, hi, L0 = 32.0, 1.0, 4.0, 8192
    freq_low = lo / L0
    hi_f = hi / L0
    freq_scaled = np.where(freq < freq_low, freq / factor, freq)
    smooth = np.clip((L0 * freq - lo) / (hi - lo), 0.0, 1.0)
    freq_smooth = (1.0 - smooth) * (freq / factor) + smooth * freq
    is_mid = (freq >= freq_low) & (freq <= hi_f)
    f = np.where(is_mid, freq_smooth, freq_scaled)
    phase = 2.0 * math.pi * pos[:, None].astype(np.float64) * f[None, :]
    c, s = np.cos(phase).T, np.sin(phase).T
    return (np.vstack([c, c]).astype(np.float32), np.vstack([s, s]).astype(np.float32))

def _perm_rot():
    P = np.zeros((128, 128), dtype=np.float32)
    for m in range(64):
        P[m + 64, m] = -1.0
        P[m, m + 64] = 1.0
    return P.astype(BF)

def _e16():
    E = np.zeros((8, 128), dtype=np.float32)
    for e in range(8):
        E[e, 16 * e:16 * e + 16] = 2.0
    return E

def _mask_atlas():
    pats = [np.ones((128, 512), np.float32)]
    i = np.arange(128)[:, None]
    j = np.arange(512)[None, :]
    for d in (0, 128, 256, 384):
        pats.append((i <= j - d).astype(np.float32))
    pats.append(np.zeros((128, 512), np.float32))
    return np.concatenate(pats, axis=1).astype(BF)

def _off_feed(c):
    out = []
    for tile, nkt in ((0, 16), (1, 32)):
        p = _core_chunks(c)[tile]
        for kt in range(nkt):
            dp = kt - 4 * p
            out.append(0 if dp < 0 else (1 + dp if dp <= 3 else 5))
    out += [0] * (64 - len(out))
    return np.array(out, dtype=np.int32)[None, :]

def _mt_bf16():
    n = 4096
    t = np.arange(n, dtype=np.float64)
    M = np.empty((n, n), dtype=np.float64)
    M[:, 0] = 1.0 / math.sqrt(n)
    jj = np.arange(1, n // 2, dtype=np.float64)
    ang = 2.0 * math.pi * t[:, None] * jj[None, :] / n
    M[:, 1:n - 1:2] = math.sqrt(2.0 / n) * np.cos(ang)
    M[:, 2:n - 1:2] = math.sqrt(2.0 / n) * np.sin(ang)
    M[:, n - 1] = ((-1.0) ** t) / math.sqrt(n)
    return np.ascontiguousarray(M.T).astype(BF)

_HC = {}

def _cached(key, fn):
    if key not in _HC:
        _HC[key] = fn()
    return _HC[key]


class Sched:
    ENGINES = ("tensor", "vector", "scalar", "gpsimd", "sync")

    def __init__(self, nc, stack):
        self.nc = nc
        self.stack = stack
        self.prog = {e: [] for e in self.ENGINES}
        self.sems = {}
        self.cnt = {}

    def sem(self, name):
        if name not in self.sems:
            self.sems[name] = self.stack.enter_context(self.nc.semaphore(name))
            self.cnt[name] = 0
        return self.sems[name]

    def emit(self, eng, fn, waits=(), inc=None, k=1):
        tok = None
        if inc is not None:
            self.sem(inc)
            self.cnt[inc] += k
            tok = (inc, self.cnt[inc])
        wl = []
        for t in waits:
            if t is None:
                continue
            sname, v = t
            self.sem(sname)
            wl.append((self.sems[sname], v))
        semh = self.sems[inc] if inc is not None else None

        def closure(e, _wl=tuple(wl), _fn=fn, _semh=semh, _k=k):
            for sh, v in _wl:
                e.wait_ge(sh, v)
            inst = _fn(e)
            if _semh is not None:
                inst.then_inc(_semh, _k)
        self.prog[eng].append(closure)
        return tok

    def dma(self, queue, out, in_, waits=()):
        sem = 'dmah' if queue == 'sync' else 'dmas'
        return self.emit(queue, lambda e, o=out, i=in_: e.dma_start(out=o, in_=i),
                         waits=waits, inc=sem, k=16)

    def wait_only(self, eng, waits):
        wl = []
        for t in waits:
            if t is None:
                continue
            sname, v = t
            self.sem(sname)
            wl.append((self.sems[sname], v))

        def closure(e, _wl=tuple(wl)):
            for sh, v in _wl:
                e.wait_ge(sh, v)
        self.prog[eng].append(closure)

    def fence(self):
        toks = [self.last(x) for x in ('pe', 'dve', 'act', 'dmah', 'dmas', 'cc')]
        toks = [t for t in toks if t is not None]
        for e in self.ENGINES:
            self.wait_only(e, toks)

    def last(self, name):
        if name not in self.cnt or self.cnt[name] == 0:
            return None
        return (name, self.cnt[name])

    def finalize(self):
        with self.nc.Block() as block:
            for eng_name in self.ENGINES:
                ops = list(self.prog[eng_name])
                if not ops:
                    continue
                dec = getattr(block, eng_name)

                def body(e, _ops=ops):
                    for f in _ops:
                        f(e)
                dec(body)



def emit_ln(s, b, tile, extra_first_wait=()):
    col = lambda c: c * TOK + tile * TT
    pe_s1 = [None, None]
    pe_s2 = [None, None]
    ac_sq = [None, None]
    t_first = list(extra_first_wait)
    last2 = None
    for c in range(NCH):
        par = c % 2
        t_cast = s.emit('vector', lambda e, c=c, par=par: e.tensor_copy(
            b['sc_bf'][par][:, :], b['x_sb'][:, col(c):col(c) + TT]),
            waits=(t_first if c == 0 else []) + [pe_s1[par], ac_sq[par]], inc='dve')
        t_sq = s.emit('scalar', lambda e, par=par: e.activation(
            out=b['sc_sq'][par][:, :], in_=b['sc_bf'][par][:, :], func=AF.Square),
            waits=[t_cast, pe_s2[par]], inc='act')
        ac_sq[par] = t_sq
        pe_s1[par] = s.emit('tensor', lambda e, c=c, par=par: e.matmul(
            b['pb'][0][0:1, :], b['ones_col'][:, 0:1], b['sc_bf'][par][:, :],
            start=(c == 0), stop=(c == NCH - 1), skip_group_check=True),
            waits=[t_cast] + (t_first if c == 0 else []), inc='pe')
        pe_s2[par] = s.emit('tensor', lambda e, c=c, par=par: e.matmul(
            b['pb'][1][0:1, :], b['ones_col'][:, 0:1], b['sc_sq'][par][:, :],
            start=(c == 0), stop=(c == NCH - 1), skip_group_check=True),
            waits=[t_sq], inc='pe')
        last2 = pe_s2[par]
    t_m = s.emit('vector', lambda e: e.tensor_scalar_mul(
        b['m_row'][:, :], b['pb'][0][0:1, :], 1.0 / D), waits=[pe_s1[1], last2], inc='dve')
    t_msq = s.emit('vector', lambda e: e.tensor_mul(
        b['a_row'][:, :], b['m_row'][:, :], b['m_row'][:, :]), waits=[t_m], inc='dve')
    t_var = s.emit('vector', lambda e: e.scalar_tensor_tensor(
        out=b['a_row'][:, :], in0=b['pb'][1][0:1, :], scalar=1.0 / D,
        in1=b['a_row'][:, :], op0=ALU.mult, op1=ALU.subtract), waits=[t_msq], inc='dve')
    t_ln = s.emit('scalar', lambda e: e.activation(
        out=b['rstd_row'][:, :], in_=b['a_row'][:, :], func=AF.Ln,
        bias=b['eps_row'][:, 0:1]), waits=[t_var], inc='act')
    t_rstd = s.emit('scalar', lambda e: e.activation(
        out=b['a_row'][:, :], in_=b['rstd_row'][:, :], func=AF.Exp, scale=-0.5),
        waits=[t_ln], inc='act')
    t_mrs = s.emit('vector', lambda e: e.tensor_mul(
        b['rstd_row'][:, :], b['m_row'][:, :], b['a_row'][:, :]),
        waits=[t_rstd], inc='dve')
    t_r1 = s.emit('tensor', lambda e: e.matmul(
        b['pb'][2][:, :], b['ones_row_f32'][0:1, :], b['a_row'][:, :],
        start=True, stop=True), waits=[t_rstd, t_mrs], inc='pe')
    t_r2 = s.emit('tensor', lambda e: e.matmul(
        b['pb'][3][:, :], b['ones_row_f32'][0:1, :], b['rstd_row'][:, :],
        start=True, stop=True), waits=[t_mrs], inc='pe')
    t_y = None
    for c in range(NCH):
        s.emit('vector', lambda e, c=c: e.tensor_mul(
            b['sc_f32'][:, :], b['x_sb'][:, col(c):col(c) + TT], b['pb'][2][:, :]),
            waits=[t_r1, t_r2] if c == 0 else [], inc='dve')
        t_y = s.emit('vector', lambda e, c=c: e.tensor_sub(
            b['y_sb'][:, c * TT:(c + 1) * TT], b['sc_f32'][:, :], b['pb'][3][:, :]),
            inc='dve')
    return t_y


def emit_wsweep(s, b, tile, w_dram, n_slab_pre, scale, extra_mm=None,
                first_waits=(), rhs='y'):
    xcol = lambda j: j * TOK + tile * TT
    if rhs == 'y':
        rhs_ap = lambda c: b['y_sb'][:, c * TT:(c + 1) * TT]
    else:
        rhs_ap = lambda c: b['ctx_sb'][:, c * TT:(c + 1) * TT]
    nbank = 4
    evac_tok = [None] * nbank
    slab_tok = [None, None]
    for j in range(2):
        slab_tok[j] = s.dma('gpsimd', b['wslab'][j][:, :], w_dram.ap()[j],
                            waits=[s.last('pe')])
    t_ev = None
    t_mm = None
    for j in range(NCH):
        par = j % 2
        bank = b['pb'][4 + (j % nbank)]
        n_extra = 0 if extra_mm is None else (1 if extra_mm == 'l0inc' else 2)
        for c in range(NCH):
            waits = []
            if c == 0:
                waits = [slab_tok[par], evac_tok[j % nbank]] + (list(first_waits) if j == 0 else [])
            is_last = (c == NCH - 1) and n_extra == 0
            t_mm = s.emit('tensor', lambda e, j=j, c=c, par=par, bank=bank, il=is_last: e.matmul(
                bank[:, :], b['wslab'][par][:, c * 128:(c + 1) * 128], rhs_ap(c),
                start=(c == 0), stop=il, skip_group_check=True),
                waits=waits, inc='pe' if (c == NCH - 1) else None)
        if extra_mm == 'l0inc':
            t_mm = s.emit('tensor', lambda e, j=j, bank=bank: e.matmul(
                bank[:, :], b['ab_sb'][:, j * 128:(j + 1) * 128], b['midw'][:, :],
                start=False, stop=True, skip_group_check=True), inc='pe')
        elif extra_mm == 'l1':
            t_mm = s.emit('tensor', lambda e, j=j, bank=bank: e.matmul(
                bank[:, :], b['g_sb'][0:40, j * 128:(j + 1) * 128], b['e_sb'][0:40, :],
                start=False, stop=True, skip_group_check=True), inc='pe')
        if j + 2 < NCH:
            slab_tok[par] = s.dma('gpsimd', b['wslab'][par][:, :], w_dram.ap()[j + 2],
                                  waits=[t_mm])
        if extra_mm == 'l1':
            s.emit('vector', lambda e, j=j, bank=bank: e.tensor_scalar_add(
                b['sc_f32'][:, :], bank[:, :], b['bb2'][:, j:j + 1]),
                waits=[t_mm], inc='dve')
            t_ev = s.emit('vector', lambda e, j=j, sc=scale: e.scalar_tensor_tensor(
                out=b['x_sb'][:, xcol(j):xcol(j) + TT], in0=b['sc_f32'][:, :], scalar=sc,
                in1=b['x_sb'][:, xcol(j):xcol(j) + TT], op0=ALU.mult, op1=ALU.add),
                inc='dve')
        else:
            t_ev = s.emit('vector', lambda e, j=j, bank=bank, sc=scale: e.scalar_tensor_tensor(
                out=b['x_sb'][:, xcol(j):xcol(j) + TT], in0=bank[:, :], scalar=sc,
                in1=b['x_sb'][:, xcol(j):xcol(j) + TT], op0=ALU.mult, op1=ALU.add),
                waits=[t_mm], inc='dve')
        evac_tok[j % nbank] = t_ev
    return t_ev, t_mm


def emit_moe_head(s, b, tile, t_y, la_tok=None, dbg=None):
    ycol = lambda c: c * TT
    t_log = None
    for sub in range(4):
        for c in range(NCH):
            t_log = s.emit('tensor', lambda e, sub=sub, c=c: e.matmul(
                b['pb'][0][:, sub * 8:sub * 8 + 8],
                b['y_sb'][:, ycol(c) + sub * 128: ycol(c) + sub * 128 + 128],
                b['rt_sb'][:, c * 8:(c + 1) * 8],
                start=(c == 0), stop=(c == NCH - 1), skip_group_check=True),
                waits=[t_y] if (sub == 0 and c == 0) else [],
                inc='pe' if c == NCH - 1 else None)
    g = lambda t: t.rearrange("p (g e) -> p g e", e=8)
    bc = lambda t: t[:, :, None].broadcast_to([128, 4, 8])
    if dbg is not None:
        s.emit('vector', lambda e: e.tensor_copy(b['log_sb'][:, :], b['pb'][0][:, 0:32]),
               waits=[t_log], inc='dve')
    tp = s.emit('vector', lambda e: e.reduce_max(
        b['w4a'][:, :, None], g(b['pb'][0][:, 0:32]), axis=AX.X), waits=[t_log], inc='dve')
    t1 = s.emit('vector', lambda e: e.tensor_tensor(
        out=g(b['w32a']), in0=g(b['pb'][0][:, 0:32]), in1=bc(b['w4a']),
        op=ALU.subtract), waits=[tp], inc='dve')
    t2 = s.emit('scalar', lambda e: e.activation(
        out=b['w32p'][:, :], in_=b['w32a'][:, :], func=AF.Exp), waits=[t1], inc='act')
    tp = s.emit('vector', lambda e: e.reduce_max(
        b['w4a'][:, :, None], g(b['w32p']), axis=AX.X), waits=[t2], inc='dve')
    tp = s.emit('vector', lambda e: e.tensor_tensor(
        out=g(b['w32a']), in0=g(b['w32p']), in1=bc(b['w4a']), op=ALU.is_equal),
        waits=[tp], inc='dve')
    tp = s.emit('vector', lambda e: e.scalar_tensor_tensor(
        out=g(b['w32b']), in0=g(b['w32a']), scalar=-2.0, in1=g(b['w32p']),
        op0=ALU.mult, op1=ALU.add), waits=[tp], inc='dve')
    t6 = s.emit('vector', lambda e: e.reduce_max(
        b['w4b'][:, :, None], g(b['w32b']), axis=AX.X), waits=[tp], inc='dve')
    t8 = s.emit('vector', lambda e: e.tensor_add(
        b['w4a'][:, :], b['w4a'][:, :], b['w4b'][:, :]), waits=[t6], inc='dve')
    tp = s.emit('vector', lambda e: e.tensor_tensor(
        out=g(b['w32b']), in0=g(b['w32p']), in1=bc(b['w4b']), op=ALU.is_ge),
        waits=[t8], inc='dve')
    t9 = s.emit('vector', lambda e: e.reciprocal(
        b['w4b'][:, :], b['w4a'][:, :]), waits=[tp], inc='dve')
    tp = s.emit('vector', lambda e: e.tensor_tensor(
        out=g(b['w32a']), in0=g(b['w32p']), in1=g(b['w32b']), op=ALU.mult),
        waits=[t9], inc='dve')
    t11 = s.emit('vector', lambda e: e.tensor_tensor(
        out=g(b['w32a']), in0=g(b['w32a']), in1=bc(b['w4b']), op=ALU.mult),
        waits=[tp], inc='dve')
    tT = None
    for sub in range(4):
        tT = s.emit('tensor', lambda e, sub=sub: e.transpose(
            b['pb'][1][0:8, sub * 128:(sub + 1) * 128],
            b['w32a'][:, sub * 8:(sub + 1) * 8], b['ident'][:, :]),
            waits=[t11] if sub == 0 else [], inc='pe' if sub == 3 else None)
    tE = s.emit('vector', lambda e: e.tensor_copy(
        b['wfm'][:, :], b['pb'][1][0:8, 0:512]), waits=[tT], inc='dve')
    tR = s.emit('tensor', lambda e: e.matmul(
        b['pb'][2][:, :], b['e16'][:, :], b['wfm'][:, :], start=True, stop=True),
        waits=[tE], inc='pe')
    tRe = s.emit('vector', lambda e: e.tensor_copy(
        b['wrep'][:, :], b['pb'][2][:, :]), waits=[tR], inc='dve')
    tM = None
    for c in range(NCH):
        tM = s.emit('tensor', lambda e, c=c: e.matmul(
            b['pb'][3][:, :], b['ab_sb'][:, c * 128:(c + 1) * 128],
            b['y_sb'][:, c * TT:(c + 1) * TT],
            start=(c == 0), stop=(c == NCH - 1), skip_group_check=True),
            waits=[la_tok] if c == 0 else [], inc='pe' if c == NCH - 1 else None)
    tMw = s.emit('vector', lambda e: e.tensor_tensor(
        out=b['midw'][:, :], in0=b['pb'][3][:, :], in1=b['wrep'][:, :], op=ALU.mult),
        waits=[tM, tRe], inc='dve')
    return tMw


def emit_l1_head(s, b, t_y):
    tE = None
    for c in range(NCH):
        tE = s.emit('tensor', lambda e, c=c: e.matmul(
            b['pb'][0][0:40, :], b['enc_sb'][:, c * 40:(c + 1) * 40],
            b['y_sb'][:, c * TT:(c + 1) * TT],
            start=(c == 0), stop=(c == NCH - 1), skip_group_check=True),
            waits=[t_y] if c == 0 else [], inc='pe' if c == NCH - 1 else None)
    tEv = s.emit('vector', lambda e: e.tensor_copy(
        b['e_sb'][:, :], b['pb'][0][0:40, :]), waits=[tE], inc='dve')
    return tEv

def emit_gprep(s, b, dr):
    ws0 = b['wslab'][0]
    cf = lambda n: ws0[:, 0:NCH * n]
    mk = lambda n: ws0[:, 1024:1024 + NCH * n]
    wbuf = {'w0s': ws0[:, 2048:2048 + 1024], 'w1s': ws0[:, 3072:3072 + 256]}
    t_acc = None
    for which, n_r, w_buf in (('c0', 32, 'w0s'), ('c1', 8, 'w1s')):
        for e_i in range(3):
            cf_ap = cf(n_r).rearrange("p (c r) -> p c r", r=n_r)
            mk_ap = mk(n_r).rearrange("p (c r) -> p c r", r=n_r)
            t_cf = s.dma('gpsimd', cf_ap,
                         dr[which].ap()[e_i].rearrange("c p r -> p c r"),
                         waits=[t_acc])
            t_mk = s.dma('gpsimd', mk_ap,
                         dr['m' + which].ap()[e_i].rearrange("c p r -> p c r"))
            s.emit('vector', lambda e, n_r=n_r: e.tensor_tensor(
                out=cf(n_r), in0=cf(n_r), in1=mk(n_r), op=ALU.mult),
                waits=[t_cf, t_mk], inc='dve')
            if e_i == 0:
                t_acc = s.emit('vector', lambda e, n_r=n_r, w_buf=w_buf: e.tensor_copy(
                    wbuf[w_buf], cf(n_r)), inc='dve')
            else:
                t_acc = s.emit('vector', lambda e, n_r=n_r, w_buf=w_buf: e.tensor_tensor(
                    out=wbuf[w_buf], in0=wbuf[w_buf], in1=cf(n_r), op=ALU.add), inc='dve')
    r3 = lambda t, r, w: t.rearrange("p (c r) -> p c r", r=w)
    s.emit('vector', lambda e: e.tensor_scalar_mul(
        b['wcomb'].rearrange("p (c r) -> p c r", r=40)[:, :, 0:32],
        wbuf['w0s'].rearrange("p (c r) -> p c r", r=32), 0.5), waits=[t_acc], inc='dve')
    t_wc = s.emit('vector', lambda e: e.tensor_scalar_mul(
        b['wcomb'].rearrange("p (c r) -> p c r", r=40)[:, :, 32:40],
        wbuf['w1s'].rearrange("p (c r) -> p c r", r=8), 0.5), inc='dve')
    slab_tok = [None, None]
    for c in range(2):
        slab_tok[c] = s.dma('sync', b['wslab'][c][:, :], dr['mt'].ap()[c],
                            waits=[t_wc] if c == 0 else [s.last('pe')])
    t_mm = None
    for c in range(NCH):
        par = c % 2
        for tt_i in range(8):
            t_mm = s.emit('tensor', lambda e, c=c, par=par, tt_i=tt_i: e.matmul(
                b['pb'][tt_i][0:40, :], b['wcomb'][:, c * 40:(c + 1) * 40],
                b['wslab'][par][:, tt_i * 512:(tt_i + 1) * 512],
                start=(c == 0), stop=(c == NCH - 1), skip_group_check=True),
                waits=([slab_tok[par], t_wc] if tt_i == 0 else []),
                inc='pe' if tt_i == 7 else None)
        if c + 2 < NCH:
            slab_tok[par] = s.dma('sync', b['wslab'][par][:, :], dr['mt'].ap()[c + 2],
                                  waits=[t_mm])
    t_g = None
    for tt_i in range(8):
        t_g = s.emit('vector', lambda e, tt_i=tt_i: e.tensor_copy(
            b['g_sb'][:, tt_i * 512:(tt_i + 1) * 512], b['pb'][tt_i][0:40, :]),
            waits=[t_mm], inc='dve')
    return t_g, t_mm


def _alloc_common(nc, st, b, pb_n=8, ln=True, ctx=True):
    al = (lambda n, sh, dt: st.enter_context(nc.sbuf_tensor(n, sh, dt))) if ctx \
        else (lambda n, sh, dt: nc.alloc_sbuf_tensor(n, sh, dt))
    b['x_sb'] = al("x_sb", [128, NCH * TOK], F32)
    b['ones_col'] = al("ones_col", [128, 1], BF16)
    if pb_n:
        b['pb'] = [st.enter_context(nc.psum_tensor(f"pb{i}", [128, TT], F32))
                   for i in range(pb_n)]
    if ln:
        b['sc_bf'] = [al(f"scbf{i}", [128, TT], BF16) for i in range(2)]
        b['sc_sq'] = [al(f"scsq{i}", [128, TT], BF16) for i in range(2)]
        b['sc_f32'] = al("scf32", [128, TT], F32)
        for r in ('m_row', 'a_row', 'rstd_row'):
            b[r] = al(r, [1, TT], F32)
        b['eps_row'] = al("eps_row", [1, 1], F32)
        b['ones_row_f32'] = al("ones_row_f32", [1, 128], F32)

def _emit_const_init(s, b):
    s.emit('vector', lambda e: e.memset(b['ones_col'][:, :], 1.0), inc='dve')
    if 'ones_row_f32' in b:
        s.emit('vector', lambda e: e.memset(b['ones_row_f32'][:, :], 1.0), inc='dve')
    if 'eps_row' in b:
        s.emit('vector', lambda e: e.memset(b['eps_row'][:, :], EPS), inc='dve')

def _emit_x_load(s, b, x_dram):
    toks = []
    for q in range(4):
        toks.append(s.dma('sync',
                          b['x_sb'].ap().rearrange("p (c t) -> p c t", t=TOK)[:, q * 8:(q + 1) * 8],
                          x_dram.ap().rearrange("(c p) t -> p c t", p=128)[:, q * 8:(q + 1) * 8]))
    return toks

def _emit_x_store(s, b, xo_dram, waits):
    toks = []
    for q in range(4):
        toks.append(s.dma('sync',
                          xo_dram.ap().rearrange("(c p) t -> p c t", p=128)[:, q * 8:(q + 1) * 8],
                          b['x_sb'].ap().rearrange("p (c t) -> p c t", t=TOK)[:, q * 8:(q + 1) * 8],
                          waits=waits))
    return toks


def build_ffn_stage(layer):
    stack = contextlib.ExitStack()
    with stack:
        nc = bass.Bass("TRN2", target_bir_lowering=False, debug=False)
        b = {}
        x_in = nc.dram_tensor("x_in", [D, TOK], F32, kind="ExternalInput")
        x_out = nc.dram_tensor("x_out", [D, TOK], F32, kind="ExternalOutput")
        b['bw_dram'] = nc.dram_tensor("bw", [NCH, 128, NCH * 128], F32, kind="ExternalInput")
        if layer == 0:
            rt = nc.dram_tensor("rt", [128, NCH * 8], F32, kind="ExternalInput")
            la = nc.dram_tensor("la", [128, NCH * 128], F32, kind="ExternalInput")
            lb = nc.dram_tensor("lb", [128, NCH * 128], F32, kind="ExternalInput")
            e16 = nc.dram_tensor("e16", [8, 128], F32, kind="ExternalInput")
            ident = nc.dram_tensor("ident", [128, 128], F32, kind="ExternalInput")
        else:
            enc = nc.dram_tensor("enc", [128, NCH * 40], F32, kind="ExternalInput")
            bbias = nc.dram_tensor("bbias", [128, NCH], F32, kind="ExternalInput")
            b['c0_dram'] = nc.dram_tensor("c0", [3, NCH, 128, 32], F32, kind="ExternalInput")
            b['c1_dram'] = nc.dram_tensor("c1", [3, NCH, 128, 8], F32, kind="ExternalInput")
            b['mc0_dram'] = nc.dram_tensor("mc0", [3, NCH, 128, 32], U8, kind="ExternalInput")
            b['mc1_dram'] = nc.dram_tensor("mc1", [3, NCH, 128, 8], U8, kind="ExternalInput")
            b['mt_dram'] = nc.dram_tensor("mt", [NCH, 128, D], BF16, kind="ExternalInput")

        _alloc_common(nc, stack, b)
        b['y_sb'] = stack.enter_context(nc.sbuf_tensor("y_sb", [128, NCH * TT], BF16))
        b['wslab'] = [stack.enter_context(nc.sbuf_tensor(f"wslab{i}", [128, NCH * 128], BF16))
                      for i in range(2)]
        if layer == 0:
            b['rt_sb'] = stack.enter_context(nc.sbuf_tensor("rt_sb", [128, NCH * 8], BF16))
            b['ab_sb'] = stack.enter_context(nc.sbuf_tensor("ab_sb", [128, NCH * 128], BF16))
            b['e16'] = stack.enter_context(nc.sbuf_tensor("e16_sb", [8, 128], F32))
            b['ident'] = stack.enter_context(nc.sbuf_tensor("ident_sb", [128, 128], F32))
            for n, sh, dt in (('w32a', [128, 32], F32), ('w32b', [128, 32], F32),
                              ('w32p', [128, 32], F32), ('w4a', [128, 4], F32),
                              ('w4b', [128, 4], F32), ('wfm', [8, TT], F32),
                              ('wrep', [128, TT], F32), ('midw', [128, TT], BF16)):
                b[n] = stack.enter_context(nc.sbuf_tensor(n, sh, dt))
        else:
            b['enc_sb'] = stack.enter_context(nc.sbuf_tensor("enc_sb", [128, NCH * 40], BF16))
            b['bb2'] = stack.enter_context(nc.sbuf_tensor("bb2", [128, NCH], F32))
            b['e_sb'] = stack.enter_context(nc.sbuf_tensor("e_sb", [40, TT], BF16))
            b['g_sb'] = stack.enter_context(nc.sbuf_tensor("g_sb", [40, D], BF16))
            b['wcomb'] = stack.enter_context(nc.sbuf_tensor("wcomb", [128, NCH * 40], BF16))

        s = Sched(nc, stack)
        _emit_const_init(s, b)
        x_toks = _emit_x_load(s, b, x_in)
        dr = dict(bw=b['bw_dram'])
        if layer == 0:
            dr.update(rt=rt, la=la, lb=lb, e16=e16, ident=ident)
        else:
            dr.update(enc=enc, bbias=bbias, c0=b['c0_dram'], c1=b['c1_dram'],
                      mc0=b['mc0_dram'], mc1=b['mc1_dram'], mt=b['mt_dram'])
        last_ev = emit_ffn_body(s, b, dr, layer, x_toks)
        st_toks = _emit_x_store(s, b, x_out, waits=[last_ev])
        s.wait_only('sync', st_toks)
        s.finalize()
    return nc


def emit_ffn_body(s, b, dr, layer, first_toks):
    if layer == 0:
        small = [s.dma('gpsimd', b['rt_sb'][:, :], dr['rt'].ap(), waits=[s.last('pe')]),
                 s.dma('sync', b['e16'][:, :], dr['e16'].ap()),
                 s.dma('sync', b['ident'][:, :], dr['ident'].ap())]
        t_g = None
    else:
        small = [s.dma('gpsimd', b['enc_sb'][:, 0:NCH * 40], dr['enc'].ap(),
                       waits=[s.last('pe')]),
                 s.dma('sync', b['bb2'][:, :], dr['bbias'].ap())]
        t_g, _ = emit_gprep(s, b, dr)

    last_ev = None
    for tile in range(2):
        first = list(first_toks) + small + ([t_g] if layer == 1 else []) + \
            ([last_ev] if last_ev else [])
        t_y = emit_ln(s, b, tile, extra_first_wait=first)
        if layer == 0:
            la_tok = s.dma('gpsimd', b['ab_sb'][:, :], dr['la'].ap(),
                           waits=[s.last('pe')])
            tMw = emit_moe_head(s, b, tile, t_y, la_tok)
            lb_tok = s.dma('gpsimd', b['ab_sb'][:, :], dr['lb'].ap(),
                           waits=[s.last('pe')])
            last_ev, _ = emit_wsweep(s, b, tile, dr['bw'], 2, 0.5,
                                     extra_mm='l0inc', first_waits=[tMw, lb_tok])
        else:
            tEv = emit_l1_head(s, b, t_y)
            last_ev, _ = emit_wsweep(s, b, tile, dr['bw'], 2, 0.5,
                                     extra_mm='l1', first_waits=[tEv])
    return last_ev


def emit_rope_head(s, b, tile, t_proj, bankA, bankB, ct, st_, out_sb, war_dve=None):
    if war_dve is None:
        war = []
    elif isinstance(war_dve, list):
        war = war_dve
    else:
        war = [war_dve]
    t_cast = s.emit('vector', lambda e: e.tensor_copy(b['qb'][:, :], bankA[:, :]),
                    waits=[t_proj] + war, inc='dve')
    t_rot = s.emit('tensor', lambda e: e.matmul(
        bankB[:, :], b['perm_sb'][:, :], b['qb'][:, :], start=True, stop=True),
        waits=[t_cast], inc='pe')
    s.emit('vector', lambda e: e.tensor_tensor(
        out=b['rtmp1'][:, :], in0=b['qb'][:, :],
        in1=ct[:, tile * TT:tile * TT + TT], op=ALU.mult), inc='dve')
    s.emit('vector', lambda e: e.tensor_tensor(
        out=out_sb[:, :], in0=bankB[:, :],
        in1=st_[:, tile * TT:tile * TT + TT], op=ALU.mult),
        waits=[t_rot], inc='dve')
    t_out = s.emit('vector', lambda e: e.tensor_add(
        out_sb[:, :], b['rtmp1'][:, :], out_sb[:, :]), inc='dve')
    return t_out


def build_qkv_stage(layer):
    stack = contextlib.ExitStack()
    with stack:
        nc = bass.Bass("TRN2", target_bir_lowering=False, debug=False)
        b = {}
        x_in = nc.dram_tensor("x_in", [D, TOK], F32, kind="ExternalInput")
        wq = nc.dram_tensor("wq", [NCH, 128, NCH * 128], F32, kind="ExternalInput")
        wk = nc.dram_tensor("wk", [8, 128, NCH * 128], F32, kind="ExternalInput")
        wv = nc.dram_tensor("wv", [NCH, 128, 1024], F32, kind="ExternalInput")
        cq_d = nc.dram_tensor("cq", [128, TOK], F32, kind="ExternalInput")
        sq_d = nc.dram_tensor("sq", [128, TOK], F32, kind="ExternalInput")
        ck_d = nc.dram_tensor("ck", [128, TOK], F32, kind="ExternalInput")
        sk_d = nc.dram_tensor("sk", [128, TOK], F32, kind="ExternalInput")
        perm_d = nc.dram_tensor("perm", [128, 128], BF16, kind="ExternalInput")
        q_out = nc.dram_tensor("q_out", [D, TOK], BF16, kind="ExternalOutput")
        k_out = nc.dram_tensor("k_out", [1024, TOK], BF16, kind="ExternalOutput")
        v_out = nc.dram_tensor("v_out", [TOK, 1024], BF16, kind="ExternalOutput")
        kn_out = nc.dram_tensor("kn_out", [8, TOK], F32, kind="ExternalOutput")
        qm_out = nc.dram_tensor("qm_out", [1, 64], F32, kind="ExternalOutput")

        _alloc_common(nc, stack, b)
        b['y_sb'] = stack.enter_context(nc.sbuf_tensor("y_sb", [128, NCH * TT], BF16))
        b['wslab'] = [stack.enter_context(nc.sbuf_tensor(f"wslab{i}", [128, NCH * 128], BF16))
                      for i in range(2)]
        b['wv_sb'] = [stack.enter_context(nc.sbuf_tensor(f"wvsb{i}", [128, 1024], BF16))
                      for i in range(2)]
        b['ct_sb'] = stack.enter_context(nc.sbuf_tensor("ct_sb", [128, TOK], BF16))
        b['st_sb'] = stack.enter_context(nc.sbuf_tensor("st_sb", [128, TOK], BF16))
        b['perm_sb'] = stack.enter_context(nc.sbuf_tensor("perm_sb", [128, 128], BF16))
        b['qb'] = stack.enter_context(nc.sbuf_tensor("qb", [128, TT], BF16))
        b['rtmp1'] = stack.enter_context(nc.sbuf_tensor("rtmp1", [128, TT], F32))
        b['qrope'] = stack.enter_context(nc.sbuf_tensor("qrope", [128, TT], BF16))
        b['qsq'] = stack.enter_context(nc.sbuf_tensor("qsq", [128, TT], BF16))
        b['kn_row'] = stack.enter_context(nc.sbuf_tensor("kn_row", [1, TT], F32))
        b['qm_sb'] = stack.enter_context(nc.sbuf_tensor("qm_sb", [1, 64], F32))
        b['vtmp'] = [b['qb'], b['qrope']]

        s = Sched(nc, stack)
        _emit_const_init(s, b)
        x_toks = _emit_x_load(s, b, x_in)
        tbl = [s.dma('sync', b['perm_sb'][:, :], perm_d.ap())]
        t_qm_last = None
        last_y = None
        for tile in range(2):
            t_y = emit_ln(s, b, tile, extra_first_wait=x_toks + tbl +
                          ([last_y] if last_y else []))
            last_y = t_y
            for phase, nh, wdram, ctd, std in (
                    ('q', NCH, wq, cq_d, sq_d),
                    ('k', 8, wk, ck_d, sk_d)):
                ct, st_ = b['ct_sb'], b['st_sb']
                t_tb = [s.dma('gpsimd', b['ct_sb'][:, :], ctd.ap(),
                              waits=[s.last('dve')]),
                        s.dma('gpsimd', b['st_sb'][:, :], std.ap())]
                slab_tok = [None, None]
                for h in range(2):
                    slab_tok[h] = s.dma('gpsimd',
                                        b['wslab'][h].ap().rearrange("p (c f) -> p c f", f=128),
                                        wdram.ap()[h].rearrange("c p f -> p c f"),
                                        waits=[s.last('pe')])
                prev_tn = None
                for h in range(nh):
                    par = h % 2
                    bankA = b['pb'][par]
                    bankB = b['pb'][2 + par]
                    t_proj = None
                    for c in range(NCH):
                        waits = []
                        if c == 0:
                            waits = [slab_tok[par], t_y, s.last('dve')] + \
                                (t_tb if h < 2 else [])
                        t_proj = s.emit('tensor', lambda e, c=c, par=par, bankA=bankA: e.matmul(
                            bankA[:, :], b['wslab'][par][:, c * 128:(c + 1) * 128],
                            b['y_sb'][:, c * TT:(c + 1) * TT],
                            start=(c == 0), stop=(c == NCH - 1), skip_group_check=True),
                            waits=waits, inc='pe' if c == NCH - 1 else None)
                    if h + 2 < nh:
                        slab_tok[par] = s.dma('gpsimd',
                                              b['wslab'][par].ap().rearrange("p (c f) -> p c f", f=128),
                                              wdram.ap()[h + 2].rearrange("c p f -> p c f"),
                                              waits=[t_proj])
                    t_rope = emit_rope_head(s, b, tile, t_proj, bankA, bankB, ct, st_,
                                            b['qrope'], war_dve=s.last('dmah'))
                    if phase == 'q':
                        dst = q_out.ap()[h * 128:(h + 1) * 128, tile * TT:tile * TT + TT]
                    else:
                        dst = k_out.ap()[h * 128:(h + 1) * 128, tile * TT:tile * TT + TT]
                    s.dma('sync', dst, b['qrope'][:, :], waits=[t_rope])
                    t_sq = s.emit('scalar', lambda e: e.activation(
                        out=b['qsq'][:, :], in_=b['qrope'][:, :], func=AF.Square),
                        waits=[t_rope, prev_tn], inc='act')
                    t_n = s.emit('tensor', lambda e: e.matmul(
                        b['pb'][4][0:1, :], b['ones_col'][:, 0:1], b['qsq'][:, :],
                        start=True, stop=True, skip_group_check=True),
                        waits=[t_sq, s.last('dve')], inc='pe')
                    prev_tn = t_n
                    if phase == 'q':
                        slot = tile * 32 + h
                        t_qm_last = s.emit('vector', lambda e, slot=slot: e.reduce_max(
                            b['qm_sb'][:, slot:slot + 1], b['pb'][4][0:1, :], axis=AX.X),
                            waits=[t_n], inc='dve')
                    else:
                        t_kn = s.emit('vector', lambda e: e.tensor_copy(
                            b['kn_row'][:, :], b['pb'][4][0:1, :]),
                            waits=[t_n, s.last('dmah')], inc='dve')
                        s.dma('sync', kn_out.ap()[h:h + 1, tile * TT:tile * TT + TT],
                              b['kn_row'][:, :], waits=[t_kn])
            wv_tok = [None, None]
            for c in range(2):
                wv_tok[c] = s.dma('gpsimd', b['wv_sb'][c][:, :], wv.ap()[c],
                                  waits=[s.last('pe')])
            t_vmm = None
            for c in range(NCH):
                par = c % 2
                for sub in range(4):
                    for dv in range(2):
                        bank = b['pb'][sub * 2 + dv]
                        waits = []
                        if sub == 0 and dv == 0:
                            waits = [wv_tok[par]]
                            if c == 0:
                                waits.append(s.last('dve'))
                        t_vmm = s.emit('tensor', lambda e, c=c, par=par, sub=sub, dv=dv, bank=bank: e.matmul(
                            bank[:, :], b['y_sb'][:, c * TT + sub * 128:c * TT + sub * 128 + 128],
                            b['wv_sb'][par][:, dv * TT:(dv + 1) * TT],
                            start=(c == 0), stop=(c == NCH - 1), skip_group_check=True),
                            waits=waits,
                            inc='pe' if (sub == 3 and dv == 1) else None)
                if c + 2 < NCH:
                    wv_tok[par] = s.dma('gpsimd', b['wv_sb'][par][:, :], wv.ap()[c + 2],
                                        waits=[t_vmm])
            for sub in range(4):
                for dv in range(2):
                    bank = b['pb'][sub * 2 + dv]
                    vp = (sub * 2 + dv) % 2
                    t_ev = s.emit('vector', lambda e, bank=bank, vp=vp: e.tensor_copy(
                        b['vtmp'][vp][:, :], bank[:, :]),
                        waits=[t_vmm, s.last('dmah')], inc='dve')
                    s.dma('sync',
                          v_out.ap()[tile * TT + sub * 128: tile * TT + sub * 128 + 128,
                                     dv * TT:(dv + 1) * TT],
                          b['vtmp'][vp][:, :], waits=[t_ev])
        s.dma('sync', qm_out.ap()[:, :], b['qm_sb'][:, :], waits=[t_qm_last])
        s.wait_only('sync', [s.last('dmah'), s.last('dmas')])
        s.finalize()
    return nc


def build_att_stage(layer):
    stack = contextlib.ExitStack()
    with stack:
        nc = bass.Bass("TRN2", target_bir_lowering=False, debug=False)
        b = {}
        x_in = nc.dram_tensor("x_in", [D, TOK], F32, kind="ExternalInput")
        q_in = nc.dram_tensor("q_in", [D, TOK], BF16, kind="ExternalInput")
        kg = nc.dram_tensor("kg", [8, 128, 4096], BF16, kind="ExternalInput")
        vg = nc.dram_tensor("vg", [8, 128, 4096], BF16, kind="ExternalInput")
        kng = nc.dram_tensor("kng", [8, 4096], F32, kind="ExternalInput")
        qm_d = nc.dram_tensor("qm", [1, 64], F32, kind="ExternalInput")
        wo = nc.dram_tensor("wo", [NCH, 128, NCH * 128], F32, kind="ExternalInput")
        qpos_d = nc.dram_tensor("qpos", [1, TOK], F32, kind="ExternalInput")
        kpos_d = nc.dram_tensor("kpos", [128, 32], F32, kind="ExternalInput")
        ident_d = nc.dram_tensor("ident", [128, 128], F32, kind="ExternalInput")
        negrow_d = nc.dram_tensor("negrow", [1, 128], BF16, kind="ExternalInput")
        posrow_d = nc.dram_tensor("posrow", [1, 128], BF16, kind="ExternalInput")
        x_out = nc.dram_tensor("x_out", [D, TOK], F32, kind="ExternalOutput")

        _alloc_common(nc, stack, b, ln=False)
        b['y_sb'] = stack.enter_context(nc.sbuf_tensor("ctx_y", [128, NCH * TT], BF16))
        b['ctx_sb'] = b['y_sb']
        b['wslab'] = [stack.enter_context(nc.sbuf_tensor(f"wslab{i}", [128, NCH * 128], BF16))
                      for i in range(2)]
        b['ksl'] = b['wslab'][0]
        b['vsl'] = b['wslab'][1]
        b['q_sb'] = stack.enter_context(nc.sbuf_tensor("q_sb", [128, TT], BF16))
        b['exp_sb'] = [stack.enter_context(nc.sbuf_tensor(f"exp{i}", [128, TT], BF16))
                       for i in range(2)]
        b['qpos'] = stack.enter_context(nc.sbuf_tensor("qpos_sb", [1, TOK], F32))
        b['kpos'] = stack.enter_context(nc.sbuf_tensor("kpos_sb", [128, 32], F32))
        b['qpos_rep'] = stack.enter_context(nc.sbuf_tensor("qpos_rep", [128, TT], F32))
        b['ones_row_f32'] = stack.enter_context(nc.sbuf_tensor("ones_row_f32", [1, 128], F32))
        b['kn_sb'] = stack.enter_context(nc.sbuf_tensor("kn_sb", [8, TT], F32))
        b['mkacc'] = stack.enter_context(nc.sbuf_tensor("mkacc", [8, 8], F32))
        b['mk8'] = stack.enter_context(nc.sbuf_tensor("mk8", [8, 1], F32))
        b['mk2'] = stack.enter_context(nc.sbuf_tensor("mk2", [1, 8], F32))
        b['qm_sb'] = stack.enter_context(nc.sbuf_tensor("qm_sb", [1, 64], F32))
        b['negB'] = stack.enter_context(nc.sbuf_tensor("negB", [128, 1], F32))
        b['b2row'] = stack.enter_context(nc.sbuf_tensor("b2row", [1, 1], F32))
        b['b1row'] = stack.enter_context(nc.sbuf_tensor("b1row", [1, 1], F32))
        b['b1bf'] = stack.enter_context(nc.sbuf_tensor("b1bf", [1, 1], BF16))
        b['rcp'] = stack.enter_context(nc.sbuf_tensor("rcp", [1, TT], F32))
        b['rcpb'] = stack.enter_context(nc.sbuf_tensor("rcpb", [1, TT], BF16))
        b['rrep'] = stack.enter_context(nc.sbuf_tensor("rrep", [128, TT], F32))
        b['ident'] = stack.enter_context(nc.sbuf_tensor("ident_sb", [128, 128], F32))
        b['negrow'] = stack.enter_context(nc.sbuf_tensor("negrow_sb", [1, 128], BF16))
        b['posrow'] = stack.enter_context(nc.sbuf_tensor("posrow_sb", [1, 128], BF16))

        s = Sched(nc, stack)
        _emit_const_init(s, b)
        x_toks = _emit_x_load(s, b, x_in)
        pre = [s.dma('sync', b['qpos'][:, :], qpos_d.ap()),
               s.dma('sync', b['kpos'][:, :], kpos_d.ap()),
               s.dma('sync', b['qm_sb'][:, :], qm_d.ap()),
               s.dma('sync', b['ident'][:, :], ident_d.ap()),
               s.dma('sync', b['negrow'][:, :], negrow_d.ap()),
               s.dma('sync', b['posrow'][:, :], posrow_d.ap())]
        t_mk = None
        for i in range(8):
            t_l = s.dma('sync', b['kn_sb'][:, :],
                        kng.ap()[:, i * 512:(i + 1) * 512], waits=[t_mk])
            t_mk = s.emit('vector', lambda e, i=i: e.reduce_max(
                b['mkacc'][:, i:i + 1], b['kn_sb'][:, :], axis=AX.X),
                waits=[t_l], inc='dve')
        t_mk = s.emit('vector', lambda e: e.reduce_max(
            b['mk8'][:, :], b['mkacc'][:, :], axis=AX.X), waits=[t_mk], inc='dve')
        t_tr = s.emit('tensor', lambda e: e.transpose(
            b['pb'][6][0:1, 0:8], b['mk8'][:, 0:1], b['ident'][0:8, 0:8]),
            waits=[t_mk, pre[3]], inc='pe')
        t_mk2 = s.emit('vector', lambda e: e.tensor_copy(
            b['mk2'][:, :], b['pb'][6][0:1, 0:8]), waits=[t_tr], inc='dve')

        last_store_gate = None
        for tile in range(2):
            nkt = 16 if tile == 0 else 32
            t_qr = s.emit('tensor', lambda e, tile=tile: e.matmul(
                b['pb'][6][:, :], b['ones_row_f32'][0:1, :],
                b['qpos'][0:1, tile * TT:tile * TT + TT],
                start=True, stop=True, skip_group_check=True),
                waits=[pre[0], s.last('dve')], inc='pe')
            t_qrep = s.emit('vector', lambda e: e.tensor_copy(
                b['qpos_rep'][:, :], b['pb'][6][:, :]), waits=[t_qr], inc='dve')
            t_ctx_all = None
            t_kv_prev = None
            for kvh in range(8):
                t_k = s.dma('sync', b['ksl'][:, 0:nkt * 128],
                            kg.ap()[kvh][:, 0:nkt * 128], waits=[s.last('pe')])
                t_v = s.dma('sync', b['vsl'][:, 0:nkt * 128],
                            vg.ap()[kvh][:, 0:nkt * 128])
                for qh in range(4):
                    h = kvh * 4 + qh
                    t_q = s.dma('sync', b['q_sb'][:, :],
                                q_in.ap()[h * 128:(h + 1) * 128, tile * TT:tile * TT + TT],
                                waits=[s.last('pe')])
                    t_b2 = s.emit('vector', lambda e, tile=tile, h=h, kvh=kvh: e.tensor_tensor(
                        out=b['b2row'][:, :], in0=b['qm_sb'][:, tile * 32 + h:tile * 32 + h + 1],
                        in1=b['mk2'][:, kvh:kvh + 1], op=ALU.mult),
                        waits=[t_mk2, pre[2], s.last('act')], inc='dve')
                    t_b1 = s.emit('scalar', lambda e: e.activation(
                        out=b['b1row'][:, :], in_=b['b2row'][:, :], func=AF.Sqrt),
                        waits=[t_b2], inc='act')
                    t_bb = s.emit('vector', lambda e: e.tensor_copy(
                        b['b1bf'][:, :], b['b1row'][:, :]),
                        waits=[t_b1, s.last('pe')], inc='dve')
                    t_bn = s.emit('tensor', lambda e: e.matmul(
                        b['pb'][4][:, 0:1], b['negrow'][0:1, :], b['b1bf'][:, :],
                        start=True, stop=True, skip_group_check=True),
                        waits=[t_bb, pre[4]], inc='pe')
                    t_nb = s.emit('vector', lambda e: e.tensor_copy(
                        b['negB'][:, :], b['pb'][4][:, 0:1]),
                        waits=[t_bn, s.last('act')], inc='dve')
                    t_exp = [None, None]
                    t_msk = [None, None]
                    t_den = [None, None]
                    for kt in range(nkt):
                        par = kt % 2
                        t_sT = s.emit('tensor', lambda e, kt=kt, par=par: e.matmul(
                            b['pb'][par][:, :], b['ksl'][:, kt * 128:(kt + 1) * 128],
                            b['q_sb'][:, :], start=True, stop=True, skip_group_check=True),
                            waits=[t_exp[par], t_q] + ([t_k, t_v, s.last('dve')] if kt == 0 else []),
                            inc='pe')
                        t_exp[par] = s.emit('scalar', lambda e, par=par: e.activation(
                            out=b['exp_sb'][par][:, :], in_=b['pb'][par][:, :], func=AF.Exp,
                            bias=b['negB'][:, 0:1]),
                            waits=[t_sT, t_nb, t_den[par], t_msk[par]], inc='act')
                        t_msk[par] = s.emit('vector', lambda e, par=par, kt=kt: e.scalar_tensor_tensor(
                            out=b['exp_sb'][par][:, :], in0=b['qpos_rep'][:, :],
                            scalar=b['kpos'][:, kt:kt + 1], in1=b['exp_sb'][par][:, :],
                            op0=ALU.is_ge, op1=ALU.mult),
                            waits=[t_exp[par], t_qrep, pre[1]], inc='dve')
                        s.emit('tensor', lambda e, kt=kt, par=par, nkt=nkt: e.matmul(
                            b['pb'][2][:, :], b['vsl'][:, kt * 128:(kt + 1) * 128],
                            b['exp_sb'][par][:, :], start=(kt == 0), stop=(kt == nkt - 1),
                            skip_group_check=True),
                            waits=[t_msk[par]], inc=None)
                        t_den[par] = s.emit('tensor', lambda e, kt=kt, par=par, nkt=nkt: e.matmul(
                            b['pb'][3][0:1, :], b['ones_col'][:, 0:1],
                            b['exp_sb'][par][:, :], start=(kt == 0), stop=(kt == nkt - 1),
                            skip_group_check=True), inc='pe')
                    t_den_last = t_den[(nkt - 1) % 2]
                    t_rc = s.emit('vector', lambda e: e.reciprocal(
                        b['rcp'][:, :], b['pb'][3][0:1, :]), waits=[t_den_last], inc='dve')
                    t_rcb = s.emit('vector', lambda e: e.tensor_copy(
                        b['rcpb'][:, :], b['rcp'][:, :]), waits=[t_rc], inc='dve')
                    t_rr = s.emit('tensor', lambda e: e.matmul(
                        b['pb'][5][:, :], b['posrow'][0:1, :], b['rcpb'][:, :],
                        start=True, stop=True, skip_group_check=True),
                        waits=[t_rcb, pre[5]], inc='pe')
                    t_re = s.emit('vector', lambda e: e.tensor_copy(
                        b['rrep'][:, :], b['pb'][5][:, :]), waits=[t_rr], inc='dve')
                    t_ctx_all = s.emit('vector', lambda e, h=h: e.tensor_tensor(
                        out=b['ctx_sb'][:, h * TT:(h + 1) * TT], in0=b['pb'][2][:, :],
                        in1=b['rrep'][:, :], op=ALU.mult),
                        waits=[t_re], inc='dve')
            last_ev, _ = emit_wsweep(s, b, tile, wo, 2, 1.0, extra_mm=None,
                                     first_waits=[t_ctx_all], rhs='y')
            last_store_gate = last_ev
        st_toks = _emit_x_store(s, b, x_out, waits=[last_store_gate])
        s.wait_only('sync', st_toks)
        s.finalize()
    return nc


def _prep_shared(inp):
    p = {}
    p['bw0'] = _tile_w(np.ascontiguousarray(inp['L0_ffn_base_W'].T))
    p['rt0'] = np.ascontiguousarray(np.asarray(inp['L0_ffn_router_W']).T.reshape(NCH, 128, 8).transpose(1, 0, 2)).reshape(128, NCH * 8)
    p['la0'] = np.ascontiguousarray(
        np.asarray(inp['L0_ffn_lora_A']).transpose(1, 0, 2).reshape(D, 128)
        .reshape(NCH, 128, 128).transpose(1, 0, 2)).reshape(128, NCH * 128)
    p['lb0'] = np.ascontiguousarray(np.asarray(inp['L0_ffn_lora_B']).reshape(128, D))
    p['wq0'] = _tile_w(inp['L0_attn_Wq'])
    p['wk0'] = _tile_w(inp['L0_attn_Wk'])
    p['wv0'] = np.ascontiguousarray(inp['L0_attn_Wv']).reshape(NCH, 128, 1024)
    p['wo0'] = _tile_w(inp['L0_attn_Wo'])
    p['bw1'] = _tile_w(np.ascontiguousarray(inp['L1_ffn_W_base'].T))
    p['enc1'] = np.ascontiguousarray(
        np.concatenate([np.asarray(inp['L1_ffn_enc0']).T,
                        np.asarray(inp['L1_ffn_enc1']).T], axis=1)
        .reshape(NCH, 128, 40).transpose(1, 0, 2)).reshape(128, NCH * 40)
    p['bb1'] = np.ascontiguousarray(np.asarray(inp['L1_ffn_b_base']).reshape(NCH, 128).T)
    p['c0'] = np.ascontiguousarray(inp['L1_ffn_coefs0']).reshape(3, NCH, 128, 32)
    p['c1'] = np.ascontiguousarray(inp['L1_ffn_coefs1']).reshape(3, NCH, 128, 8)
    p['mc0'] = np.ascontiguousarray(inp['L1_ffn_mask0']).view(np.uint8).reshape(3, NCH, 128, 32)
    p['mc1'] = np.ascontiguousarray(inp['L1_ffn_mask1']).view(np.uint8).reshape(3, NCH, 128, 8)
    p['wq1'] = _tile_w(inp['L1_attn_Wq'])
    p['wk1'] = _tile_w(inp['L1_attn_Wk'])
    p['wv1'] = np.ascontiguousarray(inp['L1_attn_Wv']).reshape(NCH, 128, 1024)
    p['wo1'] = _tile_w(inp['L1_attn_Wo'])
    p['mt'] = _cached('mt', _mt_bf16).reshape(NCH, 128, D)
    p['e16'] = _cached('e16', _e16)
    p['ident'] = _cached('ident', lambda: np.eye(128, dtype=np.float32))
    p['perm'] = _cached('perm', _perm_rot)
    p['kpos'] = _cached('kpos', lambda: (
        np.arange(32, dtype=np.float32)[None, :] * 128 +
        np.arange(128, dtype=np.float32)[:, None]).astype(np.float32))
    p['negrow'] = _cached('negrow', lambda: np.full((1, 128), -1.02, dtype=np.float32).astype(BF))
    p['posrow'] = _cached('posrow', lambda: np.ones((1, 128), dtype=np.float32).astype(BF))
    return p


def _prep_cores():
    out = []
    scale = 1.0 / math.sqrt(128.0)
    for c in range(NCORES):
        pos = _core_positions(c)
        c0q, s0q = _cached(('r0', c), lambda pos=pos: _rope_l0(pos))
        c1q, s1q = _cached(('r1', c), lambda pos=pos: _rope_l1(pos))
        out.append(dict(
            cq0=np.ascontiguousarray(c0q * scale), sq0=np.ascontiguousarray(s0q * scale),
            ck0=c0q, sk0=s0q,
            cq1=np.ascontiguousarray(c1q * scale), sq1=np.ascontiguousarray(s1q * scale),
            ck1=c1q, sk1=s1q,
            qpos=np.ascontiguousarray(pos.astype(np.float32)[None, :])))
    return out


def _gather_kv(res, cores=range(NCORES)):
    Kg = [np.zeros((1024, 4096), dtype=BF) for _ in range(2)]
    Vg = [np.zeros((4096, 1024), dtype=BF) for _ in range(2)]
    KNg = [np.zeros((8, 4096), dtype=np.float32) for _ in range(2)]
    for c in cores:
        bt = c // 4
        m, hch = _core_chunks(c)
        k = res[c]['k_out']; v = res[c]['v_out']; kn = res[c]['kn_out']
        Kg[bt][:, 512 * m:512 * m + 512] = k[:, 0:512]
        Kg[bt][:, 512 * hch:512 * hch + 512] = k[:, 512:1024]
        Vg[bt][512 * m:512 * m + 512, :] = v[0:512, :]
        Vg[bt][512 * hch:512 * hch + 512, :] = v[512:1024, :]
        KNg[bt][:, 512 * m:512 * m + 512] = kn[:, 0:512]
        KNg[bt][:, 512 * hch:512 * hch + 512] = kn[:, 512:1024]
    kg_f = [np.ascontiguousarray(Kg[bt].reshape(8, 128, 4096)) for bt in range(2)]
    vg_f = [np.ascontiguousarray(
        Vg[bt].reshape(32, 128, 8, 128).transpose(2, 1, 0, 3).reshape(8, 128, 4096))
        for bt in range(2)]
    return kg_f, vg_f, KNg


_NC_CACHE = {}

def _get_nc(kind, layer):
    key = (kind, layer)
    if key not in _NC_CACHE:
        if kind == 'ffn':
            _NC_CACHE[key] = build_ffn_stage(layer)
        elif kind == 'qkv':
            _NC_CACHE[key] = build_qkv_stage(layer)
        else:
            _NC_CACHE[key] = build_att_stage(layer)
    return _NC_CACHE[key]


def _run(nc, in_maps):
    res = run_bass_kernel_spmd(nc, in_maps, core_ids=list(range(NCORES)))
    return res.results


def kernel_unfused(**inp):
    p = _prep_shared(inp)
    pc = _prep_cores()
    X = np.asarray(inp['L0_attn_X'])
    xs = [np.ascontiguousarray(X[c // 4, _core_positions(c), :].T.astype(np.float32))
          for c in range(NCORES)]

    def ffn_maps(layer, xs_cur):
        maps = []
        for c in range(NCORES):
            m = {'x_in': xs_cur[c], 'bw': p['bw0'] if layer == 0 else p['bw1']}
            if layer == 0:
                m.update(rt=p['rt0'], la=p['la0'], lb=p['lb0'], e16=p['e16'],
                         ident=p['ident'])
            else:
                m.update(enc=p['enc1'], bbias=p['bb1'], c0=p['c0'], c1=p['c1'],
                         mc0=p['mc0'], mc1=p['mc1'], mt=p['mt'])
            maps.append(m)
        return maps

    def qkv_maps(layer, xs_cur):
        maps = []
        for c in range(NCORES):
            t = pc[c]
            if layer == 0:
                m = dict(x_in=xs_cur[c], wq=p['wq0'], wk=p['wk0'], wv=p['wv0'],
                         cq=t['cq0'], sq=t['sq0'], ck=t['ck0'], sk=t['sk0'],
                         perm=p['perm'])
            else:
                m = dict(x_in=xs_cur[c], wq=p['wq1'], wk=p['wk1'], wv=p['wv1'],
                         cq=t['cq1'], sq=t['sq1'], ck=t['ck1'], sk=t['sk1'],
                         perm=p['perm'])
            maps.append(m)
        return maps

    def att_maps(layer, xs_cur, qk):
        kg_f, vg_f, KNg = qk['gather']
        maps = []
        for c in range(NCORES):
            bt = c // 4
            maps.append(dict(
                x_in=xs_cur[c], q_in=qk['res'][c]['q_out'], kg=kg_f[bt], vg=vg_f[bt],
                kng=KNg[bt], qm=qk['res'][c]['qm_out'],
                wo=p['wo0'] if layer == 0 else p['wo1'],
                qpos=pc[c]['qpos'], kpos=p['kpos'], ident=p['ident'],
                negrow=p['negrow'], posrow=p['posrow']))
        return maps

    cur = xs
    for layer in (0, 1):
        res = _run(_get_nc('ffn', layer), ffn_maps(layer, cur))
        cur = [res[c]['x_out'] for c in range(NCORES)]
        qres = _run(_get_nc('qkv', layer), qkv_maps(layer, cur))
        qk = {'res': qres, 'gather': _gather_kv(qres)}
        res = _run(_get_nc('att', layer), att_maps(layer, cur, qk))
        cur = [res[c]['x_out'] for c in range(NCORES)]
        res = _run(_get_nc('ffn', layer), ffn_maps(layer, cur))
        cur = [res[c]['x_out'] for c in range(NCORES)]

    out = np.empty((2, 4096, D), dtype=np.float32)
    for c in range(NCORES):
        out[c // 4, _core_positions(c), :] = cur[c].T
    return out


def _carve(nc, base, cursor, name, shape, dtype):
    nbytes = int(np.prod(shape[1:])) * (2 if dtype in (BF16,) else (1 if dtype == U8 else 4))
    off = (cursor + 31) & ~31
    h = nc.alloc_sbuf_tensor_at(name, shape, dtype, offset=base + off)
    return h, off + nbytes


def build_fused():
    stack = contextlib.ExitStack()
    with stack:
        nc = bass.Bass("TRN2", target_bir_lowering=False, debug=False)
        b = {}
        x_in = nc.dram_tensor("x_in", [D, TOK], F32, kind="ExternalInput")
        x_out = nc.dram_tensor("x_out", [D, TOK], F32, kind="ExternalOutput")
        dri = {}
        for l in (0, 1):
            dri[('bw', l)] = nc.dram_tensor(f"bw{l}", [NCH, 128, NCH * 128], F32, kind="ExternalInput")
            dri[('wq', l)] = nc.dram_tensor(f"wq{l}", [NCH, 128, NCH * 128], F32, kind="ExternalInput")
            dri[('wk', l)] = nc.dram_tensor(f"wk{l}", [8, 128, NCH * 128], F32, kind="ExternalInput")
            dri[('wv', l)] = nc.dram_tensor(f"wv{l}", [NCH, 128, 1024], F32, kind="ExternalInput")
            dri[('wo', l)] = nc.dram_tensor(f"wo{l}", [NCH, 128, NCH * 128], F32, kind="ExternalInput")
            for t in ('cq', 'sq', 'ck', 'sk'):
                dri[(t, l)] = nc.dram_tensor(f"{t}{l}", [128, TOK], F32, kind="ExternalInput")
        rt_d = nc.dram_tensor("rt", [128, NCH * 8], F32, kind="ExternalInput")
        la_d = nc.dram_tensor("la", [128, NCH * 128], F32, kind="ExternalInput")
        lb_d = nc.dram_tensor("lb", [128, NCH * 128], F32, kind="ExternalInput")
        enc_d = nc.dram_tensor("enc", [128, NCH * 40], F32, kind="ExternalInput")
        bbias_d = nc.dram_tensor("bbias", [128, NCH], F32, kind="ExternalInput")
        c0_d = nc.dram_tensor("c0", [3, NCH, 128, 32], F32, kind="ExternalInput")
        c1_d = nc.dram_tensor("c1", [3, NCH, 128, 8], F32, kind="ExternalInput")
        mc0_d = nc.dram_tensor("mc0", [3, NCH, 128, 32], U8, kind="ExternalInput")
        mc1_d = nc.dram_tensor("mc1", [3, NCH, 128, 8], U8, kind="ExternalInput")
        mt_d = nc.dram_tensor("mt", [NCH, 128, D], BF16, kind="ExternalInput")
        e16_d = nc.dram_tensor("e16", [8, 128], F32, kind="ExternalInput")
        ident_d = nc.dram_tensor("ident", [128, 128], F32, kind="ExternalInput")
        perm_d = nc.dram_tensor("perm", [128, 128], BF16, kind="ExternalInput")
        kpos_d = nc.dram_tensor("kpos", [128, 32], F32, kind="ExternalInput")
        qpos_d = nc.dram_tensor("qpos", [1, TOK], F32, kind="ExternalInput")
        negrow_d = nc.dram_tensor("negrow", [1, 128], BF16, kind="ExternalInput")
        posrow_d = nc.dram_tensor("posrow", [1, 128], BF16, kind="ExternalInput")
        q_scr = nc.dram_tensor("q_scr", [D, TOK], BF16, kind="Internal")
        kv_stg = nc.dram_tensor("kv_stg", [2056, TOK], BF16, kind="Internal")
        kv_gth = nc.dram_tensor("kv_gth", [4 * 2056, TOK], BF16, kind="Internal")

        _alloc_common(nc, stack, b, ctx=False, pb_n=0)
        pb01 = stack.enter_context(nc.psum_tensor("pb01", [128, 2 * TT], F32))
        pb25 = [stack.enter_context(nc.psum_tensor(f"pbm{i}", [128, TT], F32))
                for i in range(4)]
        pb67 = stack.enter_context(nc.psum_tensor("pb67", [128, 2 * TT], F32))
        b['pb'] = [pb01.ap()[:, 0:TT], pb01.ap()[:, TT:2 * TT],
                   pb25[0].ap(), pb25[1].ap(), pb25[2].ap(), pb25[3].ap(),
                   pb67.ap()[:, 0:TT], pb67.ap()[:, TT:2 * TT]]
        b['pbpair'] = [pb01.ap(), pb67.ap()]
        b['exp_pair'] = [None, None]
        b['y_sb'] = nc.alloc_sbuf_tensor("y_sb", [128, NCH * TT], BF16)
        b['ctx_sb'] = b['y_sb']
        b['wslab'] = [nc.alloc_sbuf_tensor(f"wslab{i}", [128, NCH * 128], BF16)
                      for i in range(2)]
        b['ksl'] = b['wslab'][0]
        b['vsl'] = b['wslab'][1]
        b['rt_sb'] = nc.alloc_sbuf_tensor("rt_sb", [128, NCH * 8], BF16)
        b['e16'] = nc.alloc_sbuf_tensor("e16_sb", [8, 128], F32)
        b['ident'] = nc.alloc_sbuf_tensor("ident_sb", [128, 128], F32)
        b['perm_sb'] = nc.alloc_sbuf_tensor("perm_sb", [128, 128], BF16)
        b['bb2'] = nc.alloc_sbuf_tensor("bb2", [128, NCH], F32)
        b['kpos'] = nc.alloc_sbuf_tensor("kpos_sb", [128, 32], F32)
        b['qm_sb'] = nc.alloc_sbuf_tensor("qm_sb", [1, 64], F32)
        b['negrow'] = nc.alloc_sbuf_tensor("negrow_sb", [1, 128], BF16)
        b['posrow'] = nc.alloc_sbuf_tensor("posrow_sb", [1, 128], BF16)

        szR = 15744
        baseR = nc.bump_sbuf(szR)[0]
        cur = 0
        b['e_sb'], cur = _carve(nc, baseR, cur, "e_sb", [40, TT], BF16)
        b['wcomb'], cur = _carve(nc, baseR, cur, "wcomb", [128, NCH * 40], BF16)
        b['g_sb'], cur = _carve(nc, baseR, cur, "g_sb", [40, D], BF16)
        b['enc_sb'], cur = _carve(nc, baseR, cur, "enc_sb", [128, NCH * 40], BF16)
        assert cur <= szR, cur
        cur = 0
        b['w32a'], cur = _carve(nc, baseR, cur, "w32a", [128, 32], F32)
        b['w32b'], cur = _carve(nc, baseR, cur, "w32b", [128, 32], F32)
        b['w32p'], cur = _carve(nc, baseR, cur, "w32p", [128, 32], F32)
        b['w4a'], cur = _carve(nc, baseR, cur, "w4a", [128, 4], F32)
        b['w4b'], cur = _carve(nc, baseR, cur, "w4b", [128, 4], F32)
        b['wfm'], cur = _carve(nc, baseR, cur, "wfm", [8, TT], F32)
        b['wrep'], cur = _carve(nc, baseR, cur, "wrep", [128, TT], F32)
        b['midw'], cur = _carve(nc, baseR, cur, "midw", [128, TT], BF16)
        b['ab_sb'], cur = _carve(nc, baseR, cur, "ab_sb", [128, NCH * 128], BF16)
        assert cur <= szR, cur
        cur = 0
        b['ct_sb'], cur = _carve(nc, baseR, cur, "ct_sb", [128, TOK], BF16)
        b['st_sb'], cur = _carve(nc, baseR, cur, "st_sb", [128, TOK], BF16)
        b['qb'], cur = _carve(nc, baseR, cur, "qb", [128, TT], BF16)
        b['rtmp1'], cur = _carve(nc, baseR, cur, "rtmp1", [128, TT], F32)
        b['qrope'], cur = _carve(nc, baseR, cur, "qrope", [128, TT], BF16)
        b['qsq'], cur = _carve(nc, baseR, cur, "qsq", [128, TT], BF16)
        b['kn_row'], cur = _carve(nc, baseR, cur, "kn_row", [1, TT], F32)
        assert cur <= szR, cur
        b['wv_sb'] = [b['ct_sb'], b['st_sb']]
        b['vtmp'] = [b['qb'], b['qrope']]
        cur = 0
        b['q_sb2'] = [None, None]
        b['q_sb2'][0], cur = _carve(nc, baseR, cur, "q_sb0", [128, TT], BF16)
        b['q_sb2'][1], cur = _carve(nc, baseR, cur, "q_sb1", [128, TT], BF16)
        b['exp_sb'] = [None] * 4
        e0_off = (cur + 31) & ~31
        b['exp_sb'][0], cur = _carve(nc, baseR, cur, "exp0", [128, TT], BF16)
        b['exp_sb'][1], cur = _carve(nc, baseR, cur, "exp1", [128, TT], BF16)
        b['qpos_rep'], cur = _carve(nc, baseR, cur, "qpos_rep", [128, TT], F32)
        b['qpos'], cur = _carve(nc, baseR, cur, "qpos_sb", [1, TT], F32)
        knb_off = cur
        e2_off = (cur + 31) & ~31
        b['exp_sb'][2], cur = _carve(nc, baseR, cur, "exp2", [128, TT], BF16)
        b['exp_sb'][3], cur = _carve(nc, baseR, cur, "exp3", [128, TT], BF16)
        b['exp_pairsb'] = [
            nc.alloc_sbuf_tensor_at("expp0", [128, 2 * TT], BF16, offset=baseR + e0_off),
            nc.alloc_sbuf_tensor_at("expp1", [128, 2 * TT], BF16, offset=baseR + e2_off)]
        b['knb'] = nc.alloc_sbuf_tensor_at("knb", [8, TOK], BF16, offset=baseR + knb_off)
        b['mkacc'], cur = _carve(nc, baseR, cur, "mkacc", [8, 8], F32)
        b['mk8'], cur = _carve(nc, baseR, cur, "mk8", [8, 1], F32)
        b['mk2'], cur = _carve(nc, baseR, cur, "mk2", [1, 8], F32)
        b['negB2'] = [None, None]
        b['negB2'][0], cur = _carve(nc, baseR, cur, "negB0", [128, 1], F32)
        b['negB2'][1], cur = _carve(nc, baseR, cur, "negB1", [128, 1], F32)
        b['b2row2'] = [None, None]
        b['b2row2'][0], cur = _carve(nc, baseR, cur, "b2row0", [1, 1], F32)
        b['b2row2'][1], cur = _carve(nc, baseR, cur, "b2row1", [1, 1], F32)
        b['b1row2'] = [None, None]
        b['b1row2'][0], cur = _carve(nc, baseR, cur, "b1row0", [1, 1], F32)
        b['b1row2'][1], cur = _carve(nc, baseR, cur, "b1row1", [1, 1], F32)
        b['b1bf2'] = [None, None]
        b['b1bf2'][0], cur = _carve(nc, baseR, cur, "b1bf0", [1, 1], BF16)
        b['b1bf2'][1], cur = _carve(nc, baseR, cur, "b1bf1", [1, 1], BF16)
        b['rcp'], cur = _carve(nc, baseR, cur, "rcp", [1, TT], F32)
        b['rcpb'], cur = _carve(nc, baseR, cur, "rcpb", [1, TT], BF16)
        b['rrep'], cur = _carve(nc, baseR, cur, "rrep", [128, TT], F32)
        assert cur <= szR, cur

        s = Sched(nc, stack)
        _emit_const_init(s, b)
        x_toks = _emit_x_load(s, b, x_in)
        consts = [s.dma('sync', b['perm_sb'][:, :], perm_d.ap()),
                  s.dma('sync', b['kpos'][:, :], kpos_d.ap()),
                  s.dma('sync', b['negrow'][:, :], negrow_d.ap()),
                  s.dma('sync', b['posrow'][:, :], posrow_d.ap())]

        for layer in (0, 1):
            dr = dict(bw=dri[('bw', layer)])
            if layer == 0:
                dr.update(rt=rt_d, la=la_d, lb=lb_d, e16=e16_d, ident=ident_d)
            else:
                dr.update(enc=enc_d, bbias=bbias_d, c0=c0_d, c1=c1_d,
                          mc0=mc0_d, mc1=mc1_d, mt=mt_d)
            emit_ffn_body(s, b, dr, layer, x_toks if layer == 0 else [])
            s.fence()
            t_qm = emit_qkv_fused(s, b, dri, layer, q_scr, kv_stg, consts)
            s.wait_only('gpsimd', [s.last('dmah'), s.last('dmas')])
            t_cc = s.emit('gpsimd', lambda e: e.collective_compute(
                "AllGather", ALU.bypass,
                replica_groups=[[0, 1, 2, 3], [4, 5, 6, 7]],
                ins=[kv_stg.ap().opt()], outs=[kv_gth.ap().opt()]), inc='cc', k=1)
            s.fence()
            emit_att_fused(s, b, dri, layer, q_scr, kv_gth, qpos_d, t_cc, t_qm, consts)
            s.fence()
            emit_ffn_body(s, b, dr, layer, [])
            s.fence()
        st_toks = _emit_x_store(s, b, x_out, waits=[s.last('dve')])
        s.wait_only('sync', st_toks)
        s.finalize()
    return nc


def emit_qkv_fused(s, b, dri, layer, q_scr, kv_stg, consts):
    wq, wk, wv = dri[('wq', layer)], dri[('wk', layer)], dri[('wv', layer)]
    t_qm_last = None
    last_y = None
    prev_store = None
    prev_kn_dma = None
    vwar = []
    for tile in range(2):
        t_y = emit_ln(s, b, tile, extra_first_wait=consts +
                      ([last_y] if last_y else []))
        last_y = t_y
        for phase, nh, wdram, ctd, std in (
                ('q', NCH, wq, dri[('cq', layer)], dri[('sq', layer)]),
                ('k', 8, wk, dri[('ck', layer)], dri[('sk', layer)])):
            ct, st_ = b['ct_sb'], b['st_sb']
            t_tb = [s.dma('gpsimd', b['ct_sb'][:, :], ctd.ap(), waits=[s.last('dve')]),
                    s.dma('gpsimd', b['st_sb'][:, :], std.ap())]
            slab_tok = [None, None]
            for h in range(2):
                slab_tok[h] = s.dma('gpsimd', b['wslab'][h][:, :], wdram.ap()[h],
                                    waits=[s.last('pe')])
            prev_tn = None
            for h in range(nh):
                par = h % 2
                bankA = b['pb'][par]
                bankB = b['pb'][2 + par]
                t_proj = None
                for c in range(NCH):
                    waits = []
                    if c == 0:
                        waits = [slab_tok[par], t_y, s.last('dve')] + \
                            (t_tb if h < 2 else [])
                    t_proj = s.emit('tensor', lambda e, c=c, par=par, bankA=bankA: e.matmul(
                        bankA[:, :], b['wslab'][par][:, c * 128:(c + 1) * 128],
                        b['y_sb'][:, c * TT:(c + 1) * TT],
                        start=(c == 0), stop=(c == NCH - 1), skip_group_check=True),
                        waits=waits, inc='pe' if c == NCH - 1 else None)
                if h + 2 < nh:
                    slab_tok[par] = s.dma('gpsimd', b['wslab'][par][:, :],
                                          wdram.ap()[h + 2], waits=[t_proj])
                war = [prev_store] + (vwar if (phase == 'q' and h == 0) else [])
                t_rope = emit_rope_head(s, b, tile, t_proj, bankA, bankB, ct, st_,
                                        b['qrope'], war_dve=war)
                if phase == 'q':
                    dst = q_scr.ap()[h * 128:(h + 1) * 128, tile * TT:tile * TT + TT]
                else:
                    dst = kv_stg.ap()[h * 128:(h + 1) * 128, tile * TT:tile * TT + TT]
                prev_store = s.dma('sync', dst, b['qrope'][:, :], waits=[t_rope])
                t_sq = s.emit('scalar', lambda e: e.activation(
                    out=b['qsq'][:, :], in_=b['qrope'][:, :], func=AF.Square),
                    waits=[t_rope, prev_tn], inc='act')
                t_n = s.emit('tensor', lambda e: e.matmul(
                    b['pb'][4][0:1, :], b['ones_col'][:, 0:1], b['qsq'][:, :],
                    start=True, stop=True, skip_group_check=True),
                    waits=[t_sq, s.last('dve')], inc='pe')
                prev_tn = t_n
                if phase == 'q':
                    slot = tile * 32 + h
                    t_qm_last = s.emit('vector', lambda e, slot=slot: e.reduce_max(
                        b['qm_sb'][:, slot:slot + 1], b['pb'][4][0:1, :], axis=AX.X),
                        waits=[t_n], inc='dve')
                else:
                    t_kn = s.emit('vector', lambda e: e.tensor_copy(
                        b['kn_row'][:, :], b['pb'][4][0:1, :]),
                        waits=[t_n, prev_kn_dma], inc='dve')
                    prev_kn_dma = s.dma(
                        'gpsimd', kv_stg.ap()[2048 + h:2049 + h, tile * TT:tile * TT + TT],
                        b['kn_row'][:, :], waits=[t_kn])
        wv_tok = [None, None]
        for c in range(2):
            wv_tok[c] = s.dma('gpsimd', b['wv_sb'][c][:, :], wv.ap()[c],
                              waits=[s.last('pe'), s.last('dve')])
        t_vmm = None
        for c in range(NCH):
            par = c % 2
            for sub in range(4):
                for dv in range(2):
                    bank = b['pb'][sub * 2 + dv]
                    waits = []
                    if sub == 0 and dv == 0:
                        waits = [wv_tok[par]]
                        if c == 0:
                            waits.append(s.last('dve'))
                    t_vmm = s.emit('tensor', lambda e, c=c, par=par, sub=sub, dv=dv, bank=bank: e.matmul(
                        bank[:, :], b['y_sb'][:, c * TT + sub * 128:c * TT + sub * 128 + 128],
                        b['wv_sb'][par][:, dv * TT:(dv + 1) * TT],
                        start=(c == 0), stop=(c == NCH - 1), skip_group_check=True),
                        waits=waits, inc='pe' if (sub == 3 and dv == 1) else None)
            if c + 2 < NCH:
                wv_tok[par] = s.dma('gpsimd', b['wv_sb'][par][:, :], wv.ap()[c + 2],
                                    waits=[t_vmm])
        vstore = [None, prev_store]
        for sub in range(4):
            for dv in range(2):
                bank = b['pb'][sub * 2 + dv]
                vp = (sub * 2 + dv) % 2
                t_ev = s.emit('vector', lambda e, bank=bank, vp=vp: e.tensor_copy(
                    b['vtmp'][vp][:, :], bank[:, :]),
                    waits=[t_vmm, vstore[vp]], inc='dve')
                vstore[vp] = s.dma(
                    'sync',
                    kv_stg.ap()[1024 + tile * TT + sub * 128: 1024 + tile * TT + sub * 128 + 128,
                                dv * TT:(dv + 1) * TT],
                    b['vtmp'][vp][:, :], waits=[t_ev])
        vwar = list(vstore)
        prev_store = None
    return t_qm_last


def emit_att_fused(s, b, dri, layer, q_scr, kv_gth, qpos_d, t_cc, t_qm, consts):
    wo = dri[('wo', layer)]
    t_mk = None
    for r4 in range(4):
        t_l = s.dma('sync', b['knb'][:, :],
                    kv_gth.ap()[r4 * 2056 + 2048: r4 * 2056 + 2056, :],
                    waits=[t_cc] + ([t_mk] if t_mk else []))
        t_mk = s.emit('vector', lambda e, r4=r4: e.reduce_max(
            b['mkacc'][:, r4:r4 + 1], b['knb'][:, :], axis=AX.X),
            waits=[t_l], inc='dve')
    t_mk = s.emit('vector', lambda e: e.reduce_max(
        b['mk8'][:, :], b['mkacc'][:, 0:4], axis=AX.X), waits=[t_mk], inc='dve')
    t_tr = s.emit('tensor', lambda e: e.transpose(
        b['pb'][6][0:1, 0:8], b['mk8'][:, 0:1], b['ident'][0:8, 0:8]),
        waits=[t_mk, s.last('dve')], inc='pe')
    t_mk2 = s.emit('vector', lambda e: e.tensor_copy(
        b['mk2'][:, :], b['pb'][6][0:1, 0:8]), waits=[t_tr], inc='dve')

    for tile in range(2):
        nkt = 16 if tile == 0 else 32
        t_qp = s.dma('sync', b['qpos'][:, :],
                     qpos_d.ap()[:, tile * TT:tile * TT + TT], waits=[s.last('dve')])
        t_qr = s.emit('tensor', lambda e, tile=tile: e.matmul(
            b['pb'][6][:, :], b['ones_row_f32'][0:1, :], b['qpos'][0:1, :],
            start=True, stop=True, skip_group_check=True),
            waits=[t_qp, s.last('dve')], inc='pe')
        t_qrep = s.emit('vector', lambda e: e.tensor_copy(
            b['qpos_rep'][:, :], b['pb'][6][:, :]), waits=[t_qr], inc='dve')
        t_ctx_all = None
        t_q = [None, None]
        t_q_lastuse = [None, None]
        t_exp_head = [None, None]
        for kvh in range(8):
            kt_toks = []
            for r4 in range(4):
                halves = (0,) if tile == 0 else (0, 1)
                for half in halves:
                    chunk = r4 if half == 0 else 7 - r4
                    kt_toks.append(s.dma(
                        'sync', b['ksl'][:, 512 * chunk:512 * chunk + 512],
                        kv_gth.ap()[r4 * 2056 + kvh * 128: r4 * 2056 + kvh * 128 + 128,
                                    half * 512:half * 512 + 512],
                        waits=[s.last('pe')] if (r4 == 0 and half == 0) else []))
                    vin = kv_gth.ap().rearrange("(r t) d -> r t d", t=2056)[r4]
                    kt_toks.append(s.dma(
                        'gpsimd',
                        b['vsl'][:, 512 * chunk: 512 * chunk + 512]
                        .rearrange("p (q d) -> p q d", d=128),
                        vin[1024 + half * 512:1024 + half * 512 + 512]
                        .rearrange("(q p) d -> p q d", p=128)[:, :, kvh * 128:kvh * 128 + 128],
                        waits=[s.last('pe')] if (r4 == 0 and half == 0) else []))
            for qh in range(4):
                h = kvh * 4 + qh
                hp = h % 2
                if qh == 0 and kvh == 0:
                    t_q[hp] = s.dma('sync', b['q_sb2'][hp][:, :],
                                    q_scr.ap()[h * 128:(h + 1) * 128,
                                               tile * TT:tile * TT + TT],
                                    waits=[s.last('pe')])
                t_b2 = s.emit('vector', lambda e, tile=tile, h=h, kvh=kvh, hp=hp: e.tensor_tensor(
                    out=b['b2row2'][hp][:, :],
                    in0=b['qm_sb'][:, tile * 32 + h:tile * 32 + h + 1],
                    in1=b['mk2'][:, kvh:kvh + 1], op=ALU.mult),
                    waits=[t_mk2, t_qm, t_exp_head[hp]], inc='dve')
                t_b1 = s.emit('scalar', lambda e, hp=hp: e.activation(
                    out=b['b1row2'][hp][:, :], in_=b['b2row2'][hp][:, :], func=AF.Sqrt),
                    waits=[t_b2], inc='act')
                t_bb = s.emit('vector', lambda e, hp=hp: e.tensor_copy(
                    b['b1bf2'][hp][:, :], b['b1row2'][hp][:, :]),
                    waits=[t_b1], inc='dve')
                t_bn = s.emit('tensor', lambda e, hp=hp: e.matmul(
                    b['pb'][4][:, hp:hp + 1], b['negrow'][0:1, :], b['b1bf2'][hp][:, :],
                    start=True, stop=True, skip_group_check=True),
                    waits=[t_bb], inc='pe')
                t_nb = s.emit('vector', lambda e, hp=hp: e.tensor_copy(
                    b['negB2'][hp][:, :], b['pb'][4][:, hp:hp + 1]),
                    waits=[t_bn, t_exp_head[hp]], inc='dve')
                if h + 1 < 32:
                    np_ = (h + 1) % 2
                    t_q[np_] = s.dma('sync', b['q_sb2'][np_][:, :],
                                     q_scr.ap()[(h + 1) * 128:(h + 2) * 128,
                                                tile * TT:tile * TT + TT],
                                     waits=[t_q_lastuse[np_]])
                npair = nkt // 2
                t_expp = [None, None]
                t_mskp = [[None, None], [None, None]]
                t_denp = [None, None]
                t_avp = [None, None]
                LAGP = 1
                t_sT = None
                for ip in range(npair + LAGP):
                    pp = ip % 2
                    if ip < npair:
                        for half in range(2):
                            kt = 2 * ip + half
                            t_sT = s.emit('tensor', lambda e, kt=kt, pp=pp, half=half, hp=hp: e.matmul(
                                b['pbpair'][pp][:, half * TT:(half + 1) * TT],
                                b['ksl'][:, kt * 128:(kt + 1) * 128],
                                b['q_sb2'][hp][:, :], start=True, stop=True,
                                skip_group_check=True),
                                waits=[t_expp[pp], t_q[hp]] +
                                (kt_toks + [s.last('dve')] if kt == 0 else []),
                                inc='pe' if half == 1 else None)
                        if 2 * ip + 1 == nkt - 1:
                            t_q_lastuse[hp] = t_sT
                        t_expp[pp] = s.emit('scalar', lambda e, pp=pp, hp=hp: e.activation(
                            out=b['exp_pairsb'][pp][:, :], in_=b['pbpair'][pp][:, :],
                            func=AF.Exp, bias=b['negB2'][hp][:, 0:1]),
                            waits=[t_sT, t_nb, t_denp[pp], t_avp[pp],
                                   t_mskp[pp][0], t_mskp[pp][1]], inc='act')
                        for half in range(2):
                            kt = 2 * ip + half
                            if tile == 1 and kt < 16:
                                t_mskp[pp][half] = None
                                continue
                            t_mskp[pp][half] = s.emit('vector', lambda e, pp=pp, half=half, kt=kt: e.scalar_tensor_tensor(
                                out=b['exp_sb'][2 * pp + half][:, :], in0=b['qpos_rep'][:, :],
                                scalar=b['kpos'][:, kt:kt + 1],
                                in1=b['exp_sb'][2 * pp + half][:, :],
                                op0=ALU.is_ge, op1=ALU.mult),
                                waits=[t_expp[pp], t_qrep], inc='dve')
                    if ip >= LAGP:
                        jp = ip - LAGP
                        pj = jp % 2
                        for half in range(2):
                            kt = 2 * jp + half
                            t_avp[pj] = s.emit('tensor', lambda e, kt=kt, pj=pj, half=half, nkt=nkt: e.matmul(
                                b['pb'][2][:, :], b['vsl'][:, kt * 128:(kt + 1) * 128],
                                b['exp_sb'][2 * pj + half][:, :],
                                start=(kt == 0), stop=(kt == nkt - 1),
                                skip_group_check=True),
                                waits=[t_mskp[pj][half], t_expp[pj]],
                                inc='pe' if half == 1 else None)
                        for half in range(2):
                            kt = 2 * jp + half
                            t_denp[pj] = s.emit('tensor', lambda e, kt=kt, pj=pj, half=half, nkt=nkt: e.matmul(
                                b['pb'][3][0:1, :], b['ones_col'][:, 0:1],
                                b['exp_sb'][2 * pj + half][:, :],
                                start=(kt == 0), stop=(kt == nkt - 1),
                                skip_group_check=True), inc='pe' if half == 1 else None)
                t_exp_head[hp] = t_expp[(npair - 1) % 2]
                t_den_last = t_denp[(npair - 1) % 2]
                t_rc = s.emit('vector', lambda e: e.reciprocal(
                    b['rcp'][:, :], b['pb'][3][0:1, :]), waits=[t_den_last], inc='dve')
                t_rcb = s.emit('vector', lambda e: e.tensor_copy(
                    b['rcpb'][:, :], b['rcp'][:, :]), waits=[t_rc], inc='dve')
                t_rr = s.emit('tensor', lambda e: e.matmul(
                    b['pb'][5][:, :], b['posrow'][0:1, :], b['rcpb'][:, :],
                    start=True, stop=True, skip_group_check=True),
                    waits=[t_rcb], inc='pe')
                t_re = s.emit('vector', lambda e: e.tensor_copy(
                    b['rrep'][:, :], b['pb'][5][:, :]), waits=[t_rr], inc='dve')
                t_ctx_all = s.emit('vector', lambda e, h=h: e.tensor_tensor(
                    out=b['ctx_sb'][:, h * TT:(h + 1) * TT], in0=b['pb'][2][:, :],
                    in1=b['rrep'][:, :], op=ALU.mult),
                    waits=[t_re], inc='dve')
        last_ev, _ = emit_wsweep(s, b, tile, wo, 2, 1.0, extra_mm=None,
                                 first_waits=[t_ctx_all], rhs='y')
    return last_ev


_FUSED_NC = [None]

def kernel(**inp):
    p = _prep_shared(inp)
    pc = _prep_cores()
    X = np.asarray(inp['L0_attn_X'])
    xs = [np.ascontiguousarray(X[c // 4, _core_positions(c), :].T.astype(np.float32))
          for c in range(NCORES)]
    if _FUSED_NC[0] is None:
        _FUSED_NC[0] = build_fused()
    nc = _FUSED_NC[0]
    maps = []
    for c in range(NCORES):
        t = pc[c]
        m = dict(x_in=xs[c],
                 bw0=p['bw0'], bw1=p['bw1'],
                 wq0=p['wq0'], wk0=p['wk0'], wv0=p['wv0'], wo0=p['wo0'],
                 wq1=p['wq1'], wk1=p['wk1'], wv1=p['wv1'], wo1=p['wo1'],
                 cq0=t['cq0'], sq0=t['sq0'], ck0=t['ck0'], sk0=t['sk0'],
                 cq1=t['cq1'], sq1=t['sq1'], ck1=t['ck1'], sk1=t['sk1'],
                 rt=p['rt0'], la=p['la0'], lb=p['lb0'],
                 enc=p['enc1'], bbias=p['bb1'], c0=p['c0'], c1=p['c1'],
                 mc0=p['mc0'], mc1=p['mc1'], mt=p['mt'],
                 e16=p['e16'], ident=p['ident'], perm=p['perm'],
                 kpos=p['kpos'], qpos=t['qpos'],
                 negrow=p['negrow'], posrow=p['posrow'])
        maps.append(m)
    res = _run(nc, maps)
    out = np.empty((2, 4096, D), dtype=np.float32)
    for c in range(NCORES):
        out[c // 4, _core_positions(c), :] = res[c]['x_out'].T
    return out
